# version 15
# baseline (speedup 1.0000x reference)
import numpy as np
import ml_dtypes
from contextlib import ExitStack

import concourse.bass as bass
import concourse.mybir as mybir
from concourse.bass_utils import run_bass_kernel_spmd

F32 = mybir.dt.float32
BF16 = mybir.dt.bfloat16
I16 = mybir.dt.int16
AF = mybir.ActivationFunctionType
ALU = mybir.AluOpType
AX = mybir.AxisListType
BF16NP = ml_dtypes.bfloat16

D = 1024
NH = 16
LAT = 64
DHID = 4096
NFC = D // 128
NHC = DHID // 128
TF = 256
NQT = TF // 128
EPS = 1e-5
SCALE = 0.125


def build(T=2048, dbg=(), phases=5, p2sub=4):
    NT = T // 128
    NTQ = T // 512
    NTF = T // TF
    NQB = NT
    NUNITS = 8 * NQB
    nc = bass.Bass("TRN2", target_bir_lowering=False)

    dX = nc.dram_tensor("x_tok", [T, D], F32, kind="ExternalInput").ap()
    dXT = nc.dram_tensor("xT", [D, T], F32, kind="ExternalInput").ap()
    dWqT = nc.dram_tensor("wqT", [D, D], F32, kind="ExternalInput").ap()
    dWk = nc.dram_tensor("wk", [D, D], F32, kind="ExternalInput").ap()
    dWv = nc.dram_tensor("wv", [D, D], F32, kind="ExternalInput").ap()
    dWo = nc.dram_tensor("wo", [D, D], F32, kind="ExternalInput").ap()
    dWc = nc.dram_tensor("wc", [64, 64], F32, kind="ExternalInput").ap()
    dbq = nc.dram_tensor("bq", [128, NFC], F32, kind="ExternalInput").ap()
    dbk = nc.dram_tensor("bk", [128, NFC], F32, kind="ExternalInput").ap()
    dbv = nc.dram_tensor("bv", [1, D], F32, kind="ExternalInput").ap()
    dbo = nc.dram_tensor("bo", [128, NFC], F32, kind="ExternalInput").ap()
    dW1T = nc.dram_tensor("w1T", [D, DHID], F32, kind="ExternalInput").ap()
    dW2T = nc.dram_tensor("w2T", [DHID, D], F32, kind="ExternalInput").ap()
    dCW = nc.dram_tensor("convw", [128, NHC * 3], F32, kind="ExternalInput").ap()
    dCB = nc.dram_tensor("convb", [128, NHC], F32, kind="ExternalInput").ap()
    dID = nc.dram_tensor("ident", [128, 128], BF16, kind="ExternalInput").ap()
    dMSK = nc.dram_tensor("masksub", [128, 128], F32, kind="ExternalInput").ap()

    dOUT = nc.dram_tensor("outT", [D, T], F32, kind="ExternalOutput").ap()

    dC1 = nc.dram_tensor("c1scr", [128, 8, NFC, 512], BF16).ap()
    dC2 = nc.dram_tensor("c2scr", [128, 2, 8, 4, 512], BF16).ap()
    dK1 = nc.dram_tensor("k1scr", [NFC, 128, T], BF16).ap()

    dbg_out = {}

    def dbgt(name, shape, dt):
        if name in dbg:
            dbg_out[name] = nc.dram_tensor("dbg_" + name, shape, dt,
                                           kind="ExternalOutput").ap()
            return dbg_out[name]
        return None

    d_xnT = dbgt("xnT", [128, NFC * T], BF16)
    d_mac = dbgt("mac", [128, NT], F32)
    d_qcT = dbgt("qcT", [128, NFC * T], BF16)
    d_kT = dbgt("kT", [128, NFC * T], BF16)
    d_vtok = dbgt("vtok", [128, NT * D], BF16)
    d_oT = dbgt("oT", [128, NFC * T], BF16)
    d_attnT = dbgt("attnT", [128, NFC * T], BF16)
    d_k2 = dbgt("k2", [128, NHC * TF], BF16)
    d_g = dbgt("g", [128, NHC * TF], F32)
    d_wq = dbgt("wq", [128, 8], F32)

    W1N = float(DHID * D)

    est = ExitStack()
    with est:
        E = est.enter_context

        bufA = E(nc.sbuf_tensor("bufA", [128, max(NFC * T, NHC * TF)], BF16))
        bufC = E(nc.sbuf_tensor("bufC", [128, NFC * T], BF16))

        ident = E(nc.sbuf_tensor("s_ident", [128, 128], BF16))
        identf = E(nc.sbuf_tensor("s_identf", [128, 128], F32))
        masksub = E(nc.sbuf_tensor("s_masksub", [128, 128], F32))
        wc2 = E(nc.sbuf_tensor("wc2", [128, 64], BF16))
        bqpc = E(nc.sbuf_tensor("bqpc", [128, NFC], F32))
        bkpc = E(nc.sbuf_tensor("bkpc", [128, NFC], F32))
        bopc = E(nc.sbuf_tensor("bopc", [128, NFC], F32))
        bvbf = E(nc.sbuf_tensor("bvbf", [1, D], BF16))
        bqe = E(nc.sbuf_tensor("bqe", [128, NFC], F32))
        bqbf = E(nc.sbuf_tensor("bqbf", [128, NFC], BF16))
        cw = E(nc.sbuf_tensor("cw", [128, NHC * 3], F32))
        cb = E(nc.sbuf_tensor("cb", [128, NHC], F32))
        onesrow = E(nc.sbuf_tensor("onesrow", [1, 128], BF16))
        onescol = E(nc.sbuf_tensor("onescol", [128, 1], BF16))
        onesfrow = E(nc.sbuf_tensor("onesfrow", [1, 128], F32))
        onesfcol = E(nc.sbuf_tensor("onesfcol", [128, 1], F32))
        macol = E(nc.sbuf_tensor("macol", [128, NT], F32))
        wqacc = E(nc.sbuf_tensor("wqacc", [128, 2], F32))
        mwsum = E(nc.sbuf_tensor("mwsum", [1, 2], F32))
        sumrep = E(nc.sbuf_tensor("sumrep", [128, 2], F32))
        thrrep = E(nc.sbuf_tensor("thrrep", [128, 2], F32))
        nthrrep = E(nc.sbuf_tensor("nthrrep", [128, 2], F32))
        mw127 = E(nc.sbuf_tensor("mw127", [128, 2], F32))
        c1535 = E(nc.sbuf_tensor("c1535", [128, 1], F32))

        ps = [E(nc.psum_tensor(f"ps{i}", [128, 512], F32)) for i in range(4)]
        pbf = [E(nc.psum_tensor(f"pbf{i}", [128, 1024], BF16)) for i in range(2)]
        psS = E(nc.psum_tensor("psS", [128, 512], F32))
        psT = E(nc.psum_tensor("psT", [128, 512], F32))
        pv2 = [psS[:, 0:128], psT[:, 0:128]]
        _ban = [psT, psS]
        lsl = [(_ban[0][0:1, 128:256], _ban[0][32:33, 128:256]),
               (_ban[1][0:1, 128:256], _ban[1][32:33, 128:256])]
        lbp = [_ban[0][:, 256:384], _ban[1][:, 256:384]]
        nbc = [_ban[0][:, 384:386], _ban[1][:, 384:386]]
        wqp = [_ban[0][0:1, 388:390], _ban[1][:, 390:392]]

        sem_names = ["dIN", "dWc", "dWq", "dWk", "dWv", "dWo", "dX0", "dX1", "dOUT",
                     "dK0", "dK1",
                     "p1a", "p1b", "p1c", "p1d", "p1e", "p1f", "p1gv", "p1gs",
                     "p2f", "p2fe", "p2bq", "p2bqe", "p2m",
                     "p2q", "p2qe", "p2k", "p2ke", "p2v", "p2ve",
                     "aQK", "aRM", "aNMB", "aNM", "aEXP", "aPV", "aLR", "aLB", "aOE",
                     "p4q", "p4qe", "pB1", "pB2",
                     "wqA", "wqB", "wqC", "wqD", "wqT1", "wqT2",
                     "wqE", "wqF", "wqG1", "wqG2", "wqV", "wqW", "qab",
                     "f1", "f1e", "cv1", "cv2", "cv3", "cvS", "sil", "qmn",
                     "q1", "q2", "q3", "q4", "q5", "q6", "q7",
                     "rnA", "rnB", "rnC", "rnD",
                     "f2", "f2c", "f2e", "f2f", "f2g",
                     "dc1a", "dc1b", "dc2a", "dc2b",
                     "dxta", "dxtb", "dosa", "dosb", "dk1a", "dk1b",
                     "s1p", "s1q", "s1r", "s1c", "s1e"]
        S = {n: E(nc.semaphore(n)) for n in sem_names}
        N_IN = 7
        dbg_cnt = [0]

        b12 = ExitStack()
        E12 = b12.enter_context
        bufD = E12(nc.sbuf_tensor("bufD", [128, NFC * T], BF16))
        bufE = E12(nc.sbuf_tensor("bufE", [128, NT * D], BF16))
        wstat = E12(nc.sbuf_tensor("wstat", [128, NFC * D], BF16))

        b1 = ExitStack()
        with b1:
            E1 = b1.enter_context
            wact = E1(nc.sbuf_tensor("wact", [128, NFC * D], BF16))
            xst = E1(nc.sbuf_tensor("xst", [128, 2 * D], F32))
            sqs = E1(nc.sbuf_tensor("sqs", [128, D], BF16))
            xntok = E1(nc.sbuf_tensor("xntok", [128, D], F32))
            xnbf = E1(nc.sbuf_tensor("xnbf", [128, D], BF16))
            k1tok = E1(nc.sbuf_tensor("k1tok", [128, D], BF16))
            i16b = E1(nc.sbuf_tensor("i16b", [128, D], I16))
            rrb = E1(nc.sbuf_tensor("rrb", [128, D], F32))
            st8 = E1(nc.sbuf_tensor("st8", [128, 12], F32))
            k1stg = E1(nc.sbuf_tensor("k1stg", [128, 2 * D], BF16))

            def xslot(i):
                return xst[:, (i % 2) * D:(i % 2 + 1) * D]

            block1 = E1(nc.Block())

            @block1.sync
            def _(e):
                e.dma_start(out=ident[:], in_=dID).then_inc(S["dIN"], 16)
                e.dma_start(out=masksub[:], in_=dMSK).then_inc(S["dIN"], 16)
                e.dma_start(out=bqpc[:], in_=dbq).then_inc(S["dIN"], 16)
                e.dma_start(out=bkpc[:], in_=dbk).then_inc(S["dIN"], 16)
                e.dma_start(out=bopc[:], in_=dbo).then_inc(S["dIN"], 16)
                e.dma_start(out=cw[:], in_=dCW).then_inc(S["dIN"], 16)
                e.dma_start(out=cb[:], in_=dCB).then_inc(S["dIN"], 16)
                for i in range(NT):
                    if i >= 2:
                        e.wait_ge(S["p1c"], i - 1)
                    e.dma_start(out=xslot(i), in_=dX[i * 128:(i + 1) * 128, :]
                                ).then_inc(S["dX0" if i % 2 == 0 else "dX1"], 16)
                    if i >= 1:
                        e.wait_ge(S["p1gs"], 8 * i)
                        j = i - 1
                        e.dma_start(
                            out=dK1[:, :, j * 128:(j + 1) * 128].rearrange("c p t -> p c t"),
                            in_=k1stg[:, (j % 2) * D:(j % 2 + 1) * D].rearrange(
                                "p (c t) -> p c t", c=NFC)
                        ).then_inc(S["dK0" if j % 2 == 0 else "dK1"], 16)
                e.wait_ge(S["p1gs"], 8 * NT)
                j = NT - 1
                e.dma_start(
                    out=dK1[:, :, j * 128:(j + 1) * 128].rearrange("c p t -> p c t"),
                    in_=k1stg[:, (j % 2) * D:(j % 2 + 1) * D].rearrange(
                        "p (c t) -> p c t", c=NFC)
                ).then_inc(S["dK0" if j % 2 == 0 else "dK1"], 16)
                nd = 0
                if d_xnT is not None:
                    e.wait_ge(S["p1gv"], 8 * NT)
                    e.dma_start(out=d_xnT, in_=bufA[:, 0:NFC * T]).then_inc(S["dOUT"], 16)
                    nd += 1
                if d_mac is not None:
                    e.wait_ge(S["p1gv"], 8 * NT)
                    e.dma_start(out=d_mac, in_=macol[:]).then_inc(S["dOUT"], 16)
                    nd += 1
                if d_qcT is not None and phases >= 2 and p2sub >= 2:
                    e.wait_ge(S["p2qe"], NFC * NTQ)
                    e.dma_start(out=d_qcT, in_=bufC[:]).then_inc(S["dOUT"], 16)
                    nd += 1
                if d_kT is not None and phases >= 2 and p2sub >= 3:
                    e.wait_ge(S["p2ke"], NFC * NTQ)
                    e.dma_start(out=d_kT, in_=bufD[:]).then_inc(S["dOUT"], 16)
                    nd += 1
                if d_vtok is not None and phases >= 2 and p2sub >= 4:
                    e.wait_ge(S["p2ve"], NT)
                    e.dma_start(out=d_vtok, in_=bufE[:]).then_inc(S["dOUT"], 16)
                    nd += 1
                dbg_cnt[0] += nd
                e.wait_ge(S["dOUT"], 16 * dbg_cnt[0])
                e.wait_ge(S["dK0"], 16 * ((NT + 1) // 2))
                e.wait_ge(S["dK1"], 16 * (NT // 2))
                e.nop().then_inc(S["pB1"], 1)

            @block1.gpsimd
            def _(e):
                e.dma_start(out=bvbf[:], in_=dbv).then_inc(S["dWc"], 16)
                e.dma_start(out=wc2[0:64, :], in_=dWc).then_inc(S["dWc"], 16)
                e.dma_start(out=wc2[64:128, :], in_=dWc).then_inc(S["dWc"], 16)
                e.dma_start(out=wstat[:].rearrange("p (c f) -> p c f", c=NFC),
                            in_=dWqT.rearrange("(c p) f -> p c f", p=128)
                            ).then_inc(S["dWq"], 16)
                if phases >= 2 and p2sub >= 3:
                    e.wait_ge(S["p2f"], NFC)
                    e.dma_start(out=wstat[:].rearrange("p (c f) -> p c f", c=NFC),
                                in_=dWk.rearrange("(c p) f -> p c f", p=128)
                                ).then_inc(S["dWk"], 16)
                if phases >= 2 and p2sub >= 4:
                    e.wait_ge(S["p2qe"], NFC * NTQ)
                    e.dma_start(out=wact[:].rearrange("p (c f) -> p c f", c=NFC),
                                in_=dWv.rearrange("(c p) f -> p c f", p=128)
                                ).then_inc(S["dWv"], 16)
                e.engine_nop().then_inc(S["pB1"], 1)

            @block1.vector
            def _(e):
                e.memset(onesrow[:], 1.0)
                e.memset(onescol[:], 1.0)
                e.memset(onesfrow[:], 1.0)
                e.memset(onesfcol[:], 1.0)
                e.memset(c1535[:], 1535.5)
                e.wait_ge(S["dIN"], 16 * N_IN)
                e.tensor_copy(out=identf[:], in_=ident[:])
                e.drain()
                for i in range(NT):
                    sl = xslot(i)
                    e.wait_ge(S["dX0" if i % 2 == 0 else "dX1"], 16 * (i // 2 + 1))
                    e.reduce_sum(out=st8[:, 0:1], in_=sl, axis=AX.X)
                    e.wait_ge(S["p1a"], i + 1)
                    e.drain()
                    e.tensor_scalar(out=st8[:, 2:3], in0=st8[:, 0:1], scalar1=1.0 / D,
                                    scalar2=None, op0=ALU.mult)
                    e.tensor_scalar(out=st8[:, 3:4], in0=st8[:, 1:2], scalar1=1.0 / D,
                                    scalar2=None, op0=ALU.mult)
                    e.drain()
                    e.tensor_tensor(out=st8[:, 4:5], in0=st8[:, 2:3], in1=st8[:, 2:3],
                                    op=ALU.mult)
                    e.drain()
                    e.tensor_tensor(out=st8[:, 5:6], in0=st8[:, 3:4], in1=st8[:, 4:5],
                                    op=ALU.subtract)
                    e.drain()
                    e.tensor_scalar(out=st8[:, 6:7], in0=st8[:, 5:6], scalar1=EPS,
                                    scalar2=None, op0=ALU.add)
                    e.drain()
                    e.reciprocal(out=st8[:, 7:8], in_=st8[:, 6:7]).then_inc(S["p1b"], 1)
                    e.wait_ge(S["p1c"], i + 1)
                    e.reduce_max(out=st8[:, 9:10], in_=xntok[:], axis=AX.X,
                                 apply_absolute_value=True)
                    e.drain()
                    e.tensor_scalar(out=macol[:, i:i + 1], in0=st8[:, 9:10],
                                    scalar1=1e-5, scalar2=None, op0=ALU.max)
                    e.drain()
                    e.reciprocal(out=st8[:, 10:11], in_=macol[:, i:i + 1])
                    e.drain()
                    e.tensor_scalar(out=st8[:, 11:12], in0=st8[:, 10:11],
                                    scalar1=127.0, scalar2=None, op0=ALU.mult)
                    e.drain()
                    if i > 0:
                        e.wait_ge(S["p1f"], 16 * i)
                    e.tensor_scalar(out=i16b[:], in0=xntok[:], scalar1=st8[:, 11:12],
                                    scalar2=1535.5, op0=ALU.mult, op1=ALU.add)
                    e.drain()
                    e.scalar_tensor_tensor(out=rrb[:], in0=xntok[:], scalar=st8[:, 11:12],
                                           in1=i16b[:], op0=ALU.mult, op1=ALU.subtract)
                    e.drain()
                    e.tensor_scalar(out=rrb[:], in0=rrb[:], scalar1=-1535.5,
                                    scalar2=None, op0=ALU.is_ge)
                    e.drain()
                    e.scalar_tensor_tensor(out=k1tok[:], in0=i16b[:], scalar=-1536.0,
                                           in1=rrb[:], op0=ALU.add,
                                           op1=ALU.add).then_inc(S["p1d"], 1)
                    e.wait_ge(S["p1f"], 16 * i + 8)
                    for s in range(8):
                        e.tensor_copy(out=bufA[:, s * T + i * 128: s * T + (i + 1) * 128],
                                      in_=pbf[0][:, s * 128:(s + 1) * 128]
                                      ).then_inc(S["p1gv"], 1)
                if phases < 2:
                    return
                e.wait_ge(S["dIN"], 16 * N_IN)
                e.tensor_copy(out=bqbf[:], in_=bqpc[:])
                e.drain()
                e.engine_nop().then_inc(S["p2m"], 1)
                for fc in range(NFC if p2sub >= 0 else 0):
                    e.wait_ge(S["p2f"], fc + 1)
                    for b in range(2):
                        dst = wact[:, fc * D:(fc + 1) * D].rearrange(
                            "p (j x) -> p j x", x=128)[:, :, b * 64:(b + 1) * 64]
                        e.tensor_scalar(out=dst,
                                        in0=ps[2 + b][:].rearrange("p (j l) -> p j l", l=64),
                                        scalar1=SCALE, scalar2=None, op0=ALU.mult)
                    e.drain()
                    e.engine_nop().then_inc(S["p2fe"], 1)
                if p2sub >= 1:
                    e.wait_ge(S["p2bq"], 1)
                for h in range(NH if p2sub >= 1 else 0):
                    half = (h % 2) * 64
                    e.tensor_scalar(out=bqe[half:half + 64, h // 2:h // 2 + 1],
                                    in0=ps[h % 2][0:64, h // 2:h // 2 + 1], scalar1=SCALE,
                                    scalar2=None, op0=ALU.mult)
                e.drain()
                e.engine_nop().then_inc(S["p2bqe"], 1)
                for i in range(NT if p2sub >= 4 else 0):
                    e.wait_ge(S["p2v"], i + 1)
                    for half in range(2):
                        e.tensor_copy(out=bufE[:, i * D + half * 512:i * D + (half + 1) * 512],
                                      in_=ps[2 + half][:])
                    e.drain()
                    e.engine_nop().then_inc(S["p2ve"], 1)
                e.engine_nop().then_inc(S["pB1"], 1)

            @block1.scalar
            def _(e):
                for i in range(NT):
                    sl = xslot(i)
                    e.wait_ge(S["dX0" if i % 2 == 0 else "dX1"], 16 * (i // 2 + 1))
                    e.activation(out=sqs[:], in_=sl, func=AF.Square,
                                 accum_out=st8[:, 1:2]).then_inc(S["p1a"], 1)
                    e.wait_ge(S["p1b"], i + 1)
                    e.activation(out=st8[:, 8:9], in_=st8[:, 7:8], func=AF.Sqrt)
                    e.drain()
                    e.mul(out=st8[:, 4:5], in_=st8[:, 2:3], mul=st8[:, 8:9])
                    e.drain()
                    e.mul(out=st8[:, 5:6], in_=st8[:, 4:5], mul=-1.0)
                    e.drain()
                    e.activation(out=xntok[:], in_=sl, func=AF.Identity,
                                 scale=st8[:, 8:9], bias=st8[:, 5:6]).then_inc(S["p1c"], 1)
                    e.drain()
                    if i > 0:
                        e.wait_ge(S["p1f"], 16 * i)
                    e.copy(out=xnbf[:], in_=xntok[:]).then_inc(S["p1e"], 1)
                    e.wait_ge(S["p1f"], 16 * i + 16)
                    if i >= 2:
                        e.wait_ge(S["dK0" if i % 2 == 0 else "dK1"],
                                  16 * (i // 2))
                    for s in range(8):
                        e.copy(out=k1stg[:, (i % 2) * D + s * 128:(i % 2) * D + (s + 1) * 128],
                               in_=pbf[1][:, s * 128:(s + 1) * 128]).then_inc(S["p1gs"], 1)
                if phases < 2:
                    return
                if p2sub >= 1:
                    e.wait_ge(S["p2bqe"], 1)
                for u, (oc, tq) in enumerate([(a, b) for a in range(NFC) for b in range(NTQ)] if p2sub >= 2 else []):
                    bank = u % 2
                    e.wait_ge(S["p2q"], u + 1)
                    e.activation(out=bufC[:, oc * T + tq * 512:oc * T + (tq + 1) * 512],
                                 in_=ps[bank][:], func=AF.Identity, scale=1.0,
                                 bias=bqe[:, oc:oc + 1]).then_inc(S["p2qe"], 1)
                for u, (oc, tq) in enumerate([(a, b) for a in range(NFC) for b in range(NTQ)] if p2sub >= 3 else []):
                    bank = u % 2
                    e.wait_ge(S["p2k"], u + 1)
                    e.activation(out=bufD[:, oc * T + tq * 512:oc * T + (tq + 1) * 512],
                                 in_=ps[bank][:], func=AF.Identity, scale=1.0,
                                 bias=bkpc[:, oc:oc + 1]).then_inc(S["p2ke"], 1)

            @block1.tensor
            def _(e):
                e.wait_ge(S["dIN"], 16 * N_IN)
                for i in range(NT):
                    e.wait_ge(S["p1e"], i + 1)
                    e.wait_ge(S["p1d"], i + 1)
                    if i > 0:
                        e.wait_ge(S["p1gv"], 8 * i)
                        e.wait_ge(S["p1gs"], 8 * i)
                    for s in range(16):
                        src = xnbf if s < 8 else k1tok
                        fc = s % 8
                        e.transpose(out=pbf[s // 8][:, fc * 128:(fc + 1) * 128],
                                    in_=src[:, fc * 128:(fc + 1) * 128],
                                    identity=ident[:]).then_inc(S["p1f"], 1)
                if phases < 2:
                    return
                e.wait_ge(S["dWq"], 16)
                e.wait_ge(S["dWc"], 48)
                for fc in range(NFC if p2sub >= 0 else 0):
                    if fc >= 1:
                        e.wait_ge(S["p2fe"], fc)
                    for h in range(NH):
                        half = (h % 2) * 64
                        lhsT = wstat[half:half + 64,
                                     (h // 2) * D + fc * 128:(h // 2) * D + (fc + 1) * 128]
                        out = ps[2 + h % 2][:, (h // 2) * 64:(h // 2) * 64 + 64]
                        ins = e.matmul(out=out, lhsT=lhsT, rhs=wc2[half:half + 64, :],
                                       start=True, stop=True)
                        if h == NH - 1:
                            ins.then_inc(S["p2f"], 1)
                e.wait_ge(S["dIN"], 16 * N_IN)
                e.wait_ge(S["p2m"], 1)
                for h in range(NH if p2sub >= 1 else 0):
                    half = (h % 2) * 64
                    ins = e.matmul(out=ps[h % 2][0:64, h // 2:h // 2 + 1],
                                   lhsT=wc2[half:half + 64, :],
                                   rhs=bqbf[half:half + 64, h // 2:h // 2 + 1],
                                   start=True, stop=True)
                    if h == NH - 1:
                        ins.then_inc(S["p2bq"], 1)
                e.wait_ge(S["p1gv"], 8 * NT)
                if p2sub >= 1:
                    e.wait_ge(S["p2bqe"], 1)
                for u, (oc, tq) in enumerate([(a, b) for a in range(NFC) for b in range(NTQ)] if p2sub >= 2 else []):
                    bank = u % 2
                    if u >= 2:
                        e.wait_ge(S["p2qe"], u - 1)
                    for fc in range(NFC):
                        ins = e.matmul(out=ps[bank][:],
                                       lhsT=wact[:, fc * D + oc * 128:fc * D + (oc + 1) * 128],
                                       rhs=bufA[:, fc * T + tq * 512:fc * T + (tq + 1) * 512],
                                       start=(fc == 0), stop=(fc == NFC - 1))
                        if fc == NFC - 1:
                            ins.then_inc(S["p2q"], 1)
                if p2sub >= 3:
                    e.wait_ge(S["dWk"], 16)
                    e.wait_ge(S["p2qe"], NFC * NTQ)
                for u, (oc, tq) in enumerate([(a, b) for a in range(NFC) for b in range(NTQ)] if p2sub >= 3 else []):
                    bank = u % 2
                    if u >= 2:
                        e.wait_ge(S["p2ke"], u - 1)
                    for fc in range(NFC):
                        ins = e.matmul(out=ps[bank][:],
                                       lhsT=wstat[:, fc * D + oc * 128:fc * D + (oc + 1) * 128],
                                       rhs=bufA[:, fc * T + tq * 512:fc * T + (tq + 1) * 512],
                                       start=(fc == 0), stop=(fc == NFC - 1))
                        if fc == NFC - 1:
                            ins.then_inc(S["p2k"], 1)
                if p2sub >= 4:
                    e.wait_ge(S["dWv"], 16)
                    e.wait_ge(S["dWc"], 48)
                    e.wait_ge(S["p2m"], 1)
                    e.wait_ge(S["p2fe"], NFC)
                for i in range(NT if p2sub >= 4 else 0):
                    if i >= 1:
                        e.wait_ge(S["p2ve"], i)
                    for fc in range(NFC):
                        for half in range(2):
                            e.matmul(out=ps[2 + half][:],
                                     lhsT=bufA[:, fc * T + i * 128:fc * T + (i + 1) * 128],
                                     rhs=wact[:, fc * D + half * 512:fc * D + (half + 1) * 512],
                                     start=(fc == 0), stop=False)
                    for half in range(2):
                        ins = e.matmul(out=ps[2 + half][:], lhsT=onesrow[:],
                                       rhs=bvbf[:, half * 512:(half + 1) * 512],
                                       start=False, stop=True)
                        if half == 1:
                            ins.then_inc(S["p2v"], 1)

        if phases < 3:
            b12.close()
            return nc, dbg_out
        b2 = ExitStack()
        with b2:
            E2 = b2.enter_context
            pt = E2(nc.sbuf_tensor("pt", [128, 4 * T], BF16))
            negm = E2(nc.sbuf_tensor("negm", [128, 4], F32))
            p3row = E2(nc.sbuf_tensor("p3row", [1, 8], F32))
            linv = E2(nc.sbuf_tensor("linv", [1, 256], F32))
            lbs = E2(nc.sbuf_tensor("lbs", [128, 128], F32))
            wqst = E2(nc.sbuf_tensor("wqst", [128, 2 * 2048], F32))
            wqout = E2(nc.sbuf_tensor("wqout", [128, 2048], BF16))
            wqcnb = E2(nc.sbuf_tensor("wqcnb", [128, 2048], BF16))
            wqtmp = E2(nc.sbuf_tensor("wqtmp", [128, 1], F32))

            def p3_sched():
                sched = []
                qk_gidx = 0
                exp_gidx = 0
                last_exp = {}
                for hp in range(8):
                    for qb in range(NQB):
                        u = hp * NQB + qb
                        nkt = qb + 1
                        nch = (nkt + 3) // 4
                        groups = []
                        for c in range(nch):
                            kts = list(range(4 * c, min(4 * c + 4, nkt)))
                            for hs in range(2):
                                bank = 2 * hs + (c % 2)
                                groups.append(dict(hs=hs, c=c, kts=kts, bank=bank,
                                                   qk=qk_gidx, exp=exp_gidx,
                                                   free_at=last_exp.get(bank)))
                                last_exp[bank] = exp_gidx + 1
                                qk_gidx += 1
                                exp_gidx += 1
                        sched.append(dict(u=u, hp=hp, qb=qb, nkt=nkt, nch=nch,
                                          groups=groups, exp_end=exp_gidx))
                return sched

            SCHED = p3_sched()
            WQCH = [(0, ch, half,
                     dW1T[ch * 128:(ch + 1) * 128, half * 2048:(half + 1) * 2048])
                    for ch in range(8) for half in range(2)] + \
                   [(1, pr, 0,
                     dW2T[pr * 256:(pr + 1) * 256, :].rearrange(
                        "(c p) e -> p c e", p=128)) for pr in range(16)]
            WQ_CPU = 1 if NUNITS >= 40 else 2
            WQ_UOF = lambda i: 1 + i // WQ_CPU
            WQ_U1 = WQ_UOF(31) + 2
            WQ_U2 = WQ_U1 + 2
            WQ_TCPU = 1 if NUNITS >= 110 else (2 if NUNITS >= 60 else 4)
            WQ_TOF = lambda i: WQ_U2 + 2 + i // WQ_TCPU
            WQPLAN = []
            _oc = 0
            for _i, (_m, _ci, _hf, _src) in enumerate(WQCH):
                _nd = 1 if _m == 0 else 2
                WQPLAN.append((_m, _ci, _hf, _src, _oc, _nd))
                _oc += _nd
            WQ_OCNT = _oc
            assert WQ_TOF(len(WQCH) - 1) < NUNITS, "wq ternarize schedule overflow"

            def ptsl(u, hs, lo, hi):
                base = (u % 2) * 2 * T + hs * T
                return pt[:, base + lo:base + hi]

            block2 = E2(nc.Block())

            @block2.tensor
            def _(e):
                e.wait_ge(S["pB1"], 3)
                e.wait_ge(S["p2ke"], NFC * NTQ)
                e.wait_ge(S["p2v"], NT)
                first_bank_wait = {0: ("p2ke", NFC * NTQ), 1: ("p2ke", NFC * NTQ),
                                   2: ("p2ve", NT), 3: ("p2ve", NT)}
                for un in SCHED:
                    u, hp, qb, nkt = un["u"], un["hp"], un["qb"], un["nkt"]
                    par = u % 2
                    for g in un["groups"]:
                        hs, c, kts, bank = g["hs"], g["c"], g["kts"], g["bank"]
                        if g["free_at"] is None:
                            sem, val = first_bank_wait.pop(bank, (None, 0))
                            if sem:
                                e.wait_ge(S[sem], val)
                        else:
                            e.wait_ge(S["aEXP"], g["free_at"])
                        hb = hs * 64
                        for j, kt in enumerate(kts):
                            ins = e.matmul(
                                out=ps[bank][:, j * 128:(j + 1) * 128],
                                lhsT=bufD[hb:hb + 64, hp * T + kt * 128:hp * T + (kt + 1) * 128],
                                rhs=bufC[hb:hb + 64, hp * T + qb * 128:hp * T + (qb + 1) * 128],
                                start=True, stop=True)
                            if j == len(kts) - 1:
                                ins.then_inc(S["aQK"], 1)
                    e.wait_ge(S["aRM"], u + 1)
                    e.wait_ge(S["aOE"], u)
                    e.wait_ge(S["aLR"], u)
                    e.matmul(out=nbc[par], lhsT=onesfrow[:], rhs=p3row[0:1, 4:6],
                             start=True, stop=True).then_inc(S["aNMB"], 1)
                    if u == WQ_U1:
                        e.wait_ge(S["wqA"], 1)
                        e.matmul(out=_ban[par][0:1, 388:389], lhsT=wqacc[:, 0:1],
                                 rhs=onesfcol[:], start=True, stop=True)
                        e.matmul(out=_ban[par][0:1, 389:390], lhsT=wqacc[:, 1:2],
                                 rhs=onesfcol[:], start=True, stop=True).then_inc(S["wqC"], 1)
                    if u == WQ_U2:
                        e.wait_ge(S["wqB"], 1)
                        e.matmul(out=_ban[par][:, 388:390], lhsT=onesfrow[:],
                                 rhs=mwsum[:], start=True, stop=True).then_inc(S["wqD"], 1)
                    e.wait_ge(S["aNM"], u + 1)
                    if u == WQ_U1:
                        e.wait_ge(S["wqB"], 1)
                    if u == WQ_U2:
                        e.wait_ge(S["wqE"], 1)
                    if u == 0:
                        e.wait_ge(S["p2v"], NT)
                    lpair = lsl[par]
                    for hs in range(2):
                        for kt in range(nkt):
                            c = kt // 4
                            e.wait_ge(S["aEXP"], un["groups"][2 * c + hs]["exp"] + 1)
                            e.matmul(out=lpair[hs], lhsT=onescol[:],
                                     rhs=ptsl(u, hs, kt * 128, (kt + 1) * 128),
                                     start=(kt == 0), stop=(kt == nkt - 1),
                                     tile_position=(0, 32 * hs))
                    for hs in range(2):
                        for kt in range(nkt):
                            ins = e.matmul(
                                out=pv2[par][hs * 64:(hs + 1) * 64, :],
                                lhsT=bufE[:, kt * D + (2 * hp + hs) * 64:kt * D + (2 * hp + hs + 1) * 64],
                                rhs=ptsl(u, hs, kt * 128, (kt + 1) * 128),
                                start=(kt == 0), stop=(kt == nkt - 1),
                                tile_position=(0, hs * 64))
                            if kt == nkt - 1 and hs == 1:
                                ins.then_inc(S["aPV"], 1)
                    e.wait_ge(S["aLR"], u + 1)
                    e.matmul(out=lbp[par][0:64, :], lhsT=onesfrow[0:1, 0:64],
                             rhs=linv[0:1, 0:128], start=True, stop=True)
                    e.matmul(out=lbp[par][64:128, :], lhsT=onesfrow[0:1, 0:64],
                             rhs=linv[0:1, 128:256], start=True, stop=True,
                             tile_position=(0, 64)).then_inc(S["aLB"], 1)
                e.wait_ge(S["dWo"], 16)
                e.wait_ge(S["aOE"], NUNITS)
                for un2, (oc, tq) in enumerate([(a, b) for a in range(NFC) for b in range(NTQ)]):
                    bank = un2 % 2
                    if un2 >= 2:
                        e.wait_ge(S["p4qe"], un2 - 1)
                    for fc in range(NFC):
                        ins = e.matmul(out=ps[bank][:],
                                       lhsT=wstat[:, fc * D + oc * 128:fc * D + (oc + 1) * 128],
                                       rhs=bufA[:, fc * T + tq * 512:fc * T + (tq + 1) * 512],
                                       start=(fc == 0), stop=(fc == NFC - 1))
                        if fc == NFC - 1:
                            ins.then_inc(S["p4q"], 1)

            @block2.vector
            def _(e):
                e.wait_ge(S["pB1"], 3)
                e.wait_ge(S["p2ke"], NFC * NTQ)
                e.wait_ge(S["p2v"], NT)
                for un in SCHED:
                    u, hp, qb, nkt = un["u"], un["hp"], un["qb"], un["nkt"]
                    par = u % 2
                    dkt = qb
                    c = dkt // 4
                    j = dkt - 4 * c
                    for hs in range(2):
                        g = un["groups"][2 * c + hs]
                        e.wait_ge(S["aQK"], g["qk"] + 1)
                        dia = ps[g["bank"]][:, j * 128:(j + 1) * 128]
                        e.tensor_tensor(out=dia, in0=dia, in1=masksub[:], op=ALU.add)
                        e.drain()
                        e.reduce_max(out=p3row[0:1, hs:hs + 1], in_=dia[0:1, :], axis=AX.X)
                        e.drain()
                    if qb == 0:
                        e.tensor_copy(out=p3row[0:1, 2:4], in_=p3row[0:1, 0:2])
                    else:
                        e.tensor_tensor(out=p3row[0:1, 2:4], in0=p3row[0:1, 2:4],
                                        in1=p3row[0:1, 0:2], op=ALU.max)
                    e.drain()
                    e.tensor_scalar(out=p3row[0:1, 4:6], in0=p3row[0:1, 2:4],
                                    scalar1=-1.0, scalar2=None, op0=ALU.mult
                                    ).then_inc(S["aRM"], 1)
                    e.wait_ge(S["aNMB"], u + 1)
                    if u >= 1:
                        e.wait_ge(S["aEXP"], SCHED[u - 1]["exp_end"])
                    slot = ((qb + 1) % 2) * 2
                    e.tensor_copy(out=negm[:, slot:slot + 2], in_=nbc[par])
                    if qb == 0:
                        e.drain()
                        e.tensor_copy(out=negm[:, 0:2], in_=nbc[par])
                    e.drain()
                    e.engine_nop().then_inc(S["aNM"], 1)
                    for i in range(len(WQCH)):
                        if WQ_UOF(i) != u:
                            continue
                        m = WQCH[i][0]
                        e.wait_ge(S["wqT1" if i % 2 == 0 else "wqT2"], 16 * (i // 2 + 1))
                        sl2 = wqst[:, (i % 2) * 2048:(i % 2 + 1) * 2048]
                        if i == 0 or (m == 1 and WQCH[i - 1][0] == 0):
                            e.reduce_sum(out=wqacc[:, m:m + 1], in_=sl2, axis=AX.X,
                                         apply_absolute_value=True)
                            e.drain()
                        else:
                            e.reduce_sum(out=wqtmp[:], in_=sl2, axis=AX.X,
                                         apply_absolute_value=True)
                            e.drain()
                            e.tensor_tensor(out=wqacc[:, m:m + 1], in0=wqacc[:, m:m + 1],
                                            in1=wqtmp[:], op=ALU.add)
                            e.drain()
                        e.engine_nop().then_inc(S["wqV"], 1)
                        if i == len(WQCH) - 1:
                            e.engine_nop().then_inc(S["wqA"], 1)
                    if u == WQ_U1:
                        e.wait_ge(S["wqC"], 1)
                        e.tensor_copy(out=mwsum[:], in_=_ban[par][0:1, 388:390])
                        e.drain()
                        e.engine_nop().then_inc(S["wqB"], 1)
                    if u == WQ_U2:
                        e.wait_ge(S["wqD"], 1)
                        e.tensor_copy(out=sumrep[:], in_=_ban[par][:, 388:390])
                        e.drain()
                        e.engine_nop().then_inc(S["wqE"], 1)
                    for i3 in range(len(WQPLAN)):
                        if WQ_TOF(i3) != u:
                            continue
                        m3, _c3, _h3, _s3, oc3, nd3 = WQPLAN[i3]
                        slot3 = i3 % 2
                        if i3 == 0:
                            e.wait_ge(S["wqF"], 1)
                        e.wait_ge(S["wqT1" if slot3 == 0 else "wqT2"],
                                  16 * (16 + i3 // 2 + 1))
                        e.wait_ge(S["wqG1"], 16 * oc3)
                        sl3 = wqst[:, slot3 * 2048:(slot3 + 1) * 2048]
                        e.tensor_scalar(out=wqcnb[:], in0=sl3,
                                        scalar1=nthrrep[:, m3:m3 + 1],
                                        scalar2=None, op0=ALU.is_le)
                        e.drain()
                        e.tensor_scalar(out=wqout[:], in0=sl3,
                                        scalar1=thrrep[:, m3:m3 + 1],
                                        scalar2=None, op0=ALU.is_ge)
                        e.drain()
                        e.tensor_tensor(out=wqout[:], in0=wqout[:], in1=wqcnb[:],
                                        op=ALU.subtract)
                        e.drain()
                        e.engine_nop().then_inc(S["wqW"], 1)
                    e.wait_ge(S["aPV"], u + 1)
                    e.reciprocal(out=linv[0:1, 0:128], in_=lsl[par][0])
                    e.reciprocal(out=linv[0:1, 128:256], in_=lsl[par][1])
                    e.drain()
                    e.engine_nop().then_inc(S["aLR"], 1)
                    e.wait_ge(S["aLB"], u + 1)
                    e.tensor_copy(out=lbs[:], in_=lbp[par])
                    e.drain()
                    if u == 0:
                        e.wait_ge(S["p2v"], NT)
                    e.tensor_tensor(out=bufA[:, hp * T + qb * 128:hp * T + (qb + 1) * 128],
                                    in0=pv2[par], in1=lbs[:],
                                    op=ALU.mult).then_inc(S["aOE"], 1)
                e.engine_nop().then_inc(S["pB2"], 1)

            @block2.scalar
            def _(e):
                e.wait_ge(S["pB1"], 3)
                e.wait_ge(S["p2ke"], NFC * NTQ)
                e.wait_ge(S["p2v"], NT)
                for un in SCHED:
                    u, hp, qb, nkt = un["u"], un["hp"], un["qb"], un["nkt"]
                    dkt = qb
                    for g in un["groups"]:
                        hs, c, kts, bank = g["hs"], g["c"], g["kts"], g["bank"]
                        e.wait_ge(S["aQK"], g["qk"] + 1)
                        if dkt in kts:
                            e.wait_ge(S["aRM"], u + 1)
                        e.wait_ge(S["aNM"], u + 1 if qb == 0 else u)
                        if u >= 2:
                            e.wait_ge(S["aPV"], u - 1)
                        wid = len(kts) * 128
                        e.activation(out=ptsl(u, hs, 4 * c * 128, 4 * c * 128 + wid),
                                     in_=ps[bank][:, 0:wid], func=AF.Exp,
                                     bias=negm[:, (qb % 2) * 2 + hs:(qb % 2) * 2 + hs + 1],
                                     scale=1.0).then_inc(S["aEXP"], 1)
                for un2, (oc, tq) in enumerate([(a, b) for a in range(NFC) for b in range(NTQ)]):
                    bank = un2 % 2
                    e.wait_ge(S["p4q"], un2 + 1)
                    e.activation(out=bufC[:, oc * T + tq * 512:oc * T + (tq + 1) * 512],
                                 in_=ps[bank][:], func=AF.Identity, scale=1.0,
                                 bias=bopc[:, oc:oc + 1]).then_inc(S["p4qe"], 1)

            @block2.gpsimd
            def _(e):
                e.wait_ge(S["pB1"], 3)
                e.wait_ge(S["p2ke"], NFC * NTQ)
                e.wait_ge(S["p2v"], NT)
                e.dma_start(out=wstat[:].rearrange("p (c f) -> p c f", c=NFC),
                            in_=dWo.rearrange("(c p) f -> p c f", p=128)
                            ).then_inc(S["dWo"], 16)
                for i, (m, _c, _h, srcap) in enumerate(WQCH):
                    slot = i % 2
                    sem = S["wqT1" if slot == 0 else "wqT2"]
                    if i >= 2:
                        e.wait_ge(S["wqV"], i - 1)
                    tgt = wqst[:, slot * 2048:(slot + 1) * 2048]
                    if m == 1:
                        tgt = tgt.rearrange("p (c e) -> p c e", c=2)
                    e.dma_start(out=tgt, in_=srcap).then_inc(sem, 16)
                e.wait_ge(S["wqE"], 1)
                e.tensor_scalar(out=thrrep[:], in0=sumrep[:], scalar1=1.0 / W1N,
                                scalar2=None, op0=ALU.mult)
                e.drain()
                e.tensor_scalar(out=thrrep[:], in0=thrrep[:], scalar1=1e-5,
                                scalar2=None, op0=ALU.max)
                e.drain()
                e.tensor_scalar(out=mw127[:], in0=thrrep[:], scalar1=1.0 / 127.0,
                                scalar2=None, op0=ALU.mult)
                e.drain()
                e.tensor_scalar(out=thrrep[:], in0=thrrep[:], scalar1=0.5,
                                scalar2=None, op0=ALU.mult)
                e.drain()
                e.tensor_scalar(out=nthrrep[:], in0=thrrep[:], scalar1=-1.0,
                                scalar2=None, op0=ALU.mult)
                e.drain()
                e.engine_nop().then_inc(S["wqF"], 1)
                for i, (m, cidx, half, srcap, oc0, nd) in enumerate(WQPLAN):
                    slot = i % 2
                    sem = S["wqT1" if slot == 0 else "wqT2"]
                    tgt = wqst[:, slot * 2048:(slot + 1) * 2048]
                    if m == 1:
                        tgt = tgt.rearrange("p (c e) -> p c e", c=2)
                    if i >= 2:
                        e.wait_ge(S["wqW"], i - 1)
                    e.dma_start(out=tgt, in_=srcap).then_inc(sem, 16)
                    e.wait_ge(S["wqW"], i + 1)
                    if m == 0:
                        e.dma_start(
                            out=dC1[:, half * 4:(half + 1) * 4, cidx, :],
                            in_=wqout[:].rearrange("p (c t) -> p c t", c=4)
                            ).then_inc(S["wqG1"], 16)
                    else:
                        g2, j2 = (2 * cidx) // 4, (2 * cidx) % 4
                        obv = wqout[:].rearrange("p (c e) -> p c e", c=2)
                        e.dma_start(out=dC2[:, 0, g2, j2:j2 + 2, :],
                                    in_=obv[:, :, 0:512]).then_inc(S["wqG1"], 16)
                        e.dma_start(out=dC2[:, 1, g2, j2:j2 + 2, :],
                                    in_=obv[:, :, 512:1024]).then_inc(S["wqG1"], 16)
                e.wait_ge(S["wqG1"], 16 * WQ_OCNT)
                e.engine_nop().then_inc(S["pB2"], 1)

            @block2.sync
            def _(e):
                nd = 0
                if d_wq is not None:
                    e.wait_ge(S["wqF"], 1)
                    e.dma_start(out=d_wq[:, 0:2], in_=thrrep[:]).then_inc(S["dOUT"], 16)
                    e.dma_start(out=d_wq[:, 2:4], in_=mw127[:]).then_inc(S["dOUT"], 16)
                    nd += 2
                if d_oT is not None:
                    e.wait_ge(S["aOE"], NUNITS)
                    e.dma_start(out=d_oT, in_=bufA[:, 0:NFC * T]).then_inc(S["dOUT"], 16)
                    nd += 1
                if d_attnT is not None:
                    e.wait_ge(S["p4qe"], NFC * NTQ)
                    e.dma_start(out=d_attnT, in_=bufC[:]).then_inc(S["dOUT"], 16)
                    nd += 1
                dbg_cnt[0] += nd
                e.wait_ge(S["dOUT"], 16 * dbg_cnt[0])
                e.nop().then_inc(S["pB2"], 1)

        b12.close()

        if phases < 5:
            return nc, dbg_out
        b3 = ExitStack()
        with b3:
            E3 = b3.enter_context
            gbuf = E3(nc.sbuf_tensor("gbuf", [128, NHC * TF], F32))
            s1row = E3(nc.sbuf_tensor("s1row", [1, T], F32))
            k1st = E3(nc.sbuf_tensor("k1st", [128, 2 * NFC * (TF + 2)], BF16))
            c1b = E3(nc.sbuf_tensor("c1b", [128, 2 * NFC * 512], BF16))
            c2b = E3(nc.sbuf_tensor("c2b", [128, 2 * 4 * 512], BF16))
            hch = E3(nc.sbuf_tensor("hch", [128, 2 * (TF + 4)], F32))
            cva = E3(nc.sbuf_tensor("cva", [128, 2 * TF], F32))
            cvb = E3(nc.sbuf_tensor("cvb", [128, 2 * TF], F32))
            cvc = E3(nc.sbuf_tensor("cvc", [128, 2 * TF], F32))
            cvs = E3(nc.sbuf_tensor("cvs", [128, 2 * TF], F32))
            i16c = E3(nc.sbuf_tensor("i16c", [128, 2 * TF], I16))
            gsb = E3(nc.sbuf_tensor("gsb", [128, 2 * TF], F32))
            rr2b = E3(nc.sbuf_tensor("rr2b", [128, 2 * TF], F32))
            Mb = E3(nc.sbuf_tensor("Mb", [128, TF], F32))
            Mn = E3(nc.sbuf_tensor("Mn", [128, TF], F32))
            maCt = E3(nc.sbuf_tensor("maCt", [128, NQT], F32))
            maCrow = E3(nc.sbuf_tensor("maCrow", [1, TF], F32))
            s2rw = E3(nc.sbuf_tensor("s2rw", [1, TF], F32))
            s2erw = E3(nc.sbuf_tensor("s2erw", [1, TF], F32))
            s1bc = E3(nc.sbuf_tensor("s1bc", [128, TF + 4], F32))
            s2bc = E3(nc.sbuf_tensor("s2bc", [128, TF], F32))
            s2ebc = E3(nc.sbuf_tensor("s2ebc", [128, TF], F32))
            t1b = E3(nc.sbuf_tensor("t1b", [128, 2 * TF], F32))
            t2b = E3(nc.sbuf_tensor("t2b", [128, 2 * TF], F32))
            xst2 = E3(nc.sbuf_tensor("xst2", [128, 2 * TF], F32))
            outst = E3(nc.sbuf_tensor("outst", [128, 2 * TF], F32))

            def t_range(tau):
                lo = max(tau * TF - 1, 0)
                hi = min(tau * TF + TF + 1, T)
                return lo, hi, lo - (tau * TF - 1)

            hbank = [psS, psT]
            tbank = [ps[0], ps[1]]

            block3 = E3(nc.Block())

            @block3.sync
            def _(e):
                e.wait_ge(S["pB2"], 3)
                e.wait_ge(S["p4qe"], NFC * NTQ)
                for tau in range(NTF):
                    lo, hi, colo = t_range(tau)
                    w = hi - lo
                    ksl = S["dk1a" if tau % 2 == 0 else "dk1b"]
                    if tau >= 2:
                        e.wait_ge(S["f1"], (tau - 1) * NHC)
                    e.dma_start(
                        out=k1st[:, (tau % 2) * NFC * (TF + 2):(tau % 2 + 1) * NFC * (TF + 2)].rearrange(
                            "p (c t) -> p c t", c=NFC)[:, :, 0:w],
                        in_=dK1[:, :, lo:hi].rearrange("c p t -> p c t")).then_inc(ksl, 16)
                    for hb in range(8):
                        gslot = hb % 2
                        csem = S["dc1a" if gslot == 0 else "dc1b"]
                        ghb = tau * 8 + hb
                        if ghb >= 2:
                            e.wait_ge(S["f1"], (ghb - 1) * 4)
                        e.dma_start(out=c1b[:, gslot * NFC * 512:(gslot + 1) * NFC * 512],
                                    in_=dC1[:, hb].rearrange("p c t -> p (c t)")).then_inc(csem, 16)
                    for half in range(2):
                        for gq in range(8):
                            gslot = gq % 2
                            csem = S["dc2a" if gslot == 0 else "dc2b"]
                            gidx = (tau * 2 + half) * 8 + gq
                            if gidx >= 2:
                                e.wait_ge(S["f2c"], gidx - 1)
                            e.dma_start(out=c2b[:, gslot * 4 * 512:(gslot + 1) * 4 * 512],
                                        in_=dC2[:, half, gq].rearrange("p c t -> p (c t)")).then_inc(csem, 16)
                        for eo in range(4):
                            ec = half * 4 + eo
                            gx = tau * 8 + ec
                            xsem = S["dxta" if ec % 2 == 0 else "dxtb"]
                            if gx >= 2:
                                e.wait_ge(S["f2g"], gx - 1)
                            e.dma_start(out=xst2[:, (ec % 2) * TF:(ec % 2 + 1) * TF],
                                        in_=dXT[ec * 128:(ec + 1) * 128,
                                                tau * TF:(tau + 1) * TF]).then_inc(xsem, 16)
                            e.wait_ge(S["f2g"], gx + 1)
                            e.dma_start(out=dOUT[ec * 128:(ec + 1) * 128,
                                                 tau * TF:(tau + 1) * TF],
                                        in_=outst[:, (ec % 2) * TF:(ec % 2 + 1) * TF]
                                        ).then_inc(S["dosa" if ec % 2 == 0 else "dosb"], 16)
                nd = 0
                if d_k2 is not None:
                    e.wait_ge(S["rnD"], NTF * NHC)
                    e.dma_start(out=d_k2, in_=bufA[:, 0:NHC * TF]).then_inc(S["dOUT"], 16)
                    nd += 1
                if d_g is not None:
                    e.wait_ge(S["sil"], NTF * NHC)
                    e.dma_start(out=d_g, in_=gbuf[:]).then_inc(S["dOUT"], 16)
                    nd += 1
                dbg_cnt[0] += nd
                e.wait_ge(S["dOUT"], 16 * dbg_cnt[0])
                e.wait_ge(S["dosa"], 16 * NTF * 4)
                e.wait_ge(S["dosb"], 16 * NTF * 4)

            @block3.tensor
            def _(e):
                e.wait_ge(S["pB2"], 3)
                e.wait_ge(S["p4qe"], NFC * NTQ)
                e.wait_ge(S["p4qe"], NFC * NTQ)
                e.wait_ge(S["p1d"], NT)
                for b4 in range(NT // 4):
                    if b4 >= 1:
                        e.wait_ge(S["s1q"], b4)
                    for j4 in range(4):
                        i = b4 * 4 + j4
                        ins = e.transpose(out=ps[0][0:1, j4 * 128:(j4 + 1) * 128],
                                          in_=macol[:, i:i + 1], identity=identf[:])
                    ins.then_inc(S["s1p"], 1)
                for tau in range(NTF):
                    lo, hi, colo = t_range(tau)
                    w = hi - lo
                    mainw = min(512, w)
                    tw = w - mainw
                    e.wait_ge(S["s1r"], 1)
                    if tau == 0:
                        e.wait_ge(S["s1q"], NT // 4)
                    if tau >= 1:
                        e.wait_ge(S["f2e"], tau * 8)
                    ins = e.matmul(out=ps[2][:, 0:mainw], lhsT=onesfrow[:],
                                   rhs=s1row[0:1, lo:lo + mainw], start=True, stop=True)
                    if tw > 0:
                        ins = e.matmul(out=ps[3][:, 0:tw], lhsT=onesfrow[:],
                                       rhs=s1row[0:1, lo + mainw:lo + mainw + tw],
                                       start=True, stop=True)
                    ins.then_inc(S["s1c"], 1)
                    ksl = S["dk1a" if tau % 2 == 0 else "dk1b"]
                    e.wait_ge(ksl, 16 * (tau // 2 + 1))
                    for hc in range(NHC):
                        ghc = tau * NHC + hc
                        hb, jj = hc // 4, hc % 4
                        gslot = hb % 2
                        e.wait_ge(S["dc1a" if gslot == 0 else "dc1b"],
                                  16 * ((tau * 8 + hb) // 2 + 1))
                        if ghc >= 2:
                            e.wait_ge(S["f1e"], ghc - 1)
                        bk = hbank[hc % 2]
                        tb = tbank[hc % 2]
                        for dc in range(NFC):
                            lt = c1b[:, gslot * NFC * 512 + dc * 512 + jj * 128:
                                     gslot * NFC * 512 + dc * 512 + (jj + 1) * 128]
                            rh = k1st[:, (tau % 2) * NFC * (TF + 2) + dc * (TF + 2):
                                      (tau % 2) * NFC * (TF + 2) + dc * (TF + 2) + w]
                            ins = e.matmul(out=bk[:, 0:mainw], lhsT=lt,
                                           rhs=rh[:, 0:mainw],
                                           start=(dc == 0), stop=(dc == NFC - 1))
                            if tw > 0:
                                ins = e.matmul(out=tb[:, hc * 2:hc * 2 + tw],
                                               lhsT=lt, rhs=rh[:, mainw:w],
                                               start=(dc == 0), stop=(dc == NFC - 1))
                            if dc == NFC - 1:
                                ins.then_inc(S["f1"], 1)
                    e.wait_ge(S["q1"], tau + 1)
                    e.wait_ge(S["s1e"], tau + 1)
                    for k4 in range(NQT):
                        ins = e.transpose(out=ps[2][:, k4 * 128:(k4 + 1) * 128],
                                          in_=Mb[:, k4 * 128:(k4 + 1) * 128],
                                          identity=identf[:])
                    ins.then_inc(S["q2"], 1)
                    e.wait_ge(S["q3"], tau + 1)
                    for k4 in range(NQT):
                        ins = e.transpose(out=ps[3][0:1, k4 * 128:(k4 + 1) * 128],
                                          in_=maCt[:, k4:k4 + 1], identity=identf[:])
                    ins.then_inc(S["q4"], 1)
                    e.wait_ge(S["q5"], tau + 1)
                    e.matmul(out=ps[2][:, 0:TF], lhsT=onesfrow[:], rhs=s2rw[:],
                             start=True, stop=True)
                    e.matmul(out=ps[3][:, 0:TF], lhsT=onesfrow[:], rhs=s2erw[:],
                             start=True, stop=True).then_inc(S["q6"], 1)
                    e.wait_ge(S["q7"], tau + 1)
                    for half in range(2):
                        if tau * 2 + half >= 1:
                            e.wait_ge(S["f2e"], (tau * 2 + half) * 4)
                        for gq in range(8):
                            gslot = gq % 2
                            e.wait_ge(S["dc2a" if gslot == 0 else "dc2b"],
                                      16 * (((tau * 2 + half) * 8 + gq) // 2 + 1))
                            for jj in range(4):
                                hc = gq * 4 + jj
                                e.wait_ge(S["rnD"], tau * NHC + hc + 1)
                                for eo in range(4):
                                    ins = e.matmul(
                                        out=ps[eo][:, 0:TF],
                                        lhsT=c2b[:, gslot * 4 * 512 + jj * 512 + eo * 128:
                                                 gslot * 4 * 512 + jj * 512 + (eo + 1) * 128],
                                        rhs=bufA[:, hc * TF:(hc + 1) * TF],
                                        start=(hc == 0), stop=(hc == NHC - 1))
                                if jj == 3 and eo == 3:
                                    ins.then_inc(S["f2c"], 1)

            @block3.vector
            def _(e):
                e.wait_ge(S["pB2"], 3)
                e.wait_ge(S["p4qe"], NFC * NTQ)
                for b4 in range(NT // 4):
                    e.wait_ge(S["s1p"], b4 + 1)
                    e.tensor_copy(out=s1row[0:1, b4 * 512:(b4 + 1) * 512],
                                  in_=ps[0][0:1, 0:512])
                    e.drain()
                    e.engine_nop().then_inc(S["s1q"], 1)
                e.wait_ge(S["wqF"], 1)
                e.tensor_scalar(out=s1row[:], in0=s1row[:], scalar1=mw127[0:1, 0:1],
                                scalar2=None, op0=ALU.mult)
                e.drain()
                e.engine_nop().then_inc(S["s1r"], 1)
                for tau in range(NTF):
                    lo, hi, colo = t_range(tau)
                    w = hi - lo
                    mainw = min(512, w)
                    assert w - mainw == 0
                    e.wait_ge(S["s1c"], tau + 1)
                    e.tensor_copy(out=s1bc[:, 0:mainw], in_=ps[2][:, 0:mainw])
                    e.drain()
                    e.engine_nop().then_inc(S["s1e"], 1)
                    for v in range(NHC + 2):
                        if v < NHC:
                            hc = v
                            ghc = tau * NHC + hc
                            slot = hc % 2
                            hsl = hch[:, slot * (TF + 4):(slot + 1) * (TF + 4)]
                            e.wait_ge(S["f1"], ghc + 1)
                            if ghc >= 2:
                                e.wait_ge(S["cv1"], ghc - 1)
                                e.wait_ge(S["cv3"], ghc - 1)
                            if tau == 0:
                                e.memset(hsl[:, 0:1], 0.0)
                                e.drain()
                            if tau == NTF - 1:
                                e.memset(hsl[:, TF + 1:TF + 2], 0.0)
                                e.drain()
                            e.tensor_tensor(out=hsl[:, colo:colo + mainw],
                                            in0=hbank[hc % 2][:, 0:mainw],
                                            in1=s1bc[:, 0:mainw],
                                            op=ALU.mult).then_inc(S["f1e"], 1)
                        if 0 <= v - 1 < NHC:
                            h1 = v - 1
                            g1 = tau * NHC + h1
                            s1_ = h1 % 2
                            hs1 = hch[:, s1_ * (TF + 4):(s1_ + 1) * (TF + 4)]
                            e.wait_ge(S["cv2"], g1 + 1)
                            e.wait_ge(S["f1e"], g1 + 1)
                            if g1 >= 2:
                                e.wait_ge(S["sil"], g1 - 1)
                            e.scalar_tensor_tensor(out=cvc[:, s1_ * TF:(s1_ + 1) * TF],
                                                   in0=hs1[:, 2:TF + 2],
                                                   scalar=cw[:, h1 * 3 + 2:h1 * 3 + 3],
                                                   in1=cvb[:, s1_ * TF:(s1_ + 1) * TF],
                                                   op0=ALU.mult,
                                                   op1=ALU.add).then_inc(S["cv3"], 1)
                        if 0 <= v - 2 < NHC:
                            h2 = v - 2
                            g2 = tau * NHC + h2
                            e.wait_ge(S["sil"], g2 + 1)
                            e.wait_ge(S["qmn"], g2)
                            e.wait_ge(S["qab"], g2)
                            gsl = gbuf[:, h2 * TF:(h2 + 1) * TF]
                            if h2 == 0:
                                if tau >= 1:
                                    e.wait_ge(S["q2"], tau)
                                e.tensor_copy(out=Mb[:], in_=gsl).then_inc(S["qmn"], 1)
                                e.tensor_copy(out=Mn[:], in_=gsl).then_inc(S["qab"], 1)
                            else:
                                e.tensor_tensor(out=Mb[:], in0=Mb[:], in1=gsl,
                                                op=ALU.max).then_inc(S["qmn"], 1)
                                e.tensor_tensor(out=Mn[:], in0=Mn[:], in1=gsl,
                                                op=ALU.min).then_inc(S["qab"], 1)
                    e.wait_ge(S["qmn"], (tau + 1) * NHC)
                    e.wait_ge(S["qab"], (tau + 1) * NHC)
                    e.drain()
                    e.scalar_tensor_tensor(out=Mb[:], in0=Mn[:], scalar=-1.0,
                                           in1=Mb[:], op0=ALU.mult,
                                           op1=ALU.max).then_inc(S["q1"], 1)
                    e.wait_ge(S["q2"], tau + 1)
                    if tau >= 1:
                        e.wait_ge(S["q4"], tau)
                    for k4 in range(NQT):
                        e.reduce_max(out=maCt[:, k4:k4 + 1],
                                     in_=ps[2][:, k4 * 128:(k4 + 1) * 128], axis=AX.X)
                    e.drain()
                    e.tensor_scalar(out=maCt[:], in0=maCt[:], scalar1=1e-5,
                                    scalar2=None, op0=ALU.max).then_inc(S["q3"], 1)
                    e.wait_ge(S["q4"], tau + 1)
                    e.tensor_copy(out=maCrow[0:1, 0:TF], in_=ps[3][0:1, 0:TF])
                    e.drain()
                    e.reciprocal(out=s2rw[:], in_=maCrow[:])
                    e.drain()
                    e.tensor_scalar(out=s2rw[:], in0=s2rw[:], scalar1=127.0,
                                    scalar2=None, op0=ALU.mult)
                    e.tensor_scalar(out=s2erw[:], in0=maCrow[:],
                                    scalar1=mw127[0:1, 1:2], scalar2=None,
                                    op0=ALU.mult).then_inc(S["q5"], 1)
                    e.wait_ge(S["q6"], tau + 1)
                    e.tensor_copy(out=s2bc[:], in_=ps[2][:, 0:TF])
                    e.tensor_copy(out=s2ebc[:], in_=ps[3][:, 0:TF])
                    e.drain()
                    e.engine_nop().then_inc(S["q7"], 1)
                    for v in range(NHC + 2):
                        if v < NHC:
                            hc = v
                            ghc = tau * NHC + hc
                            slot = hc % 2
                            if ghc >= 2:
                                e.wait_ge(S["rnC"], ghc - 1)
                            e.tensor_tensor(out=gsb[:, slot * TF:(slot + 1) * TF],
                                            in0=gbuf[:, hc * TF:(hc + 1) * TF],
                                            in1=s2bc[:], op=ALU.mult).then_inc(S["rnA"], 1)
                        if 0 <= v - 2 < NHC:
                            h2 = v - 2
                            g2 = tau * NHC + h2
                            s2_ = h2 % 2
                            rsl = rr2b[:, s2_ * TF:(s2_ + 1) * TF]
                            e.wait_ge(S["rnC"], g2 + 1)
                            e.tensor_scalar(out=rsl, in0=rsl, scalar1=-1535.5,
                                            scalar2=None, op0=ALU.is_ge)
                            e.drain()
                            if tau == 0 and h2 == 0:
                                e.wait_ge(S["p4q"], NFC * NTQ)
                            e.scalar_tensor_tensor(out=bufA[:, h2 * TF:(h2 + 1) * TF],
                                                   in0=i16c[:, s2_ * TF:(s2_ + 1) * TF],
                                                   scalar=-1536.0,
                                                   in1=rsl, op0=ALU.add,
                                                   op1=ALU.add).then_inc(S["rnD"], 1)
                    for half in range(2):
                        e.wait_ge(S["f2c"], 8 * (tau * 2 + half + 1))
                        for v in range(5):
                            if v < 4:
                                ec = half * 4 + v
                                gx = tau * 8 + ec
                                if gx >= 2:
                                    e.wait_ge(S["f2f"], gx - 1)
                                e.tensor_tensor(out=t1b[:, (ec % 2) * TF:(ec % 2 + 1) * TF],
                                                in0=ps[v][:, 0:TF], in1=s2ebc[:],
                                                op=ALU.mult).then_inc(S["f2e"], 1)
                            if v >= 1:
                                ec1 = half * 4 + v - 1
                                gx1 = tau * 8 + ec1
                                e.wait_ge(S["f2f"], gx1 + 1)
                                e.wait_ge(S["dxta" if ec1 % 2 == 0 else "dxtb"],
                                          16 * (gx1 // 2 + 1))
                                if gx1 >= 2:
                                    e.wait_ge(S["dosa" if ec1 % 2 == 0 else "dosb"],
                                              16 * ((gx1 - 2) // 2 + 1))
                                e.tensor_tensor(out=outst[:, (ec1 % 2) * TF:(ec1 % 2 + 1) * TF],
                                                in0=t2b[:, (ec1 % 2) * TF:(ec1 % 2 + 1) * TF],
                                                in1=xst2[:, (ec1 % 2) * TF:(ec1 % 2 + 1) * TF],
                                                op=ALU.add).then_inc(S["f2g"], 1)

            @block3.scalar
            def _(e):
                e.wait_ge(S["pB2"], 3)
                e.wait_ge(S["p4qe"], NFC * NTQ)
                for tau in range(NTF):
                    for v in range(NHC + 1):
                        if v < NHC:
                            hc = v
                            ghc = tau * NHC + hc
                            slot = hc % 2
                            hsl = hch[:, slot * (TF + 4):(slot + 1) * (TF + 4)]
                            e.wait_ge(S["f1e"], ghc + 1)
                            if ghc >= 2:
                                e.wait_ge(S["cv2"], ghc - 1)
                            e.activation(out=cvs[:, slot * TF:(slot + 1) * TF],
                                         in_=hsl[:, 1:TF + 1], func=AF.Identity,
                                         scale=cw[:, hc * 3 + 1:hc * 3 + 2],
                                         bias=0.0).then_inc(S["cvS"], 1)
                            e.activation(out=cva[:, slot * TF:(slot + 1) * TF],
                                         in_=hsl[:, 0:TF], func=AF.Identity,
                                         scale=cw[:, hc * 3:hc * 3 + 1],
                                         bias=0.0).then_inc(S["cv1"], 1)
                        if 0 <= v - 1 < NHC:
                            h1 = v - 1
                            g1 = tau * NHC + h1
                            s1_ = h1 % 2
                            e.wait_ge(S["cv3"], g1 + 1)
                            if tau >= 1:
                                e.wait_ge(S["rnA"], (tau - 1) * NHC + h1 + 1)
                            e.activation(out=gbuf[:, h1 * TF:(h1 + 1) * TF],
                                         in_=cvc[:, s1_ * TF:(s1_ + 1) * TF],
                                         func=AF.Silu,
                                         bias=cb[:, h1:h1 + 1]).then_inc(S["sil"], 1)
                    for hc in range(NHC):
                        ghc = tau * NHC + hc
                        slot = hc % 2
                        e.wait_ge(S["rnA"], ghc + 1)
                        if ghc >= 2:
                            e.wait_ge(S["rnD"], ghc - 1)
                        e.activation(out=i16c[:, slot * TF:(slot + 1) * TF],
                                     in_=gsb[:, slot * TF:(slot + 1) * TF],
                                     func=AF.Identity,
                                     scale=1.0, bias=c1535[:, 0:1]).then_inc(S["rnB"], 1)

            @block3.gpsimd
            def _(e):
                e.wait_ge(S["pB2"], 3)
                e.wait_ge(S["p4qe"], NFC * NTQ)
                for tau in range(NTF):
                    for v in range(NHC):
                        if True:
                            hc = v
                            ghc = tau * NHC + hc
                            slot = hc % 2
                            e.wait_ge(S["cvS"], ghc + 1)
                            e.wait_ge(S["cv1"], ghc + 1)
                            if ghc >= 2:
                                e.wait_ge(S["cv3"], ghc - 1)
                            e.tensor_tensor(out=cvb[:, slot * TF:(slot + 1) * TF],
                                            in0=cvs[:, slot * TF:(slot + 1) * TF],
                                            in1=cva[:, slot * TF:(slot + 1) * TF],
                                            op=ALU.add).then_inc(S["cv2"], 1)
                    for hc in range(NHC):
                        ghc = tau * NHC + hc
                        slot = hc % 2
                        e.wait_ge(S["rnB"], ghc + 1)
                        if ghc >= 2:
                            e.wait_ge(S["rnD"], ghc - 1)
                        e.tensor_tensor(out=rr2b[:, slot * TF:(slot + 1) * TF],
                                        in0=gsb[:, slot * TF:(slot + 1) * TF],
                                        in1=i16c[:, slot * TF:(slot + 1) * TF],
                                        op=ALU.subtract).then_inc(S["rnC"], 1)
                    for half in range(2):
                        for eo in range(4):
                            ec = half * 4 + eo
                            gx = tau * 8 + ec
                            e.wait_ge(S["f2e"], gx + 1)
                            if gx >= 2:
                                e.wait_ge(S["f2g"], gx - 1)
                            e.tensor_tensor(out=t2b[:, (ec % 2) * TF:(ec % 2 + 1) * TF],
                                            in0=t1b[:, (ec % 2) * TF:(ec % 2 + 1) * TF],
                                            in1=bufC[:, ec * T + tau * TF:ec * T + (tau + 1) * TF],
                                            op=ALU.add).then_inc(S["f2f"], 1)

    return nc, dbg_out




def _prep_inputs(inputs, T):
    x = np.ascontiguousarray(inputs["L0_attn_x"])
    B = x.shape[0]
    shared = {
        "wqT": np.ascontiguousarray(inputs["L0_attn_Wq"].T),
        "wk": np.ascontiguousarray(inputs["L0_attn_Wk"]),
        "wv": np.ascontiguousarray(inputs["L0_attn_Wv"]),
        "wo": np.ascontiguousarray(inputs["L0_attn_Wo"]),
        "wc": np.ascontiguousarray(inputs["L0_attn_Wc"]),
        "bq": np.ascontiguousarray(inputs["L0_attn_bq"].reshape(NFC, 128).T),
        "bk": np.ascontiguousarray(inputs["L0_attn_bk"].reshape(NFC, 128).T),
        "bv": np.ascontiguousarray(inputs["L0_attn_bv"].reshape(1, D)),
        "bo": np.ascontiguousarray(inputs["L0_attn_bo"].reshape(NFC, 128).T),
        "w1T": np.ascontiguousarray(inputs["L0_ffn_w1"].T),
        "w2T": np.ascontiguousarray(inputs["L0_ffn_w2"].T),
        "convw": np.ascontiguousarray(
            inputs["L0_ffn_conv_w"].reshape(NHC, 128, 3).transpose(1, 0, 2).reshape(128, NHC * 3)),
        "convb": np.ascontiguousarray(inputs["L0_ffn_conv_b"].reshape(NHC, 128).T),
        "ident": np.eye(128, dtype=BF16NP),
        "masksub": np.tril(np.full((128, 128), -1e9, np.float32), k=-1),
    }
    maps = []
    for b in range(B):
        m = dict(shared)
        m["x_tok"] = np.ascontiguousarray(x[b, :T, :])
        m["xT"] = np.ascontiguousarray(x[b, :T, :].T)
        maps.append(m)
    return maps


def kernel(**inputs):
    T = 2048
    maps = _prep_inputs(inputs, T)
    nc, _ = build(T=T)
    res = run_bass_kernel_spmd(nc, maps, core_ids=list(range(len(maps))))
    out = np.stack([np.ascontiguousarray(r["outT"].T) for r in res.results])
    return out.astype(np.float32)


# version 19
# speedup vs baseline: 1.0969x; 1.0969x over previous
import numpy as np
import ml_dtypes
from contextlib import ExitStack

import concourse.bass as bass
import concourse.mybir as mybir
from concourse.bass_utils import run_bass_kernel_spmd

F32 = mybir.dt.float32
BF16 = mybir.dt.bfloat16
I16 = mybir.dt.int16
AF = mybir.ActivationFunctionType
ALU = mybir.AluOpType
AX = mybir.AxisListType
BF16NP = ml_dtypes.bfloat16

D = 1024
NH = 16
LAT = 64
DHID = 4096
NFC = D // 128
NHC = DHID // 128
TF = 256
NQT = TF // 128
EPS = 1e-5
SCALE = 0.125


def build(T=2048, dbg=(), phases=5, p2sub=4):
    NT = T // 128
    NTQ = T // 512
    NTF = T // TF
    NQB = NT
    NUNITS = 8 * NQB
    nc = bass.Bass("TRN2", target_bir_lowering=False)

    dX = nc.dram_tensor("x_tok", [T, D], F32, kind="ExternalInput").ap()
    dXT = nc.dram_tensor("xT", [D, T], F32, kind="ExternalInput").ap()
    dWqT = nc.dram_tensor("wqT", [D, D], F32, kind="ExternalInput").ap()
    dWk = nc.dram_tensor("wk", [D, D], F32, kind="ExternalInput").ap()
    dWv = nc.dram_tensor("wv", [D, D], F32, kind="ExternalInput").ap()
    dWo = nc.dram_tensor("wo", [D, D], F32, kind="ExternalInput").ap()
    dWc = nc.dram_tensor("wc", [64, 64], F32, kind="ExternalInput").ap()
    dbq = nc.dram_tensor("bq", [128, NFC], F32, kind="ExternalInput").ap()
    dbk = nc.dram_tensor("bk", [128, NFC], F32, kind="ExternalInput").ap()
    dbv = nc.dram_tensor("bv", [1, D], F32, kind="ExternalInput").ap()
    dbo = nc.dram_tensor("bo", [128, NFC], F32, kind="ExternalInput").ap()
    dW1T = nc.dram_tensor("w1T", [D, DHID], F32, kind="ExternalInput").ap()
    dW2T = nc.dram_tensor("w2T", [DHID, D], F32, kind="ExternalInput").ap()
    dCW = nc.dram_tensor("convw", [128, NHC * 3], F32, kind="ExternalInput").ap()
    dCB = nc.dram_tensor("convb", [128, NHC], F32, kind="ExternalInput").ap()
    dID = nc.dram_tensor("ident", [128, 128], BF16, kind="ExternalInput").ap()
    dMSK = nc.dram_tensor("masksub", [128, 128], F32, kind="ExternalInput").ap()

    dOUT = nc.dram_tensor("outT", [D, T], F32, kind="ExternalOutput").ap()

    dC1 = nc.dram_tensor("c1scr", [128, 8, NFC, 512], BF16).ap()
    dC2 = nc.dram_tensor("c2scr", [128, 2, 8, 4, 512], BF16).ap()
    dK1 = nc.dram_tensor("k1scr", [NFC, 128, T], BF16).ap()

    dbg_out = {}

    def dbgt(name, shape, dt):
        if name in dbg:
            dbg_out[name] = nc.dram_tensor("dbg_" + name, shape, dt,
                                           kind="ExternalOutput").ap()
            return dbg_out[name]
        return None

    d_xnT = dbgt("xnT", [128, NFC * T], BF16)
    d_mac = dbgt("mac", [128, NT], F32)
    d_qcT = dbgt("qcT", [128, NFC * T], BF16)
    d_kT = dbgt("kT", [128, NFC * T], BF16)
    d_vtok = dbgt("vtok", [128, NT * D], BF16)
    d_oT = dbgt("oT", [128, NFC * T], BF16)
    d_attnT = dbgt("attnT", [128, NFC * T], BF16)
    d_k2 = dbgt("k2", [128, NHC * TF], BF16)
    d_g = dbgt("g", [128, NHC * TF], F32)
    d_wq = dbgt("wq", [128, 8], F32)

    W1N = float(DHID * D)

    est = ExitStack()
    with est:
        E = est.enter_context

        bufA = E(nc.sbuf_tensor("bufA", [128, max(NFC * T, NHC * TF)], BF16))
        bufC = E(nc.sbuf_tensor("bufC", [128, NFC * T], BF16))

        ident = E(nc.sbuf_tensor("s_ident", [128, 128], BF16))
        identf = E(nc.sbuf_tensor("s_identf", [128, 128], F32))
        masksub = E(nc.sbuf_tensor("s_masksub", [128, 128], F32))
        wc2 = E(nc.sbuf_tensor("wc2", [128, 64], BF16))
        bqpc = E(nc.sbuf_tensor("bqpc", [128, NFC], F32))
        bkpc = E(nc.sbuf_tensor("bkpc", [128, NFC], F32))
        bopc = E(nc.sbuf_tensor("bopc", [128, NFC], F32))
        bvbf = E(nc.sbuf_tensor("bvbf", [1, D], BF16))
        bqe = E(nc.sbuf_tensor("bqe", [128, NFC], F32))
        bqbf = E(nc.sbuf_tensor("bqbf", [128, NFC], BF16))
        cw = E(nc.sbuf_tensor("cw", [128, NHC * 3], F32))
        cb = E(nc.sbuf_tensor("cb", [128, NHC], F32))
        onesrow = E(nc.sbuf_tensor("onesrow", [1, 128], BF16))
        onescol = E(nc.sbuf_tensor("onescol", [128, 1], BF16))
        onesfrow = E(nc.sbuf_tensor("onesfrow", [1, 128], F32))
        onesfcol = E(nc.sbuf_tensor("onesfcol", [128, 1], F32))
        macol = E(nc.sbuf_tensor("macol", [128, NT], F32))
        wqacc = E(nc.sbuf_tensor("wqacc", [128, 2], F32))
        mwsum = E(nc.sbuf_tensor("mwsum", [1, 2], F32))
        sumrep = E(nc.sbuf_tensor("sumrep", [128, 2], F32))
        thrrep = E(nc.sbuf_tensor("thrrep", [128, 2], F32))
        nthrrep = E(nc.sbuf_tensor("nthrrep", [128, 2], F32))
        mw127 = E(nc.sbuf_tensor("mw127", [128, 2], F32))
        c1535 = E(nc.sbuf_tensor("c1535", [128, 1], F32))
        cneg16 = E(nc.sbuf_tensor("cneg16", [128, 1], F32))

        ps = [E(nc.psum_tensor(f"ps{i}", [128, 512], F32)) for i in range(4)]
        pbf = [E(nc.psum_tensor(f"pbf{i}", [128, 1024], BF16)) for i in range(2)]
        psS = E(nc.psum_tensor("psS", [128, 512], F32))
        psT = E(nc.psum_tensor("psT", [128, 512], F32))
        pv2 = [psS[:, 0:128], psT[:, 0:128]]
        _ban = [psT, psS]
        lsl = [(_ban[0][0:1, 128:256], _ban[0][32:33, 128:256]),
               (_ban[1][0:1, 128:256], _ban[1][32:33, 128:256])]
        lbp = [_ban[0][:, 256:384], _ban[1][:, 256:384]]

        sem_names = ["dIN", "dWc", "dWq", "dWk", "dWv", "dWo", "dX0", "dX1", "dOUT",
                     "dK0", "dK1",
                     "p1a", "p1b", "p1c", "p1d", "p1e", "p1f", "p1gv", "p1gs",
                     "p2f", "p2fe", "p2bq", "p2bqe", "p2m",
                     "p2q", "p2qe", "p2k", "p2ke", "p2v", "p2ve",
                     "aQK", "aRM", "aNMB", "aNM", "aEXP", "aPV", "aLR", "aLB", "aOE",
                     "p4q", "p4qe", "pB1", "pB2",
                     "wqA", "wqB", "wqC", "wqD", "wqT1", "wqT2",
                     "wqE", "wqF", "wqG1", "wqG2", "wqV", "wqW", "qab",
                     "f1", "f1e", "cv1", "cv2", "cv3", "cvS", "sil", "qmn",
                     "q1", "q2", "q3", "q4", "q5", "q6", "q7",
                     "rnA", "rnB", "rnC", "rnD",
                     "f2", "f2c", "f2e", "f2f", "f2g",
                     "dc1a", "dc1b", "dc2a", "dc2b",
                     "dxta", "dxtb", "dosa", "dosb", "dk1a", "dk1b",
                     "s1p", "s1q", "s1r", "s1c", "s1e"]
        S = {n: E(nc.semaphore(n)) for n in sem_names}
        N_IN = 7
        dbg_cnt = [0]

        b12 = ExitStack()
        E12 = b12.enter_context
        bufD = E12(nc.sbuf_tensor("bufD", [128, NFC * T], BF16))
        bufE = E12(nc.sbuf_tensor("bufE", [128, NT * D], BF16))
        wstat = E12(nc.sbuf_tensor("wstat", [128, NFC * D], BF16))

        b1 = ExitStack()
        with b1:
            E1 = b1.enter_context
            wact = E1(nc.sbuf_tensor("wact", [128, NFC * D], BF16))
            xst = E1(nc.sbuf_tensor("xst", [128, 2 * D], F32))
            sqs = E1(nc.sbuf_tensor("sqs", [128, D], BF16))
            xntok = E1(nc.sbuf_tensor("xntok", [128, D], F32))
            xnbf = E1(nc.sbuf_tensor("xnbf", [128, D], BF16))
            k1tok = E1(nc.sbuf_tensor("k1tok", [128, D], BF16))
            i16b = E1(nc.sbuf_tensor("i16b", [128, D], I16))
            rrb = E1(nc.sbuf_tensor("rrb", [128, D], F32))
            st8 = E1(nc.sbuf_tensor("st8", [128, 12], F32))
            k1stg = E1(nc.sbuf_tensor("k1stg", [128, 2 * D], BF16))

            def xslot(i):
                return xst[:, (i % 2) * D:(i % 2 + 1) * D]

            block1 = E1(nc.Block())

            @block1.sync
            def _(e):
                e.dma_start(out=ident[:], in_=dID).then_inc(S["dIN"], 16)
                e.dma_start(out=masksub[:], in_=dMSK).then_inc(S["dIN"], 16)
                e.dma_start(out=bqpc[:], in_=dbq).then_inc(S["dIN"], 16)
                e.dma_start(out=bkpc[:], in_=dbk).then_inc(S["dIN"], 16)
                e.dma_start(out=bopc[:], in_=dbo).then_inc(S["dIN"], 16)
                e.dma_start(out=cw[:], in_=dCW).then_inc(S["dIN"], 16)
                e.dma_start(out=cb[:], in_=dCB).then_inc(S["dIN"], 16)
                for i in range(NT):
                    if i >= 2:
                        e.wait_ge(S["p1c"], i - 1)
                    e.dma_start(out=xslot(i), in_=dX[i * 128:(i + 1) * 128, :]
                                ).then_inc(S["dX0" if i % 2 == 0 else "dX1"], 16)
                    if i >= 1:
                        e.wait_ge(S["p1gs"], 8 * i)
                        j = i - 1
                        e.dma_start(
                            out=dK1[:, :, j * 128:(j + 1) * 128].rearrange("c p t -> p c t"),
                            in_=k1stg[:, (j % 2) * D:(j % 2 + 1) * D].rearrange(
                                "p (c t) -> p c t", c=NFC)
                        ).then_inc(S["dK0" if j % 2 == 0 else "dK1"], 16)
                e.wait_ge(S["p1gs"], 8 * NT)
                j = NT - 1
                e.dma_start(
                    out=dK1[:, :, j * 128:(j + 1) * 128].rearrange("c p t -> p c t"),
                    in_=k1stg[:, (j % 2) * D:(j % 2 + 1) * D].rearrange(
                        "p (c t) -> p c t", c=NFC)
                ).then_inc(S["dK0" if j % 2 == 0 else "dK1"], 16)
                nd = 0
                if d_xnT is not None:
                    e.wait_ge(S["p1gv"], 8 * NT)
                    e.dma_start(out=d_xnT, in_=bufA[:, 0:NFC * T]).then_inc(S["dOUT"], 16)
                    nd += 1
                if d_mac is not None:
                    e.wait_ge(S["p1gv"], 8 * NT)
                    e.dma_start(out=d_mac, in_=macol[:]).then_inc(S["dOUT"], 16)
                    nd += 1
                if d_qcT is not None and phases >= 2 and p2sub >= 2:
                    e.wait_ge(S["p2qe"], NFC * NTQ)
                    e.dma_start(out=d_qcT, in_=bufC[:]).then_inc(S["dOUT"], 16)
                    nd += 1
                if d_kT is not None and phases >= 2 and p2sub >= 3:
                    e.wait_ge(S["p2ke"], NFC * NTQ)
                    e.dma_start(out=d_kT, in_=bufD[:]).then_inc(S["dOUT"], 16)
                    nd += 1
                if d_vtok is not None and phases >= 2 and p2sub >= 4:
                    e.wait_ge(S["p2ve"], NT)
                    e.dma_start(out=d_vtok, in_=bufE[:]).then_inc(S["dOUT"], 16)
                    nd += 1
                dbg_cnt[0] += nd
                e.wait_ge(S["dOUT"], 16 * dbg_cnt[0])
                e.wait_ge(S["dK0"], 16 * ((NT + 1) // 2))
                e.wait_ge(S["dK1"], 16 * (NT // 2))
                e.nop().then_inc(S["pB1"], 1)

            @block1.gpsimd
            def _(e):
                e.dma_start(out=bvbf[:], in_=dbv).then_inc(S["dWc"], 16)
                e.dma_start(out=wc2[0:64, :], in_=dWc).then_inc(S["dWc"], 16)
                e.dma_start(out=wc2[64:128, :], in_=dWc).then_inc(S["dWc"], 16)
                e.dma_start(out=wstat[:].rearrange("p (c f) -> p c f", c=NFC),
                            in_=dWqT.rearrange("(c p) f -> p c f", p=128)
                            ).then_inc(S["dWq"], 16)
                if phases >= 2 and p2sub >= 3:
                    e.wait_ge(S["p2f"], NFC)
                    e.dma_start(out=wstat[:].rearrange("p (c f) -> p c f", c=NFC),
                                in_=dWk.rearrange("(c p) f -> p c f", p=128)
                                ).then_inc(S["dWk"], 16)
                if phases >= 2 and p2sub >= 4:
                    e.wait_ge(S["p2qe"], NFC * NTQ)
                    e.dma_start(out=wact[:].rearrange("p (c f) -> p c f", c=NFC),
                                in_=dWv.rearrange("(c p) f -> p c f", p=128)
                                ).then_inc(S["dWv"], 16)
                e.engine_nop().then_inc(S["pB1"], 1)

            @block1.vector
            def _(e):
                e.memset(onesrow[:], 1.0)
                e.memset(onescol[:], 1.0)
                e.memset(onesfrow[:], 1.0)
                e.memset(onesfcol[:], 1.0)
                e.memset(c1535[:], 1535.5)
                e.memset(cneg16[:], -16.0)
                e.wait_ge(S["dIN"], 16 * N_IN)
                e.tensor_copy(out=identf[:], in_=ident[:])
                e.drain()
                for i in range(NT):
                    sl = xslot(i)
                    e.wait_ge(S["dX0" if i % 2 == 0 else "dX1"], 16 * (i // 2 + 1))
                    e.reduce_sum(out=st8[:, 0:1], in_=sl, axis=AX.X)
                    e.wait_ge(S["p1a"], i + 1)
                    e.drain()
                    e.tensor_scalar(out=st8[:, 2:3], in0=st8[:, 0:1], scalar1=1.0 / D,
                                    scalar2=None, op0=ALU.mult)
                    e.tensor_scalar(out=st8[:, 3:4], in0=st8[:, 1:2], scalar1=1.0 / D,
                                    scalar2=None, op0=ALU.mult)
                    e.drain()
                    e.tensor_tensor(out=st8[:, 4:5], in0=st8[:, 2:3], in1=st8[:, 2:3],
                                    op=ALU.mult)
                    e.drain()
                    e.tensor_tensor(out=st8[:, 5:6], in0=st8[:, 3:4], in1=st8[:, 4:5],
                                    op=ALU.subtract)
                    e.drain()
                    e.tensor_scalar(out=st8[:, 6:7], in0=st8[:, 5:6], scalar1=EPS,
                                    scalar2=None, op0=ALU.add)
                    e.drain()
                    e.reciprocal(out=st8[:, 7:8], in_=st8[:, 6:7]).then_inc(S["p1b"], 1)
                    e.wait_ge(S["p1c"], i + 1)
                    e.reduce_max(out=st8[:, 9:10], in_=xntok[:], axis=AX.X,
                                 apply_absolute_value=True)
                    e.drain()
                    e.tensor_scalar(out=macol[:, i:i + 1], in0=st8[:, 9:10],
                                    scalar1=1e-5, scalar2=None, op0=ALU.max)
                    e.drain()
                    e.reciprocal(out=st8[:, 10:11], in_=macol[:, i:i + 1])
                    e.drain()
                    e.tensor_scalar(out=st8[:, 11:12], in0=st8[:, 10:11],
                                    scalar1=127.0, scalar2=None, op0=ALU.mult)
                    e.drain()
                    if i > 0:
                        e.wait_ge(S["p1f"], 16 * i)
                    e.tensor_scalar(out=i16b[:], in0=xntok[:], scalar1=st8[:, 11:12],
                                    scalar2=1535.5, op0=ALU.mult, op1=ALU.add)
                    e.drain()
                    e.scalar_tensor_tensor(out=rrb[:], in0=xntok[:], scalar=st8[:, 11:12],
                                           in1=i16b[:], op0=ALU.mult, op1=ALU.subtract)
                    e.drain()
                    e.tensor_scalar(out=rrb[:], in0=rrb[:], scalar1=-1535.5,
                                    scalar2=None, op0=ALU.is_ge)
                    e.drain()
                    e.scalar_tensor_tensor(out=k1tok[:], in0=i16b[:], scalar=-1536.0,
                                           in1=rrb[:], op0=ALU.add,
                                           op1=ALU.add).then_inc(S["p1d"], 1)
                    e.wait_ge(S["p1f"], 16 * i + 8)
                    for s in range(8):
                        e.tensor_copy(out=bufA[:, s * T + i * 128: s * T + (i + 1) * 128],
                                      in_=pbf[0][:, s * 128:(s + 1) * 128]
                                      ).then_inc(S["p1gv"], 1)
                if phases < 2:
                    return
                e.wait_ge(S["dIN"], 16 * N_IN)
                e.tensor_copy(out=bqbf[:], in_=bqpc[:])
                e.drain()
                e.engine_nop().then_inc(S["p2m"], 1)
                for fc in range(NFC if p2sub >= 0 else 0):
                    e.wait_ge(S["p2f"], fc + 1)
                    for b in range(2):
                        dst = wact[:, fc * D:(fc + 1) * D].rearrange(
                            "p (j x) -> p j x", x=128)[:, :, b * 64:(b + 1) * 64]
                        e.tensor_scalar(out=dst,
                                        in0=ps[2 + b][:].rearrange("p (j l) -> p j l", l=64),
                                        scalar1=SCALE, scalar2=None, op0=ALU.mult)
                    e.drain()
                    e.engine_nop().then_inc(S["p2fe"], 1)
                if p2sub >= 1:
                    e.wait_ge(S["p2bq"], 1)
                for h in range(NH if p2sub >= 1 else 0):
                    half = (h % 2) * 64
                    e.tensor_scalar(out=bqe[half:half + 64, h // 2:h // 2 + 1],
                                    in0=ps[h % 2][0:64, h // 2:h // 2 + 1], scalar1=SCALE,
                                    scalar2=None, op0=ALU.mult)
                e.drain()
                e.engine_nop().then_inc(S["p2bqe"], 1)
                for i in range(NT if p2sub >= 4 else 0):
                    e.wait_ge(S["p2v"], i + 1)
                    for half in range(2):
                        e.tensor_copy(out=bufE[:, i * D + half * 512:i * D + (half + 1) * 512],
                                      in_=ps[2 + half][:])
                    e.drain()
                    e.engine_nop().then_inc(S["p2ve"], 1)
                e.engine_nop().then_inc(S["pB1"], 1)

            @block1.scalar
            def _(e):
                for i in range(NT):
                    sl = xslot(i)
                    e.wait_ge(S["dX0" if i % 2 == 0 else "dX1"], 16 * (i // 2 + 1))
                    e.activation(out=sqs[:], in_=sl, func=AF.Square,
                                 accum_out=st8[:, 1:2]).then_inc(S["p1a"], 1)
                    e.wait_ge(S["p1b"], i + 1)
                    e.activation(out=st8[:, 8:9], in_=st8[:, 7:8], func=AF.Sqrt)
                    e.drain()
                    e.mul(out=st8[:, 4:5], in_=st8[:, 2:3], mul=st8[:, 8:9])
                    e.drain()
                    e.mul(out=st8[:, 5:6], in_=st8[:, 4:5], mul=-1.0)
                    e.drain()
                    e.activation(out=xntok[:], in_=sl, func=AF.Identity,
                                 scale=st8[:, 8:9], bias=st8[:, 5:6]).then_inc(S["p1c"], 1)
                    e.drain()
                    if i > 0:
                        e.wait_ge(S["p1f"], 16 * i)
                    e.copy(out=xnbf[:], in_=xntok[:]).then_inc(S["p1e"], 1)
                    e.wait_ge(S["p1f"], 16 * i + 16)
                    if i >= 2:
                        e.wait_ge(S["dK0" if i % 2 == 0 else "dK1"],
                                  16 * (i // 2))
                    for s in range(8):
                        e.copy(out=k1stg[:, (i % 2) * D + s * 128:(i % 2) * D + (s + 1) * 128],
                               in_=pbf[1][:, s * 128:(s + 1) * 128]).then_inc(S["p1gs"], 1)
                if phases < 2:
                    return
                if p2sub >= 1:
                    e.wait_ge(S["p2bqe"], 1)
                for u, (oc, tq) in enumerate([(a, b) for a in range(NFC) for b in range(NTQ)] if p2sub >= 2 else []):
                    bank = u % 2
                    e.wait_ge(S["p2q"], u + 1)
                    e.activation(out=bufC[:, oc * T + tq * 512:oc * T + (tq + 1) * 512],
                                 in_=ps[bank][:], func=AF.Identity, scale=1.0,
                                 bias=bqe[:, oc:oc + 1]).then_inc(S["p2qe"], 1)
                for u, (oc, tq) in enumerate([(a, b) for a in range(NFC) for b in range(NTQ)] if p2sub >= 3 else []):
                    bank = u % 2
                    e.wait_ge(S["p2k"], u + 1)
                    e.activation(out=bufD[:, oc * T + tq * 512:oc * T + (tq + 1) * 512],
                                 in_=ps[bank][:], func=AF.Identity, scale=1.0,
                                 bias=bkpc[:, oc:oc + 1]).then_inc(S["p2ke"], 1)

            @block1.tensor
            def _(e):
                e.wait_ge(S["dIN"], 16 * N_IN)
                for i in range(NT):
                    e.wait_ge(S["p1e"], i + 1)
                    e.wait_ge(S["p1d"], i + 1)
                    if i > 0:
                        e.wait_ge(S["p1gv"], 8 * i)
                        e.wait_ge(S["p1gs"], 8 * i)
                    for s in range(16):
                        src = xnbf if s < 8 else k1tok
                        fc = s % 8
                        e.transpose(out=pbf[s // 8][:, fc * 128:(fc + 1) * 128],
                                    in_=src[:, fc * 128:(fc + 1) * 128],
                                    identity=ident[:]).then_inc(S["p1f"], 1)
                if phases < 2:
                    return
                e.wait_ge(S["dWq"], 16)
                e.wait_ge(S["dWc"], 48)
                for fc in range(NFC if p2sub >= 0 else 0):
                    if fc >= 1:
                        e.wait_ge(S["p2fe"], fc)
                    for h in range(NH):
                        half = (h % 2) * 64
                        lhsT = wstat[half:half + 64,
                                     (h // 2) * D + fc * 128:(h // 2) * D + (fc + 1) * 128]
                        out = ps[2 + h % 2][:, (h // 2) * 64:(h // 2) * 64 + 64]
                        ins = e.matmul(out=out, lhsT=lhsT, rhs=wc2[half:half + 64, :],
                                       start=True, stop=True)
                        if h == NH - 1:
                            ins.then_inc(S["p2f"], 1)
                e.wait_ge(S["dIN"], 16 * N_IN)
                e.wait_ge(S["p2m"], 1)
                for h in range(NH if p2sub >= 1 else 0):
                    half = (h % 2) * 64
                    ins = e.matmul(out=ps[h % 2][0:64, h // 2:h // 2 + 1],
                                   lhsT=wc2[half:half + 64, :],
                                   rhs=bqbf[half:half + 64, h // 2:h // 2 + 1],
                                   start=True, stop=True)
                    if h == NH - 1:
                        ins.then_inc(S["p2bq"], 1)
                e.wait_ge(S["p1gv"], 8 * NT)
                if p2sub >= 1:
                    e.wait_ge(S["p2bqe"], 1)
                for u, (oc, tq) in enumerate([(a, b) for a in range(NFC) for b in range(NTQ)] if p2sub >= 2 else []):
                    bank = u % 2
                    if u >= 2:
                        e.wait_ge(S["p2qe"], u - 1)
                    for fc in range(NFC):
                        ins = e.matmul(out=ps[bank][:],
                                       lhsT=wact[:, fc * D + oc * 128:fc * D + (oc + 1) * 128],
                                       rhs=bufA[:, fc * T + tq * 512:fc * T + (tq + 1) * 512],
                                       start=(fc == 0), stop=(fc == NFC - 1))
                        if fc == NFC - 1:
                            ins.then_inc(S["p2q"], 1)
                if p2sub >= 3:
                    e.wait_ge(S["dWk"], 16)
                    e.wait_ge(S["p2qe"], NFC * NTQ)
                for u, (oc, tq) in enumerate([(a, b) for a in range(NFC) for b in range(NTQ)] if p2sub >= 3 else []):
                    bank = u % 2
                    if u >= 2:
                        e.wait_ge(S["p2ke"], u - 1)
                    for fc in range(NFC):
                        ins = e.matmul(out=ps[bank][:],
                                       lhsT=wstat[:, fc * D + oc * 128:fc * D + (oc + 1) * 128],
                                       rhs=bufA[:, fc * T + tq * 512:fc * T + (tq + 1) * 512],
                                       start=(fc == 0), stop=(fc == NFC - 1))
                        if fc == NFC - 1:
                            ins.then_inc(S["p2k"], 1)
                if p2sub >= 4:
                    e.wait_ge(S["dWv"], 16)
                    e.wait_ge(S["dWc"], 48)
                    e.wait_ge(S["p2m"], 1)
                    e.wait_ge(S["p2fe"], NFC)
                for i in range(NT if p2sub >= 4 else 0):
                    if i >= 1:
                        e.wait_ge(S["p2ve"], i)
                    for fc in range(NFC):
                        for half in range(2):
                            e.matmul(out=ps[2 + half][:],
                                     lhsT=bufA[:, fc * T + i * 128:fc * T + (i + 1) * 128],
                                     rhs=wact[:, fc * D + half * 512:fc * D + (half + 1) * 512],
                                     start=(fc == 0), stop=False)
                    for half in range(2):
                        ins = e.matmul(out=ps[2 + half][:], lhsT=onesrow[:],
                                       rhs=bvbf[:, half * 512:(half + 1) * 512],
                                       start=False, stop=True)
                        if half == 1:
                            ins.then_inc(S["p2v"], 1)

        if phases < 3:
            b12.close()
            return nc, dbg_out
        b2 = ExitStack()
        with b2:
            E2 = b2.enter_context
            pt = E2(nc.sbuf_tensor("pt", [128, 4 * T], BF16))
            negm = E2(nc.sbuf_tensor("negm", [128, 4], F32))
            p3row = E2(nc.sbuf_tensor("p3row", [1, 8], F32))
            linv = E2(nc.sbuf_tensor("linv", [1, 256], F32))
            lbs = E2(nc.sbuf_tensor("lbs", [128, 128], F32))
            wqst = E2(nc.sbuf_tensor("wqst", [128, 2 * 2048], F32))
            wqout = E2(nc.sbuf_tensor("wqout", [128, 2048], BF16))
            wqcnb = E2(nc.sbuf_tensor("wqcnb", [128, 2048], BF16))
            wqtmp = E2(nc.sbuf_tensor("wqtmp", [128, 1], F32))

            def p3_sched():
                sched = []
                qk_gidx = 0
                exp_gidx = 0
                last_exp = {}
                for hp in range(8):
                    for qb in range(NQB):
                        u = hp * NQB + qb
                        nkt = qb + 1
                        nch = (nkt + 3) // 4
                        groups = []
                        for c in range(nch):
                            kts = list(range(4 * c, min(4 * c + 4, nkt)))
                            for hs in range(2):
                                bank = 2 * hs + (c % 2)
                                groups.append(dict(hs=hs, c=c, kts=kts, bank=bank,
                                                   qk=qk_gidx, exp=exp_gidx,
                                                   free_at=last_exp.get(bank)))
                                last_exp[bank] = exp_gidx + 1
                                qk_gidx += 1
                                exp_gidx += 1
                        sched.append(dict(u=u, hp=hp, qb=qb, nkt=nkt, nch=nch,
                                          groups=groups, exp_end=exp_gidx))
                return sched

            SCHED = p3_sched()
            WQCH = [(0, ch, half,
                     dW1T[ch * 128:(ch + 1) * 128, half * 2048:(half + 1) * 2048])
                    for ch in range(8) for half in range(2)] + \
                   [(1, pr, 0,
                     dW2T[pr * 256:(pr + 1) * 256, :].rearrange(
                        "(c p) e -> p c e", p=128)) for pr in range(16)]
            WQ_CPU = 1 if NUNITS >= 40 else 2
            WQ_UOF = lambda i: 1 + i // WQ_CPU
            WQ_U1 = WQ_UOF(31) + 2
            WQ_U2 = WQ_U1 + 2
            WQ_TCPU = 1 if NUNITS >= 110 else (2 if NUNITS >= 60 else 4)
            WQ_TOF = lambda i: WQ_U2 + 2 + i // WQ_TCPU
            WQPLAN = []
            _oc = 0
            for _i, (_m, _ci, _hf, _src) in enumerate(WQCH):
                _nd = 1 if _m == 0 else 2
                WQPLAN.append((_m, _ci, _hf, _src, _oc, _nd))
                _oc += _nd
            WQ_OCNT = _oc
            assert WQ_TOF(len(WQCH) - 1) < NUNITS, "wq ternarize schedule overflow"

            def ptsl(u, hs, lo, hi):
                base = (u % 2) * 2 * T + hs * T
                return pt[:, base + lo:base + hi]

            block2 = E2(nc.Block())

            @block2.tensor
            def _(e):
                e.wait_ge(S["pB1"], 3)
                e.wait_ge(S["p2ke"], NFC * NTQ)
                e.wait_ge(S["p2v"], NT)
                first_bank_wait = {0: ("p2ke", NFC * NTQ), 1: ("p2ke", NFC * NTQ),
                                   2: ("p2ve", NT), 3: ("p2ve", NT)}
                for un in SCHED:
                    u, hp, qb, nkt = un["u"], un["hp"], un["qb"], un["nkt"]
                    par = u % 2
                    for g in un["groups"]:
                        hs, c, kts, bank = g["hs"], g["c"], g["kts"], g["bank"]
                        if g["free_at"] is None:
                            sem, val = first_bank_wait.pop(bank, (None, 0))
                            if sem:
                                e.wait_ge(S[sem], val)
                        else:
                            e.wait_ge(S["aEXP"], g["free_at"])
                        hb = hs * 64
                        for j, kt in enumerate(kts):
                            ins = e.matmul(
                                out=ps[bank][:, j * 128:(j + 1) * 128],
                                lhsT=bufD[hb:hb + 64, hp * T + kt * 128:hp * T + (kt + 1) * 128],
                                rhs=bufC[hb:hb + 64, hp * T + qb * 128:hp * T + (qb + 1) * 128],
                                start=True, stop=True)
                            if j == len(kts) - 1:
                                ins.then_inc(S["aQK"], 1)
                    if u >= 1:
                        e.wait_ge(S["aOE"], u - 1)
                        e.wait_ge(S["aLR"], u - 1)
                    if u == WQ_U1:
                        e.wait_ge(S["wqA"], 1)
                        e.matmul(out=_ban[par][0:1, 388:389], lhsT=wqacc[:, 0:1],
                                 rhs=onesfcol[:], start=True, stop=True)
                        e.matmul(out=_ban[par][0:1, 389:390], lhsT=wqacc[:, 1:2],
                                 rhs=onesfcol[:], start=True, stop=True).then_inc(S["wqC"], 1)
                    if u == WQ_U2:
                        e.wait_ge(S["wqB"], 1)
                        e.matmul(out=_ban[par][:, 388:390], lhsT=onesfrow[:],
                                 rhs=mwsum[:], start=True, stop=True).then_inc(S["wqD"], 1)
                    if u == WQ_U1:
                        e.wait_ge(S["wqB"], 1)
                    if u == WQ_U2:
                        e.wait_ge(S["wqE"], 1)
                    if u == 0:
                        e.wait_ge(S["p2v"], NT)
                    lpair = lsl[par]
                    for hs in range(2):
                        for kt in range(nkt):
                            c = kt // 4
                            e.wait_ge(S["aEXP"], un["groups"][2 * c + hs]["exp"] + 1)
                            e.matmul(out=lpair[hs], lhsT=onescol[:],
                                     rhs=ptsl(u, hs, kt * 128, (kt + 1) * 128),
                                     start=(kt == 0), stop=(kt == nkt - 1),
                                     tile_position=(0, 32 * hs))
                    for hs in range(2):
                        for kt in range(nkt):
                            ins = e.matmul(
                                out=pv2[par][hs * 64:(hs + 1) * 64, :],
                                lhsT=bufE[:, kt * D + (2 * hp + hs) * 64:kt * D + (2 * hp + hs + 1) * 64],
                                rhs=ptsl(u, hs, kt * 128, (kt + 1) * 128),
                                start=(kt == 0), stop=(kt == nkt - 1),
                                tile_position=(0, hs * 64))
                            if kt == nkt - 1 and hs == 1:
                                ins.then_inc(S["aPV"], 1)
                    e.wait_ge(S["aLR"], u + 1)
                    e.matmul(out=lbp[par][0:64, :], lhsT=onesfrow[0:1, 0:64],
                             rhs=linv[0:1, 0:128], start=True, stop=True)
                    e.matmul(out=lbp[par][64:128, :], lhsT=onesfrow[0:1, 0:64],
                             rhs=linv[0:1, 128:256], start=True, stop=True,
                             tile_position=(0, 64)).then_inc(S["aLB"], 1)
                e.wait_ge(S["dWo"], 16)
                e.wait_ge(S["aOE"], NUNITS)
                for un2, (oc, tq) in enumerate([(a, b) for a in range(NFC) for b in range(NTQ)]):
                    bank = un2 % 2
                    if un2 >= 2:
                        e.wait_ge(S["p4qe"], un2 - 1)
                    for fc in range(NFC):
                        ins = e.matmul(out=ps[bank][:],
                                       lhsT=wstat[:, fc * D + oc * 128:fc * D + (oc + 1) * 128],
                                       rhs=bufA[:, fc * T + tq * 512:fc * T + (tq + 1) * 512],
                                       start=(fc == 0), stop=(fc == NFC - 1))
                        if fc == NFC - 1:
                            ins.then_inc(S["p4q"], 1)

            @block2.vector
            def _(e):
                e.wait_ge(S["pB1"], 3)
                e.wait_ge(S["p2ke"], NFC * NTQ)
                e.wait_ge(S["p2v"], NT)
                for un in SCHED:
                    u, hp, qb, nkt = un["u"], un["hp"], un["qb"], un["nkt"]
                    par = u % 2
                    dkt = qb
                    c = dkt // 4
                    j = dkt - 4 * c
                    for hs in range(2):
                        g = un["groups"][2 * c + hs]
                        e.wait_ge(S["aQK"], g["qk"] + 1)
                        dia = ps[g["bank"]][:, j * 128:(j + 1) * 128]
                        e.tensor_tensor(out=dia, in0=dia, in1=masksub[:],
                                        op=ALU.add)
                    e.drain()
                    e.engine_nop().then_inc(S["aRM"], 2)
                    for i in range(len(WQCH)):
                        if WQ_UOF(i) != u:
                            continue
                        m = WQCH[i][0]
                        e.wait_ge(S["wqT1" if i % 2 == 0 else "wqT2"], 16 * (i // 2 + 1))
                        sl2 = wqst[:, (i % 2) * 2048:(i % 2 + 1) * 2048]
                        if i == 0 or (m == 1 and WQCH[i - 1][0] == 0):
                            e.reduce_sum(out=wqacc[:, m:m + 1], in_=sl2, axis=AX.X,
                                         apply_absolute_value=True)
                            e.drain()
                        else:
                            e.reduce_sum(out=wqtmp[:], in_=sl2, axis=AX.X,
                                         apply_absolute_value=True)
                            e.drain()
                            e.tensor_tensor(out=wqacc[:, m:m + 1], in0=wqacc[:, m:m + 1],
                                            in1=wqtmp[:], op=ALU.add)
                            e.drain()
                        e.engine_nop().then_inc(S["wqV"], 1)
                        if i == len(WQCH) - 1:
                            e.engine_nop().then_inc(S["wqA"], 1)
                    if u == WQ_U1:
                        e.wait_ge(S["wqC"], 1)
                        e.tensor_copy(out=mwsum[:], in_=_ban[par][0:1, 388:390])
                        e.drain()
                        e.engine_nop().then_inc(S["wqB"], 1)
                    if u == WQ_U2:
                        e.wait_ge(S["wqD"], 1)
                        e.tensor_copy(out=sumrep[:], in_=_ban[par][:, 388:390])
                        e.drain()
                        e.engine_nop().then_inc(S["wqE"], 1)
                    for i3 in range(len(WQPLAN)):
                        if WQ_TOF(i3) != u:
                            continue
                        m3, _c3, _h3, _s3, oc3, nd3 = WQPLAN[i3]
                        slot3 = i3 % 2
                        if i3 == 0:
                            e.wait_ge(S["wqF"], 1)
                        e.wait_ge(S["wqT1" if slot3 == 0 else "wqT2"],
                                  16 * (16 + i3 // 2 + 1))
                        e.wait_ge(S["wqG1"], 16 * oc3)
                        sl3 = wqst[:, slot3 * 2048:(slot3 + 1) * 2048]
                        e.tensor_scalar(out=wqcnb[:], in0=sl3,
                                        scalar1=nthrrep[:, m3:m3 + 1],
                                        scalar2=None, op0=ALU.is_le)
                        e.drain()
                        e.tensor_scalar(out=wqout[:], in0=sl3,
                                        scalar1=thrrep[:, m3:m3 + 1],
                                        scalar2=None, op0=ALU.is_ge)
                        e.drain()
                        e.tensor_tensor(out=wqout[:], in0=wqout[:], in1=wqcnb[:],
                                        op=ALU.subtract)
                        e.drain()
                        e.engine_nop().then_inc(S["wqW"], 1)
                    e.wait_ge(S["aPV"], u + 1)
                    e.reciprocal(out=linv[0:1, 0:128], in_=lsl[par][0])
                    e.reciprocal(out=linv[0:1, 128:256], in_=lsl[par][1]
                                 ).then_inc(S["aLR"], 1)
                    e.wait_ge(S["aLB"], u + 1)
                    e.tensor_copy(out=lbs[:], in_=lbp[par])
                    e.drain()
                    if u == 0:
                        e.wait_ge(S["p2v"], NT)
                    e.tensor_tensor(out=bufA[:, hp * T + qb * 128:hp * T + (qb + 1) * 128],
                                    in0=pv2[par], in1=lbs[:],
                                    op=ALU.mult).then_inc(S["aOE"], 1)
                e.engine_nop().then_inc(S["pB2"], 1)

            @block2.scalar
            def _(e):
                e.wait_ge(S["pB1"], 3)
                e.wait_ge(S["p2ke"], NFC * NTQ)
                e.wait_ge(S["p2v"], NT)
                for un in SCHED:
                    u, hp, qb, nkt = un["u"], un["hp"], un["qb"], un["nkt"]
                    dkt = qb
                    for g in un["groups"]:
                        hs, c, kts, bank = g["hs"], g["c"], g["kts"], g["bank"]
                        e.wait_ge(S["aQK"], g["qk"] + 1)
                        if dkt in kts:
                            e.wait_ge(S["aRM"], 2 * u + 2)
                        if u >= 2:
                            e.wait_ge(S["aPV"], u - 1)
                        wid = len(kts) * 128
                        e.activation(out=ptsl(u, hs, 4 * c * 128, 4 * c * 128 + wid),
                                     in_=ps[bank][:, 0:wid], func=AF.Exp,
                                     bias=cneg16[:, 0:1],
                                     scale=1.0).then_inc(S["aEXP"], 1)
                for un2, (oc, tq) in enumerate([(a, b) for a in range(NFC) for b in range(NTQ)]):
                    bank = un2 % 2
                    e.wait_ge(S["p4q"], un2 + 1)
                    e.activation(out=bufC[:, oc * T + tq * 512:oc * T + (tq + 1) * 512],
                                 in_=ps[bank][:], func=AF.Identity, scale=1.0,
                                 bias=bopc[:, oc:oc + 1]).then_inc(S["p4qe"], 1)

            @block2.gpsimd
            def _(e):
                e.wait_ge(S["pB1"], 3)
                e.wait_ge(S["p2ke"], NFC * NTQ)
                e.wait_ge(S["p2v"], NT)
                e.dma_start(out=wstat[:].rearrange("p (c f) -> p c f", c=NFC),
                            in_=dWo.rearrange("(c p) f -> p c f", p=128)
                            ).then_inc(S["dWo"], 16)
                for i, (m, _c, _h, srcap) in enumerate(WQCH):
                    slot = i % 2
                    sem = S["wqT1" if slot == 0 else "wqT2"]
                    if i >= 2:
                        e.wait_ge(S["wqV"], i - 1)
                    tgt = wqst[:, slot * 2048:(slot + 1) * 2048]
                    if m == 1:
                        tgt = tgt.rearrange("p (c e) -> p c e", c=2)
                    e.dma_start(out=tgt, in_=srcap).then_inc(sem, 16)
                e.wait_ge(S["wqE"], 1)
                e.tensor_scalar(out=thrrep[:], in0=sumrep[:], scalar1=1.0 / W1N,
                                scalar2=None, op0=ALU.mult)
                e.drain()
                e.tensor_scalar(out=thrrep[:], in0=thrrep[:], scalar1=1e-5,
                                scalar2=None, op0=ALU.max)
                e.drain()
                e.tensor_scalar(out=mw127[:], in0=thrrep[:], scalar1=1.0 / 127.0,
                                scalar2=None, op0=ALU.mult)
                e.drain()
                e.tensor_scalar(out=thrrep[:], in0=thrrep[:], scalar1=0.5,
                                scalar2=None, op0=ALU.mult)
                e.drain()
                e.tensor_scalar(out=nthrrep[:], in0=thrrep[:], scalar1=-1.0,
                                scalar2=None, op0=ALU.mult)
                e.drain()
                e.engine_nop().then_inc(S["wqF"], 1)
                for i, (m, cidx, half, srcap, oc0, nd) in enumerate(WQPLAN):
                    slot = i % 2
                    sem = S["wqT1" if slot == 0 else "wqT2"]
                    tgt = wqst[:, slot * 2048:(slot + 1) * 2048]
                    if m == 1:
                        tgt = tgt.rearrange("p (c e) -> p c e", c=2)
                    if i >= 2:
                        e.wait_ge(S["wqW"], i - 1)
                    e.dma_start(out=tgt, in_=srcap).then_inc(sem, 16)
                    e.wait_ge(S["wqW"], i + 1)
                    if m == 0:
                        e.dma_start(
                            out=dC1[:, half * 4:(half + 1) * 4, cidx, :],
                            in_=wqout[:].rearrange("p (c t) -> p c t", c=4)
                            ).then_inc(S["wqG1"], 16)
                    else:
                        g2, j2 = (2 * cidx) // 4, (2 * cidx) % 4
                        obv = wqout[:].rearrange("p (c e) -> p c e", c=2)
                        e.dma_start(out=dC2[:, 0, g2, j2:j2 + 2, :],
                                    in_=obv[:, :, 0:512]).then_inc(S["wqG1"], 16)
                        e.dma_start(out=dC2[:, 1, g2, j2:j2 + 2, :],
                                    in_=obv[:, :, 512:1024]).then_inc(S["wqG1"], 16)
                e.wait_ge(S["wqG1"], 16 * WQ_OCNT)
                e.engine_nop().then_inc(S["pB2"], 1)

            @block2.sync
            def _(e):
                nd = 0
                if d_wq is not None:
                    e.wait_ge(S["wqF"], 1)
                    e.dma_start(out=d_wq[:, 0:2], in_=thrrep[:]).then_inc(S["dOUT"], 16)
                    e.dma_start(out=d_wq[:, 2:4], in_=mw127[:]).then_inc(S["dOUT"], 16)
                    nd += 2
                if d_oT is not None:
                    e.wait_ge(S["aOE"], NUNITS)
                    e.dma_start(out=d_oT, in_=bufA[:, 0:NFC * T]).then_inc(S["dOUT"], 16)
                    nd += 1
                if d_attnT is not None:
                    e.wait_ge(S["p4qe"], NFC * NTQ)
                    e.dma_start(out=d_attnT, in_=bufC[:]).then_inc(S["dOUT"], 16)
                    nd += 1
                dbg_cnt[0] += nd
                e.wait_ge(S["dOUT"], 16 * dbg_cnt[0])
                e.nop().then_inc(S["pB2"], 1)

        b12.close()

        if phases < 5:
            return nc, dbg_out
        b3 = ExitStack()
        with b3:
            E3 = b3.enter_context
            gbuf = E3(nc.sbuf_tensor("gbuf", [128, NHC * TF], F32))
            s1row = E3(nc.sbuf_tensor("s1row", [1, T], F32))
            k1st = E3(nc.sbuf_tensor("k1st", [128, 2 * NFC * (TF + 2)], BF16))
            c1b = E3(nc.sbuf_tensor("c1b", [128, 2 * NFC * 512], BF16))
            c2b = E3(nc.sbuf_tensor("c2b", [128, 2 * 4 * 512], BF16))
            hch = E3(nc.sbuf_tensor("hch", [128, 2 * (TF + 4)], F32))
            cva = E3(nc.sbuf_tensor("cva", [128, 2 * TF], F32))
            cvb = E3(nc.sbuf_tensor("cvb", [128, 2 * TF], F32))
            cvc = E3(nc.sbuf_tensor("cvc", [128, 2 * TF], F32))
            cvs = E3(nc.sbuf_tensor("cvs", [128, 2 * TF], F32))
            i16c = E3(nc.sbuf_tensor("i16c", [128, 2 * TF], I16))
            gsb = E3(nc.sbuf_tensor("gsb", [128, 2 * TF], F32))
            rr2b = E3(nc.sbuf_tensor("rr2b", [128, 2 * TF], F32))
            Mb = E3(nc.sbuf_tensor("Mb", [128, TF], F32))
            Mn = E3(nc.sbuf_tensor("Mn", [128, TF], F32))
            maCt = E3(nc.sbuf_tensor("maCt", [128, NQT], F32))
            maCrow = E3(nc.sbuf_tensor("maCrow", [1, TF], F32))
            s2rw = E3(nc.sbuf_tensor("s2rw", [1, TF], F32))
            s2erw = E3(nc.sbuf_tensor("s2erw", [1, TF], F32))
            s1bc = E3(nc.sbuf_tensor("s1bc", [128, TF + 4], F32))
            s2bc = E3(nc.sbuf_tensor("s2bc", [128, TF], F32))
            s2ebc = E3(nc.sbuf_tensor("s2ebc", [128, TF], F32))
            t1b = E3(nc.sbuf_tensor("t1b", [128, 2 * TF], F32))
            t2b = E3(nc.sbuf_tensor("t2b", [128, 2 * TF], F32))
            xst2 = E3(nc.sbuf_tensor("xst2", [128, 2 * TF], F32))
            outst = E3(nc.sbuf_tensor("outst", [128, 2 * TF], F32))

            def t_range(tau):
                lo = max(tau * TF - 1, 0)
                hi = min(tau * TF + TF + 1, T)
                return lo, hi, lo - (tau * TF - 1)

            hbank = [psS, psT]
            tbank = [ps[0], ps[1]]

            block3 = E3(nc.Block())

            @block3.sync
            def _(e):
                e.wait_ge(S["pB2"], 3)
                e.wait_ge(S["p4qe"], NFC * NTQ)
                for tau in range(NTF):
                    lo, hi, colo = t_range(tau)
                    w = hi - lo
                    ksl = S["dk1a" if tau % 2 == 0 else "dk1b"]
                    if tau >= 2:
                        e.wait_ge(S["f1"], (tau - 1) * NHC)
                    e.dma_start(
                        out=k1st[:, (tau % 2) * NFC * (TF + 2):(tau % 2 + 1) * NFC * (TF + 2)].rearrange(
                            "p (c t) -> p c t", c=NFC)[:, :, 0:w],
                        in_=dK1[:, :, lo:hi].rearrange("c p t -> p c t")).then_inc(ksl, 16)
                    for hb in range(8):
                        gslot = hb % 2
                        csem = S["dc1a" if gslot == 0 else "dc1b"]
                        ghb = tau * 8 + hb
                        if ghb >= 2:
                            e.wait_ge(S["f1"], (ghb - 1) * 4)
                        e.dma_start(out=c1b[:, gslot * NFC * 512:(gslot + 1) * NFC * 512],
                                    in_=dC1[:, hb].rearrange("p c t -> p (c t)")).then_inc(csem, 16)
                    for half in range(2):
                        for gq in range(8):
                            gslot = gq % 2
                            csem = S["dc2a" if gslot == 0 else "dc2b"]
                            gidx = (tau * 2 + half) * 8 + gq
                            if gidx >= 2:
                                e.wait_ge(S["f2c"], gidx - 1)
                            e.dma_start(out=c2b[:, gslot * 4 * 512:(gslot + 1) * 4 * 512],
                                        in_=dC2[:, half, gq].rearrange("p c t -> p (c t)")).then_inc(csem, 16)
                        for eo in range(4):
                            ec = half * 4 + eo
                            gx = tau * 8 + ec
                            xsem = S["dxta" if ec % 2 == 0 else "dxtb"]
                            if gx >= 2:
                                e.wait_ge(S["f2g"], gx - 1)
                            e.dma_start(out=xst2[:, (ec % 2) * TF:(ec % 2 + 1) * TF],
                                        in_=dXT[ec * 128:(ec + 1) * 128,
                                                tau * TF:(tau + 1) * TF]).then_inc(xsem, 16)
                            e.wait_ge(S["f2g"], gx + 1)
                            e.dma_start(out=dOUT[ec * 128:(ec + 1) * 128,
                                                 tau * TF:(tau + 1) * TF],
                                        in_=outst[:, (ec % 2) * TF:(ec % 2 + 1) * TF]
                                        ).then_inc(S["dosa" if ec % 2 == 0 else "dosb"], 16)
                nd = 0
                if d_k2 is not None:
                    e.wait_ge(S["rnD"], NTF * NHC)
                    e.dma_start(out=d_k2, in_=bufA[:, 0:NHC * TF]).then_inc(S["dOUT"], 16)
                    nd += 1
                if d_g is not None:
                    e.wait_ge(S["sil"], NTF * NHC)
                    e.dma_start(out=d_g, in_=gbuf[:]).then_inc(S["dOUT"], 16)
                    nd += 1
                dbg_cnt[0] += nd
                e.wait_ge(S["dOUT"], 16 * dbg_cnt[0])
                e.wait_ge(S["dosa"], 16 * NTF * 4)
                e.wait_ge(S["dosb"], 16 * NTF * 4)

            @block3.tensor
            def _(e):
                e.wait_ge(S["pB2"], 3)
                e.wait_ge(S["p4qe"], NFC * NTQ)
                e.wait_ge(S["p4qe"], NFC * NTQ)
                e.wait_ge(S["p1d"], NT)
                for b4 in range(NT // 4):
                    if b4 >= 1:
                        e.wait_ge(S["s1q"], b4)
                    for j4 in range(4):
                        i = b4 * 4 + j4
                        ins = e.transpose(out=ps[0][0:1, j4 * 128:(j4 + 1) * 128],
                                          in_=macol[:, i:i + 1], identity=identf[:])
                    ins.then_inc(S["s1p"], 1)
                for tau in range(NTF):
                    lo, hi, colo = t_range(tau)
                    w = hi - lo
                    mainw = min(512, w)
                    tw = w - mainw
                    e.wait_ge(S["s1r"], 1)
                    if tau == 0:
                        e.wait_ge(S["s1q"], NT // 4)
                    if tau >= 1:
                        e.wait_ge(S["f2e"], tau * 8)
                    ins = e.matmul(out=ps[2][:, 0:mainw], lhsT=onesfrow[:],
                                   rhs=s1row[0:1, lo:lo + mainw], start=True, stop=True)
                    if tw > 0:
                        ins = e.matmul(out=ps[3][:, 0:tw], lhsT=onesfrow[:],
                                       rhs=s1row[0:1, lo + mainw:lo + mainw + tw],
                                       start=True, stop=True)
                    ins.then_inc(S["s1c"], 1)
                    ksl = S["dk1a" if tau % 2 == 0 else "dk1b"]
                    e.wait_ge(ksl, 16 * (tau // 2 + 1))
                    for hc in range(NHC):
                        ghc = tau * NHC + hc
                        hb, jj = hc // 4, hc % 4
                        gslot = hb % 2
                        e.wait_ge(S["dc1a" if gslot == 0 else "dc1b"],
                                  16 * ((tau * 8 + hb) // 2 + 1))
                        if ghc >= 2:
                            e.wait_ge(S["f1e"], ghc - 1)
                        bk = hbank[hc % 2]
                        tb = tbank[hc % 2]
                        for dc in range(NFC):
                            lt = c1b[:, gslot * NFC * 512 + dc * 512 + jj * 128:
                                     gslot * NFC * 512 + dc * 512 + (jj + 1) * 128]
                            rh = k1st[:, (tau % 2) * NFC * (TF + 2) + dc * (TF + 2):
                                      (tau % 2) * NFC * (TF + 2) + dc * (TF + 2) + w]
                            ins = e.matmul(out=bk[:, 0:mainw], lhsT=lt,
                                           rhs=rh[:, 0:mainw],
                                           start=(dc == 0), stop=(dc == NFC - 1))
                            if tw > 0:
                                ins = e.matmul(out=tb[:, hc * 2:hc * 2 + tw],
                                               lhsT=lt, rhs=rh[:, mainw:w],
                                               start=(dc == 0), stop=(dc == NFC - 1))
                            if dc == NFC - 1:
                                ins.then_inc(S["f1"], 1)
                    e.wait_ge(S["q1"], tau + 1)
                    e.wait_ge(S["s1e"], tau + 1)
                    for k4 in range(NQT):
                        ins = e.transpose(out=ps[2][:, k4 * 128:(k4 + 1) * 128],
                                          in_=Mb[:, k4 * 128:(k4 + 1) * 128],
                                          identity=identf[:])
                    ins.then_inc(S["q2"], 1)
                    e.wait_ge(S["q3"], tau + 1)
                    for k4 in range(NQT):
                        ins = e.transpose(out=ps[3][0:1, k4 * 128:(k4 + 1) * 128],
                                          in_=maCt[:, k4:k4 + 1], identity=identf[:])
                    ins.then_inc(S["q4"], 1)
                    e.wait_ge(S["q5"], tau + 1)
                    e.matmul(out=ps[2][:, 0:TF], lhsT=onesfrow[:], rhs=s2rw[:],
                             start=True, stop=True)
                    e.matmul(out=ps[3][:, 0:TF], lhsT=onesfrow[:], rhs=s2erw[:],
                             start=True, stop=True).then_inc(S["q6"], 1)
                    e.wait_ge(S["q7"], tau + 1)
                    for half in range(2):
                        if tau * 2 + half >= 1:
                            e.wait_ge(S["f2e"], (tau * 2 + half) * 4)
                        for gq in range(8):
                            gslot = gq % 2
                            e.wait_ge(S["dc2a" if gslot == 0 else "dc2b"],
                                      16 * (((tau * 2 + half) * 8 + gq) // 2 + 1))
                            for jj in range(4):
                                hc = gq * 4 + jj
                                e.wait_ge(S["rnD"], tau * NHC + hc + 1)
                                for eo in range(4):
                                    ins = e.matmul(
                                        out=ps[eo][:, 0:TF],
                                        lhsT=c2b[:, gslot * 4 * 512 + jj * 512 + eo * 128:
                                                 gslot * 4 * 512 + jj * 512 + (eo + 1) * 128],
                                        rhs=bufA[:, hc * TF:(hc + 1) * TF],
                                        start=(hc == 0), stop=(hc == NHC - 1))
                                if jj == 3 and eo == 3:
                                    ins.then_inc(S["f2c"], 1)

            @block3.vector
            def _(e):
                e.wait_ge(S["pB2"], 3)
                e.wait_ge(S["p4qe"], NFC * NTQ)
                for b4 in range(NT // 4):
                    e.wait_ge(S["s1p"], b4 + 1)
                    e.tensor_copy(out=s1row[0:1, b4 * 512:(b4 + 1) * 512],
                                  in_=ps[0][0:1, 0:512])
                    e.drain()
                    e.engine_nop().then_inc(S["s1q"], 1)
                e.wait_ge(S["wqF"], 1)
                e.tensor_scalar(out=s1row[:], in0=s1row[:], scalar1=mw127[0:1, 0:1],
                                scalar2=None, op0=ALU.mult)
                e.drain()
                e.engine_nop().then_inc(S["s1r"], 1)
                for tau in range(NTF):
                    lo, hi, colo = t_range(tau)
                    w = hi - lo
                    mainw = min(512, w)
                    assert w - mainw == 0
                    e.wait_ge(S["s1c"], tau + 1)
                    e.tensor_copy(out=s1bc[:, 0:mainw], in_=ps[2][:, 0:mainw])
                    e.drain()
                    e.engine_nop().then_inc(S["s1e"], 1)
                    for v in range(NHC + 2):
                        if v < NHC:
                            hc = v
                            ghc = tau * NHC + hc
                            slot = hc % 2
                            hsl = hch[:, slot * (TF + 4):(slot + 1) * (TF + 4)]
                            e.wait_ge(S["f1"], ghc + 1)
                            if ghc >= 2:
                                e.wait_ge(S["cv1"], ghc - 1)
                                e.wait_ge(S["cv3"], ghc - 1)
                            if tau == 0:
                                e.memset(hsl[:, 0:1], 0.0)
                                e.drain()
                            if tau == NTF - 1:
                                e.memset(hsl[:, TF + 1:TF + 2], 0.0)
                                e.drain()
                            e.tensor_tensor(out=hsl[:, colo:colo + mainw],
                                            in0=hbank[hc % 2][:, 0:mainw],
                                            in1=s1bc[:, 0:mainw],
                                            op=ALU.mult).then_inc(S["f1e"], 1)
                        if 0 <= v - 1 < NHC:
                            h1 = v - 1
                            g1 = tau * NHC + h1
                            s1_ = h1 % 2
                            hs1 = hch[:, s1_ * (TF + 4):(s1_ + 1) * (TF + 4)]
                            e.wait_ge(S["cv2"], g1 + 1)
                            e.wait_ge(S["f1e"], g1 + 1)
                            if g1 >= 2:
                                e.wait_ge(S["sil"], g1 - 1)
                            e.scalar_tensor_tensor(out=cvc[:, s1_ * TF:(s1_ + 1) * TF],
                                                   in0=hs1[:, 2:TF + 2],
                                                   scalar=cw[:, h1 * 3 + 2:h1 * 3 + 3],
                                                   in1=cvb[:, s1_ * TF:(s1_ + 1) * TF],
                                                   op0=ALU.mult,
                                                   op1=ALU.add).then_inc(S["cv3"], 1)
                        if 0 <= v - 2 < NHC:
                            h2 = v - 2
                            g2 = tau * NHC + h2
                            e.wait_ge(S["sil"], g2 + 1)
                            e.wait_ge(S["qmn"], g2)
                            e.wait_ge(S["qab"], g2)
                            gsl = gbuf[:, h2 * TF:(h2 + 1) * TF]
                            if h2 == 0:
                                if tau >= 1:
                                    e.wait_ge(S["q2"], tau)
                                e.tensor_copy(out=Mb[:], in_=gsl).then_inc(S["qmn"], 1)
                                e.tensor_copy(out=Mn[:], in_=gsl).then_inc(S["qab"], 1)
                            else:
                                e.tensor_tensor(out=Mb[:], in0=Mb[:], in1=gsl,
                                                op=ALU.max).then_inc(S["qmn"], 1)
                                e.tensor_tensor(out=Mn[:], in0=Mn[:], in1=gsl,
                                                op=ALU.min).then_inc(S["qab"], 1)
                    e.wait_ge(S["qmn"], (tau + 1) * NHC)
                    e.wait_ge(S["qab"], (tau + 1) * NHC)
                    e.drain()
                    e.scalar_tensor_tensor(out=Mb[:], in0=Mn[:], scalar=-1.0,
                                           in1=Mb[:], op0=ALU.mult,
                                           op1=ALU.max).then_inc(S["q1"], 1)
                    e.wait_ge(S["q2"], tau + 1)
                    if tau >= 1:
                        e.wait_ge(S["q4"], tau)
                    for k4 in range(NQT):
                        e.reduce_max(out=maCt[:, k4:k4 + 1],
                                     in_=ps[2][:, k4 * 128:(k4 + 1) * 128], axis=AX.X)
                    e.drain()
                    e.tensor_scalar(out=maCt[:], in0=maCt[:], scalar1=1e-5,
                                    scalar2=None, op0=ALU.max).then_inc(S["q3"], 1)
                    e.wait_ge(S["q4"], tau + 1)
                    e.tensor_copy(out=maCrow[0:1, 0:TF], in_=ps[3][0:1, 0:TF])
                    e.drain()
                    e.reciprocal(out=s2rw[:], in_=maCrow[:])
                    e.drain()
                    e.tensor_scalar(out=s2rw[:], in0=s2rw[:], scalar1=127.0,
                                    scalar2=None, op0=ALU.mult)
                    e.tensor_scalar(out=s2erw[:], in0=maCrow[:],
                                    scalar1=mw127[0:1, 1:2], scalar2=None,
                                    op0=ALU.mult).then_inc(S["q5"], 1)
                    e.wait_ge(S["q6"], tau + 1)
                    e.tensor_copy(out=s2bc[:], in_=ps[2][:, 0:TF])
                    e.tensor_copy(out=s2ebc[:], in_=ps[3][:, 0:TF])
                    e.drain()
                    e.engine_nop().then_inc(S["q7"], 1)
                    for v in range(NHC + 2):
                        if v < NHC:
                            hc = v
                            ghc = tau * NHC + hc
                            slot = hc % 2
                            if ghc >= 2:
                                e.wait_ge(S["rnC"], ghc - 1)
                            e.tensor_tensor(out=gsb[:, slot * TF:(slot + 1) * TF],
                                            in0=gbuf[:, hc * TF:(hc + 1) * TF],
                                            in1=s2bc[:], op=ALU.mult).then_inc(S["rnA"], 1)
                        if 0 <= v - 2 < NHC:
                            h2 = v - 2
                            g2 = tau * NHC + h2
                            s2_ = h2 % 2
                            rsl = rr2b[:, s2_ * TF:(s2_ + 1) * TF]
                            e.wait_ge(S["rnC"], g2 + 1)
                            e.tensor_scalar(out=rsl, in0=rsl, scalar1=-1535.5,
                                            scalar2=None, op0=ALU.is_ge)
                            e.drain()
                            if tau == 0 and h2 == 0:
                                e.wait_ge(S["p4q"], NFC * NTQ)
                            e.scalar_tensor_tensor(out=bufA[:, h2 * TF:(h2 + 1) * TF],
                                                   in0=i16c[:, s2_ * TF:(s2_ + 1) * TF],
                                                   scalar=-1536.0,
                                                   in1=rsl, op0=ALU.add,
                                                   op1=ALU.add).then_inc(S["rnD"], 1)
                    for half in range(2):
                        e.wait_ge(S["f2c"], 8 * (tau * 2 + half + 1))
                        for v in range(5):
                            if v < 4:
                                ec = half * 4 + v
                                gx = tau * 8 + ec
                                if gx >= 2:
                                    e.wait_ge(S["f2f"], gx - 1)
                                e.tensor_tensor(out=t1b[:, (ec % 2) * TF:(ec % 2 + 1) * TF],
                                                in0=ps[v][:, 0:TF], in1=s2ebc[:],
                                                op=ALU.mult).then_inc(S["f2e"], 1)
                            if v >= 1:
                                ec1 = half * 4 + v - 1
                                gx1 = tau * 8 + ec1
                                e.wait_ge(S["f2f"], gx1 + 1)
                                e.wait_ge(S["dxta" if ec1 % 2 == 0 else "dxtb"],
                                          16 * (gx1 // 2 + 1))
                                if gx1 >= 2:
                                    e.wait_ge(S["dosa" if ec1 % 2 == 0 else "dosb"],
                                              16 * ((gx1 - 2) // 2 + 1))
                                e.tensor_tensor(out=outst[:, (ec1 % 2) * TF:(ec1 % 2 + 1) * TF],
                                                in0=t2b[:, (ec1 % 2) * TF:(ec1 % 2 + 1) * TF],
                                                in1=xst2[:, (ec1 % 2) * TF:(ec1 % 2 + 1) * TF],
                                                op=ALU.add).then_inc(S["f2g"], 1)

            @block3.scalar
            def _(e):
                e.wait_ge(S["pB2"], 3)
                e.wait_ge(S["p4qe"], NFC * NTQ)
                for tau in range(NTF):
                    for v in range(NHC + 1):
                        if v < NHC:
                            hc = v
                            ghc = tau * NHC + hc
                            slot = hc % 2
                            hsl = hch[:, slot * (TF + 4):(slot + 1) * (TF + 4)]
                            e.wait_ge(S["f1e"], ghc + 1)
                            if ghc >= 2:
                                e.wait_ge(S["cv2"], ghc - 1)
                            e.activation(out=cvs[:, slot * TF:(slot + 1) * TF],
                                         in_=hsl[:, 1:TF + 1], func=AF.Identity,
                                         scale=cw[:, hc * 3 + 1:hc * 3 + 2],
                                         bias=0.0).then_inc(S["cvS"], 1)
                            e.activation(out=cva[:, slot * TF:(slot + 1) * TF],
                                         in_=hsl[:, 0:TF], func=AF.Identity,
                                         scale=cw[:, hc * 3:hc * 3 + 1],
                                         bias=0.0).then_inc(S["cv1"], 1)
                        if 0 <= v - 1 < NHC:
                            h1 = v - 1
                            g1 = tau * NHC + h1
                            s1_ = h1 % 2
                            e.wait_ge(S["cv3"], g1 + 1)
                            if tau >= 1:
                                e.wait_ge(S["rnA"], (tau - 1) * NHC + h1 + 1)
                            e.activation(out=gbuf[:, h1 * TF:(h1 + 1) * TF],
                                         in_=cvc[:, s1_ * TF:(s1_ + 1) * TF],
                                         func=AF.Silu,
                                         bias=cb[:, h1:h1 + 1]).then_inc(S["sil"], 1)
                    for hc in range(NHC):
                        ghc = tau * NHC + hc
                        slot = hc % 2
                        e.wait_ge(S["rnA"], ghc + 1)
                        if ghc >= 2:
                            e.wait_ge(S["rnD"], ghc - 1)
                        e.activation(out=i16c[:, slot * TF:(slot + 1) * TF],
                                     in_=gsb[:, slot * TF:(slot + 1) * TF],
                                     func=AF.Identity,
                                     scale=1.0, bias=c1535[:, 0:1]).then_inc(S["rnB"], 1)

            @block3.gpsimd
            def _(e):
                e.wait_ge(S["pB2"], 3)
                e.wait_ge(S["p4qe"], NFC * NTQ)
                for tau in range(NTF):
                    for v in range(NHC):
                        if True:
                            hc = v
                            ghc = tau * NHC + hc
                            slot = hc % 2
                            e.wait_ge(S["cvS"], ghc + 1)
                            e.wait_ge(S["cv1"], ghc + 1)
                            if ghc >= 2:
                                e.wait_ge(S["cv3"], ghc - 1)
                            e.tensor_tensor(out=cvb[:, slot * TF:(slot + 1) * TF],
                                            in0=cvs[:, slot * TF:(slot + 1) * TF],
                                            in1=cva[:, slot * TF:(slot + 1) * TF],
                                            op=ALU.add).then_inc(S["cv2"], 1)
                    for hc in range(NHC):
                        ghc = tau * NHC + hc
                        slot = hc % 2
                        e.wait_ge(S["rnB"], ghc + 1)
                        if ghc >= 2:
                            e.wait_ge(S["rnD"], ghc - 1)
                        e.tensor_tensor(out=rr2b[:, slot * TF:(slot + 1) * TF],
                                        in0=gsb[:, slot * TF:(slot + 1) * TF],
                                        in1=i16c[:, slot * TF:(slot + 1) * TF],
                                        op=ALU.subtract).then_inc(S["rnC"], 1)
                    for half in range(2):
                        for eo in range(4):
                            ec = half * 4 + eo
                            gx = tau * 8 + ec
                            e.wait_ge(S["f2e"], gx + 1)
                            if gx >= 2:
                                e.wait_ge(S["f2g"], gx - 1)
                            e.tensor_tensor(out=t2b[:, (ec % 2) * TF:(ec % 2 + 1) * TF],
                                            in0=t1b[:, (ec % 2) * TF:(ec % 2 + 1) * TF],
                                            in1=bufC[:, ec * T + tau * TF:ec * T + (tau + 1) * TF],
                                            op=ALU.add).then_inc(S["f2f"], 1)

    return nc, dbg_out




def _prep_inputs(inputs, T):
    x = np.ascontiguousarray(inputs["L0_attn_x"])
    B = x.shape[0]
    shared = {
        "wqT": np.ascontiguousarray(inputs["L0_attn_Wq"].T),
        "wk": np.ascontiguousarray(inputs["L0_attn_Wk"]),
        "wv": np.ascontiguousarray(inputs["L0_attn_Wv"]),
        "wo": np.ascontiguousarray(inputs["L0_attn_Wo"]),
        "wc": np.ascontiguousarray(inputs["L0_attn_Wc"]),
        "bq": np.ascontiguousarray(inputs["L0_attn_bq"].reshape(NFC, 128).T),
        "bk": np.ascontiguousarray(inputs["L0_attn_bk"].reshape(NFC, 128).T),
        "bv": np.ascontiguousarray(inputs["L0_attn_bv"].reshape(1, D)),
        "bo": np.ascontiguousarray(inputs["L0_attn_bo"].reshape(NFC, 128).T),
        "w1T": np.ascontiguousarray(inputs["L0_ffn_w1"].T),
        "w2T": np.ascontiguousarray(inputs["L0_ffn_w2"].T),
        "convw": np.ascontiguousarray(
            inputs["L0_ffn_conv_w"].reshape(NHC, 128, 3).transpose(1, 0, 2).reshape(128, NHC * 3)),
        "convb": np.ascontiguousarray(inputs["L0_ffn_conv_b"].reshape(NHC, 128).T),
        "ident": np.eye(128, dtype=BF16NP),
        "masksub": np.tril(np.full((128, 128), -1e9, np.float32), k=-1),
    }
    maps = []
    for b in range(B):
        m = dict(shared)
        m["x_tok"] = np.ascontiguousarray(x[b, :T, :])
        m["xT"] = np.ascontiguousarray(x[b, :T, :].T)
        maps.append(m)
    return maps


def kernel(**inputs):
    T = 2048
    maps = _prep_inputs(inputs, T)
    nc, _ = build(T=T)
    res = run_bass_kernel_spmd(nc, maps, core_ids=list(range(len(maps))))
    out = np.stack([np.ascontiguousarray(r["outT"].T) for r in res.results])
    return out.astype(np.float32)


# version 23 (final)
# speedup vs baseline: 1.1515x; 1.0498x over previous
import numpy as np
import ml_dtypes
from contextlib import ExitStack

import concourse.bass as bass
import concourse.mybir as mybir
from concourse.bass_utils import run_bass_kernel_spmd

F32 = mybir.dt.float32
BF16 = mybir.dt.bfloat16
I16 = mybir.dt.int16
AF = mybir.ActivationFunctionType
ALU = mybir.AluOpType
AX = mybir.AxisListType
BF16NP = ml_dtypes.bfloat16

D = 1024
NH = 16
LAT = 64
DHID = 4096
NFC = D // 128
NHC = DHID // 128
TF = 256
NQT = TF // 128
EPS = 1e-5
SCALE = 0.125


def build(T=2048, dbg=(), phases=5, p2sub=4):
    NT = T // 128
    NTQ = T // 512
    NTF = T // TF
    NQB = NT
    NUNITS = 8 * NQB
    nc = bass.Bass("TRN2", target_bir_lowering=False)

    dX = nc.dram_tensor("x_tok", [T, D], F32, kind="ExternalInput").ap()
    dXT = nc.dram_tensor("xT", [D, T], F32, kind="ExternalInput").ap()
    dWqT = nc.dram_tensor("wqT", [D, D], F32, kind="ExternalInput").ap()
    dWk = nc.dram_tensor("wk", [D, D], F32, kind="ExternalInput").ap()
    dWv = nc.dram_tensor("wv", [D, D], F32, kind="ExternalInput").ap()
    dWo = nc.dram_tensor("wo", [D, D], F32, kind="ExternalInput").ap()
    dWc = nc.dram_tensor("wc", [64, 64], F32, kind="ExternalInput").ap()
    dbq = nc.dram_tensor("bq", [128, NFC], F32, kind="ExternalInput").ap()
    dbk = nc.dram_tensor("bk", [128, NFC], F32, kind="ExternalInput").ap()
    dbv = nc.dram_tensor("bv", [1, D], F32, kind="ExternalInput").ap()
    dbo = nc.dram_tensor("bo", [128, NFC], F32, kind="ExternalInput").ap()
    dW1T = nc.dram_tensor("w1T", [D, DHID], F32, kind="ExternalInput").ap()
    dW2T = nc.dram_tensor("w2T", [DHID, D], F32, kind="ExternalInput").ap()
    dCW = nc.dram_tensor("convw", [128, NHC * 3], F32, kind="ExternalInput").ap()
    dCB = nc.dram_tensor("convb", [128, NHC], F32, kind="ExternalInput").ap()
    dID = nc.dram_tensor("ident", [128, 128], BF16, kind="ExternalInput").ap()
    dMSK = nc.dram_tensor("masksub", [128, 128], F32, kind="ExternalInput").ap()

    dOUT = nc.dram_tensor("outT", [D, T], F32, kind="ExternalOutput").ap()

    dC1 = nc.dram_tensor("c1scr", [128, 8, NFC, 512], BF16).ap()
    dC2 = nc.dram_tensor("c2scr", [128, 2, 8, 4, 512], BF16).ap()
    dK1 = nc.dram_tensor("k1scr", [NFC, 128, T], BF16).ap()

    dbg_out = {}

    def dbgt(name, shape, dt):
        if name in dbg:
            dbg_out[name] = nc.dram_tensor("dbg_" + name, shape, dt,
                                           kind="ExternalOutput").ap()
            return dbg_out[name]
        return None

    d_xnT = dbgt("xnT", [128, NFC * T], BF16)
    d_mac = dbgt("mac", [128, NT], F32)
    d_qcT = dbgt("qcT", [128, NFC * T], BF16)
    d_kT = dbgt("kT", [128, NFC * T], BF16)
    d_vtok = dbgt("vtok", [128, NT * D], BF16)
    d_oT = dbgt("oT", [128, NFC * T], BF16)
    d_attnT = dbgt("attnT", [128, NFC * T], BF16)
    d_k2 = dbgt("k2", [128, NHC * TF], BF16)
    d_g = dbgt("g", [128, NHC * TF], F32)
    d_wq = dbgt("wq", [128, 8], F32)

    W1N = float(DHID * D)

    est = ExitStack()
    with est:
        E = est.enter_context

        bufA = E(nc.sbuf_tensor("bufA", [128, max(NFC * T, NHC * TF)], BF16))
        bufC = E(nc.sbuf_tensor("bufC", [128, NFC * T], BF16))

        ident = E(nc.sbuf_tensor("s_ident", [128, 128], BF16))
        identf = E(nc.sbuf_tensor("s_identf", [128, 128], F32))
        masksub = E(nc.sbuf_tensor("s_masksub", [128, 128], F32))
        wc2 = E(nc.sbuf_tensor("wc2", [128, 64], BF16))
        bqpc = E(nc.sbuf_tensor("bqpc", [128, NFC], F32))
        bkpc = E(nc.sbuf_tensor("bkpc", [128, NFC], F32))
        bopc = E(nc.sbuf_tensor("bopc", [128, NFC], F32))
        bvbf = E(nc.sbuf_tensor("bvbf", [1, D], BF16))
        bqe = E(nc.sbuf_tensor("bqe", [128, NFC], F32))
        bqbf = E(nc.sbuf_tensor("bqbf", [128, NFC], BF16))
        cw = E(nc.sbuf_tensor("cw", [128, NHC * 3], F32))
        cb = E(nc.sbuf_tensor("cb", [128, NHC], F32))
        onesrow = E(nc.sbuf_tensor("onesrow", [1, 128], BF16))
        onescol = E(nc.sbuf_tensor("onescol", [128, 1], BF16))
        onesfrow = E(nc.sbuf_tensor("onesfrow", [1, 128], F32))
        onesfcol = E(nc.sbuf_tensor("onesfcol", [128, 1], F32))
        macol = E(nc.sbuf_tensor("macol", [128, NT], F32))
        wqacc = E(nc.sbuf_tensor("wqacc", [128, 2], F32))
        mwsum = E(nc.sbuf_tensor("mwsum", [1, 2], F32))
        sumrep = E(nc.sbuf_tensor("sumrep", [128, 2], F32))
        thrrep = E(nc.sbuf_tensor("thrrep", [128, 2], F32))
        nthrrep = E(nc.sbuf_tensor("nthrrep", [128, 2], F32))
        mw127 = E(nc.sbuf_tensor("mw127", [128, 2], F32))
        c1535 = E(nc.sbuf_tensor("c1535", [128, 1], F32))
        cneg16 = E(nc.sbuf_tensor("cneg16", [128, 1], F32))

        ps = [E(nc.psum_tensor(f"ps{i}", [128, 512], F32)) for i in range(4)]
        pbf = [E(nc.psum_tensor(f"pbf{i}", [128, 1024], BF16)) for i in range(2)]
        psS = E(nc.psum_tensor("psS", [128, 512], F32))
        psT = E(nc.psum_tensor("psT", [128, 512], F32))
        pv2 = [psS[:, 0:128], psT[:, 0:128]]
        _ban = [psT, psS]
        lsl = [(_ban[0][0:1, 128:256], _ban[0][32:33, 128:256]),
               (_ban[1][0:1, 128:256], _ban[1][32:33, 128:256])]
        lbp = [_ban[0][:, 256:384], _ban[1][:, 256:384]]

        sem_names = ["dIN", "dWc", "dWq", "dWk", "dWv", "dWo", "dX0", "dX1", "dOUT",
                     "dK0", "dK1",
                     "p1a", "p1b", "p1c", "p1d", "p1e", "p1f", "p1gv", "p1gs",
                     "p2f", "p2fe", "p2bq", "p2bqe", "p2m",
                     "p2q", "p2qe", "p2k", "p2ke", "p2v", "p2ve",
                     "aQK", "aRM", "aNMB", "aNM", "aEXP", "aPV", "aLR", "aLB", "aOE",
                     "p4q", "p4qe", "pB1", "pB2",
                     "wqA", "wqB", "wqC", "wqD", "wqT1", "wqT2",
                     "wqE", "wqF", "wqG1", "wqG2", "wqV", "wqW", "qab",
                     "f1", "f1e", "cv1", "cv2", "cv3", "cvS", "sil", "qmn",
                     "q1", "q2", "q3", "q4", "q5", "q6", "q7",
                     "rnA", "rnB", "rnC", "rnD",
                     "f2", "f2c", "f2e", "f2f", "f2g",
                     "dc1a", "dc1b", "dc2a", "dc2b",
                     "dxta", "dxtb", "dosa", "dosb", "dk1a", "dk1b",
                     "s1p", "s1q", "s1r", "s1c", "s1e"]
        S = {n: E(nc.semaphore(n)) for n in sem_names}
        N_IN = 7
        dbg_cnt = [0]

        b12 = ExitStack()
        E12 = b12.enter_context
        bufD = E12(nc.sbuf_tensor("bufD", [128, NFC * T], BF16))
        bufE = E12(nc.sbuf_tensor("bufE", [128, NT * D], BF16))
        wstat = E12(nc.sbuf_tensor("wstat", [128, NFC * D], BF16))

        b1 = ExitStack()
        with b1:
            E1 = b1.enter_context
            wact = E1(nc.sbuf_tensor("wact", [128, NFC * D], BF16))
            xst = E1(nc.sbuf_tensor("xst", [128, 2 * D], F32))
            sqs = E1(nc.sbuf_tensor("sqs", [128, D], BF16))
            xntok = E1(nc.sbuf_tensor("xntok", [128, D], F32))
            xnbf = E1(nc.sbuf_tensor("xnbf", [128, D], BF16))
            k1tok = E1(nc.sbuf_tensor("k1tok", [128, D], BF16))
            i16b = E1(nc.sbuf_tensor("i16b", [128, D], I16))
            rrb = E1(nc.sbuf_tensor("rrb", [128, D], F32))
            st8 = E1(nc.sbuf_tensor("st8", [128, 12], F32))
            k1stg = E1(nc.sbuf_tensor("k1stg", [128, 2 * D], BF16))

            def xslot(i):
                return xst[:, (i % 2) * D:(i % 2 + 1) * D]

            block1 = E1(nc.Block())

            @block1.sync
            def _(e):
                e.dma_start(out=ident[:], in_=dID).then_inc(S["dIN"], 16)
                e.dma_start(out=masksub[:], in_=dMSK).then_inc(S["dIN"], 16)
                e.dma_start(out=bqpc[:], in_=dbq).then_inc(S["dIN"], 16)
                e.dma_start(out=bkpc[:], in_=dbk).then_inc(S["dIN"], 16)
                e.dma_start(out=bopc[:], in_=dbo).then_inc(S["dIN"], 16)
                e.dma_start(out=cw[:], in_=dCW).then_inc(S["dIN"], 16)
                e.dma_start(out=cb[:], in_=dCB).then_inc(S["dIN"], 16)
                for i in range(NT):
                    if i >= 2:
                        e.wait_ge(S["p1c"], i - 1)
                    e.dma_start(out=xslot(i), in_=dX[i * 128:(i + 1) * 128, :]
                                ).then_inc(S["dX0" if i % 2 == 0 else "dX1"], 16)
                    if i >= 1:
                        e.wait_ge(S["p1gs"], 8 * i)
                        j = i - 1
                        e.dma_start(
                            out=dK1[:, :, j * 128:(j + 1) * 128].rearrange("c p t -> p c t"),
                            in_=k1stg[:, (j % 2) * D:(j % 2 + 1) * D].rearrange(
                                "p (c t) -> p c t", c=NFC)
                        ).then_inc(S["dK0" if j % 2 == 0 else "dK1"], 16)
                e.wait_ge(S["p1gs"], 8 * NT)
                j = NT - 1
                e.dma_start(
                    out=dK1[:, :, j * 128:(j + 1) * 128].rearrange("c p t -> p c t"),
                    in_=k1stg[:, (j % 2) * D:(j % 2 + 1) * D].rearrange(
                        "p (c t) -> p c t", c=NFC)
                ).then_inc(S["dK0" if j % 2 == 0 else "dK1"], 16)
                nd = 0
                if d_xnT is not None:
                    e.wait_ge(S["p1gv"], 8 * NT)
                    e.dma_start(out=d_xnT, in_=bufA[:, 0:NFC * T]).then_inc(S["dOUT"], 16)
                    nd += 1
                if d_mac is not None:
                    e.wait_ge(S["p1gv"], 8 * NT)
                    e.dma_start(out=d_mac, in_=macol[:]).then_inc(S["dOUT"], 16)
                    nd += 1
                if d_qcT is not None and phases >= 2 and p2sub >= 2:
                    e.wait_ge(S["p2qe"], NFC * NTQ)
                    e.dma_start(out=d_qcT, in_=bufC[:]).then_inc(S["dOUT"], 16)
                    nd += 1
                if d_kT is not None and phases >= 2 and p2sub >= 3:
                    e.wait_ge(S["p2ke"], NFC * NTQ)
                    e.dma_start(out=d_kT, in_=bufD[:]).then_inc(S["dOUT"], 16)
                    nd += 1
                if d_vtok is not None and phases >= 2 and p2sub >= 4:
                    e.wait_ge(S["p2ve"], NT)
                    e.dma_start(out=d_vtok, in_=bufE[:]).then_inc(S["dOUT"], 16)
                    nd += 1
                dbg_cnt[0] += nd
                e.wait_ge(S["dOUT"], 16 * dbg_cnt[0])
                e.wait_ge(S["dK0"], 16 * ((NT + 1) // 2))
                e.wait_ge(S["dK1"], 16 * (NT // 2))
                e.nop().then_inc(S["pB1"], 1)

            @block1.gpsimd
            def _(e):
                e.dma_start(out=bvbf[:], in_=dbv).then_inc(S["dWc"], 16)
                e.dma_start(out=wc2[0:64, :], in_=dWc).then_inc(S["dWc"], 16)
                e.dma_start(out=wc2[64:128, :], in_=dWc).then_inc(S["dWc"], 16)
                e.dma_start(out=wstat[:].rearrange("p (c f) -> p c f", c=NFC),
                            in_=dWqT.rearrange("(c p) f -> p c f", p=128)
                            ).then_inc(S["dWq"], 16)
                if phases >= 2 and p2sub >= 3:
                    e.wait_ge(S["p2f"], NFC)
                    e.dma_start(out=wstat[:].rearrange("p (c f) -> p c f", c=NFC),
                                in_=dWk.rearrange("(c p) f -> p c f", p=128)
                                ).then_inc(S["dWk"], 16)
                if phases >= 2 and p2sub >= 4:
                    e.wait_ge(S["p2qe"], NFC * NTQ)
                    e.dma_start(out=wact[:].rearrange("p (c f) -> p c f", c=NFC),
                                in_=dWv.rearrange("(c p) f -> p c f", p=128)
                                ).then_inc(S["dWv"], 16)
                e.engine_nop().then_inc(S["pB1"], 1)

            @block1.vector
            def _(e):
                e.memset(onesrow[:], 1.0)
                e.memset(onescol[:], 1.0)
                e.memset(onesfrow[:], 1.0)
                e.memset(onesfcol[:], 1.0)
                e.memset(c1535[:], 1535.5)
                e.memset(cneg16[:], -16.0)
                e.wait_ge(S["dIN"], 16 * N_IN)
                e.tensor_copy(out=identf[:], in_=ident[:])
                e.drain()
                for i in range(NT):
                    sl = xslot(i)
                    e.wait_ge(S["dX0" if i % 2 == 0 else "dX1"], 16 * (i // 2 + 1))
                    e.reduce_sum(out=st8[:, 0:1], in_=sl, axis=AX.X)
                    e.wait_ge(S["p1a"], i + 1)
                    e.drain()
                    e.tensor_scalar(out=st8[:, 2:3], in0=st8[:, 0:1], scalar1=1.0 / D,
                                    scalar2=None, op0=ALU.mult)
                    e.tensor_scalar(out=st8[:, 3:4], in0=st8[:, 1:2], scalar1=1.0 / D,
                                    scalar2=None, op0=ALU.mult)
                    e.drain()
                    e.tensor_tensor(out=st8[:, 4:5], in0=st8[:, 2:3], in1=st8[:, 2:3],
                                    op=ALU.mult)
                    e.drain()
                    e.tensor_tensor(out=st8[:, 5:6], in0=st8[:, 3:4], in1=st8[:, 4:5],
                                    op=ALU.subtract)
                    e.drain()
                    e.tensor_scalar(out=st8[:, 6:7], in0=st8[:, 5:6], scalar1=EPS,
                                    scalar2=None, op0=ALU.add)
                    e.drain()
                    e.reciprocal(out=st8[:, 7:8], in_=st8[:, 6:7]).then_inc(S["p1b"], 1)
                    e.wait_ge(S["p1c"], i + 1)
                    e.reduce_max(out=st8[:, 9:10], in_=xntok[:], axis=AX.X,
                                 apply_absolute_value=True)
                    e.drain()
                    e.tensor_scalar(out=macol[:, i:i + 1], in0=st8[:, 9:10],
                                    scalar1=1e-5, scalar2=None, op0=ALU.max)
                    e.drain()
                    e.reciprocal(out=st8[:, 10:11], in_=macol[:, i:i + 1])
                    e.drain()
                    e.tensor_scalar(out=st8[:, 11:12], in0=st8[:, 10:11],
                                    scalar1=127.0, scalar2=None, op0=ALU.mult)
                    e.drain()
                    if i > 0:
                        e.wait_ge(S["p1f"], 16 * i)
                    e.tensor_scalar(out=i16b[:], in0=xntok[:], scalar1=st8[:, 11:12],
                                    scalar2=1535.5, op0=ALU.mult, op1=ALU.add)
                    e.drain()
                    e.scalar_tensor_tensor(out=rrb[:], in0=xntok[:], scalar=st8[:, 11:12],
                                           in1=i16b[:], op0=ALU.mult, op1=ALU.subtract)
                    e.drain()
                    e.tensor_scalar(out=rrb[:], in0=rrb[:], scalar1=-1535.5,
                                    scalar2=None, op0=ALU.is_ge)
                    e.drain()
                    e.scalar_tensor_tensor(out=k1tok[:], in0=i16b[:], scalar=-1536.0,
                                           in1=rrb[:], op0=ALU.add,
                                           op1=ALU.add).then_inc(S["p1d"], 1)
                    e.wait_ge(S["p1f"], 16 * i + 8)
                    for s in range(8):
                        e.tensor_copy(out=bufA[:, s * T + i * 128: s * T + (i + 1) * 128],
                                      in_=pbf[0][:, s * 128:(s + 1) * 128]
                                      ).then_inc(S["p1gv"], 1)
                if phases < 2:
                    return
                e.wait_ge(S["dIN"], 16 * N_IN)
                e.tensor_copy(out=bqbf[:], in_=bqpc[:])
                e.drain()
                e.engine_nop().then_inc(S["p2m"], 1)
                for fc in range(NFC if p2sub >= 0 else 0):
                    e.wait_ge(S["p2f"], fc + 1)
                    for b in range(2):
                        dst = wact[:, fc * D:(fc + 1) * D].rearrange(
                            "p (j x) -> p j x", x=128)[:, :, b * 64:(b + 1) * 64]
                        e.tensor_scalar(out=dst,
                                        in0=ps[2 + b][:].rearrange("p (j l) -> p j l", l=64),
                                        scalar1=SCALE, scalar2=None, op0=ALU.mult)
                    e.drain()
                    e.engine_nop().then_inc(S["p2fe"], 1)
                if p2sub >= 1:
                    e.wait_ge(S["p2bq"], 1)
                for h in range(NH if p2sub >= 1 else 0):
                    half = (h % 2) * 64
                    e.tensor_scalar(out=bqe[half:half + 64, h // 2:h // 2 + 1],
                                    in0=ps[h % 2][0:64, h // 2:h // 2 + 1], scalar1=SCALE,
                                    scalar2=None, op0=ALU.mult)
                e.drain()
                e.engine_nop().then_inc(S["p2bqe"], 1)
                for i in range(NT if p2sub >= 4 else 0):
                    e.wait_ge(S["p2v"], i + 1)
                    for half in range(2):
                        e.tensor_copy(out=bufE[:, i * D + half * 512:i * D + (half + 1) * 512],
                                      in_=ps[2 + half][:])
                    e.drain()
                    e.engine_nop().then_inc(S["p2ve"], 1)
                e.engine_nop().then_inc(S["pB1"], 1)

            @block1.scalar
            def _(e):
                for i in range(NT):
                    sl = xslot(i)
                    e.wait_ge(S["dX0" if i % 2 == 0 else "dX1"], 16 * (i // 2 + 1))
                    e.activation(out=sqs[:], in_=sl, func=AF.Square,
                                 accum_out=st8[:, 1:2]).then_inc(S["p1a"], 1)
                    e.wait_ge(S["p1b"], i + 1)
                    e.activation(out=st8[:, 8:9], in_=st8[:, 7:8], func=AF.Sqrt)
                    e.drain()
                    e.mul(out=st8[:, 4:5], in_=st8[:, 2:3], mul=st8[:, 8:9])
                    e.drain()
                    e.mul(out=st8[:, 5:6], in_=st8[:, 4:5], mul=-1.0)
                    e.drain()
                    e.activation(out=xntok[:], in_=sl, func=AF.Identity,
                                 scale=st8[:, 8:9], bias=st8[:, 5:6]).then_inc(S["p1c"], 1)
                    e.drain()
                    if i > 0:
                        e.wait_ge(S["p1f"], 16 * i)
                    e.copy(out=xnbf[:], in_=xntok[:]).then_inc(S["p1e"], 1)
                    e.wait_ge(S["p1f"], 16 * i + 16)
                    if i >= 2:
                        e.wait_ge(S["dK0" if i % 2 == 0 else "dK1"],
                                  16 * (i // 2))
                    for s in range(8):
                        e.copy(out=k1stg[:, (i % 2) * D + s * 128:(i % 2) * D + (s + 1) * 128],
                               in_=pbf[1][:, s * 128:(s + 1) * 128]).then_inc(S["p1gs"], 1)
                if phases < 2:
                    return
                if p2sub >= 1:
                    e.wait_ge(S["p2bqe"], 1)
                for u, (oc, tq) in enumerate([(a, b) for a in range(NFC) for b in range(NTQ)] if p2sub >= 2 else []):
                    bank = u % 2
                    e.wait_ge(S["p2q"], u + 1)
                    e.activation(out=bufC[:, oc * T + tq * 512:oc * T + (tq + 1) * 512],
                                 in_=ps[bank][:], func=AF.Identity, scale=1.0,
                                 bias=bqe[:, oc:oc + 1]).then_inc(S["p2qe"], 1)
                for u, (oc, tq) in enumerate([(a, b) for a in range(NFC) for b in range(NTQ)] if p2sub >= 3 else []):
                    bank = u % 2
                    e.wait_ge(S["p2k"], u + 1)
                    e.activation(out=bufD[:, oc * T + tq * 512:oc * T + (tq + 1) * 512],
                                 in_=ps[bank][:], func=AF.Identity, scale=1.0,
                                 bias=bkpc[:, oc:oc + 1]).then_inc(S["p2ke"], 1)

            @block1.tensor
            def _(e):
                e.wait_ge(S["dIN"], 16 * N_IN)
                for i in range(NT):
                    e.wait_ge(S["p1e"], i + 1)
                    e.wait_ge(S["p1d"], i + 1)
                    if i > 0:
                        e.wait_ge(S["p1gv"], 8 * i)
                        e.wait_ge(S["p1gs"], 8 * i)
                    for s in range(16):
                        src = xnbf if s < 8 else k1tok
                        fc = s % 8
                        e.transpose(out=pbf[s // 8][:, fc * 128:(fc + 1) * 128],
                                    in_=src[:, fc * 128:(fc + 1) * 128],
                                    identity=ident[:]).then_inc(S["p1f"], 1)
                if phases < 2:
                    return
                e.wait_ge(S["dWq"], 16)
                e.wait_ge(S["dWc"], 48)
                for fc in range(NFC if p2sub >= 0 else 0):
                    if fc >= 1:
                        e.wait_ge(S["p2fe"], fc)
                    for h in range(NH):
                        half = (h % 2) * 64
                        lhsT = wstat[half:half + 64,
                                     (h // 2) * D + fc * 128:(h // 2) * D + (fc + 1) * 128]
                        out = ps[2 + h % 2][:, (h // 2) * 64:(h // 2) * 64 + 64]
                        ins = e.matmul(out=out, lhsT=lhsT, rhs=wc2[half:half + 64, :],
                                       start=True, stop=True)
                        if h == NH - 1:
                            ins.then_inc(S["p2f"], 1)
                e.wait_ge(S["dIN"], 16 * N_IN)
                e.wait_ge(S["p2m"], 1)
                for h in range(NH if p2sub >= 1 else 0):
                    half = (h % 2) * 64
                    ins = e.matmul(out=ps[h % 2][0:64, h // 2:h // 2 + 1],
                                   lhsT=wc2[half:half + 64, :],
                                   rhs=bqbf[half:half + 64, h // 2:h // 2 + 1],
                                   start=True, stop=True)
                    if h == NH - 1:
                        ins.then_inc(S["p2bq"], 1)
                e.wait_ge(S["p1gv"], 8 * NT)
                if p2sub >= 1:
                    e.wait_ge(S["p2bqe"], 1)
                for u, (oc, tq) in enumerate([(a, b) for a in range(NFC) for b in range(NTQ)] if p2sub >= 2 else []):
                    bank = u % 2
                    if u >= 2:
                        e.wait_ge(S["p2qe"], u - 1)
                    for fc in range(NFC):
                        ins = e.matmul(out=ps[bank][:],
                                       lhsT=wact[:, fc * D + oc * 128:fc * D + (oc + 1) * 128],
                                       rhs=bufA[:, fc * T + tq * 512:fc * T + (tq + 1) * 512],
                                       start=(fc == 0), stop=(fc == NFC - 1))
                        if fc == NFC - 1:
                            ins.then_inc(S["p2q"], 1)
                if p2sub >= 3:
                    e.wait_ge(S["dWk"], 16)
                    e.wait_ge(S["p2qe"], NFC * NTQ)
                for u, (oc, tq) in enumerate([(a, b) for a in range(NFC) for b in range(NTQ)] if p2sub >= 3 else []):
                    bank = u % 2
                    if u >= 2:
                        e.wait_ge(S["p2ke"], u - 1)
                    for fc in range(NFC):
                        ins = e.matmul(out=ps[bank][:],
                                       lhsT=wstat[:, fc * D + oc * 128:fc * D + (oc + 1) * 128],
                                       rhs=bufA[:, fc * T + tq * 512:fc * T + (tq + 1) * 512],
                                       start=(fc == 0), stop=(fc == NFC - 1))
                        if fc == NFC - 1:
                            ins.then_inc(S["p2k"], 1)
                if p2sub >= 4:
                    e.wait_ge(S["dWv"], 16)
                    e.wait_ge(S["dWc"], 48)
                    e.wait_ge(S["p2m"], 1)
                    e.wait_ge(S["p2fe"], NFC)
                for i in range(NT if p2sub >= 4 else 0):
                    if i >= 1:
                        e.wait_ge(S["p2ve"], i)
                    for fc in range(NFC):
                        for half in range(2):
                            e.matmul(out=ps[2 + half][:],
                                     lhsT=bufA[:, fc * T + i * 128:fc * T + (i + 1) * 128],
                                     rhs=wact[:, fc * D + half * 512:fc * D + (half + 1) * 512],
                                     start=(fc == 0), stop=False)
                    for half in range(2):
                        ins = e.matmul(out=ps[2 + half][:], lhsT=onesrow[:],
                                       rhs=bvbf[:, half * 512:(half + 1) * 512],
                                       start=False, stop=True)
                        if half == 1:
                            ins.then_inc(S["p2v"], 1)

        if phases < 3:
            b12.close()
            return nc, dbg_out
        b2 = ExitStack()
        with b2:
            E2 = b2.enter_context
            pt = E2(nc.sbuf_tensor("pt", [128, 4 * T], BF16))
            negm = E2(nc.sbuf_tensor("negm", [128, 4], F32))
            p3row = E2(nc.sbuf_tensor("p3row", [1, 8], F32))
            linv = E2(nc.sbuf_tensor("linv", [1, 256], F32))
            lbs = E2(nc.sbuf_tensor("lbs", [128, 128], F32))
            wqst = E2(nc.sbuf_tensor("wqst", [128, 2 * 2048], F32))
            wqout = E2(nc.sbuf_tensor("wqout", [128, 2048], BF16))
            wqcnb = E2(nc.sbuf_tensor("wqcnb", [128, 2048], BF16))
            wqtmp = E2(nc.sbuf_tensor("wqtmp", [128, 1], F32))

            def p3_sched():
                sched = []
                qk_gidx = 0
                exp_gidx = 0
                last_exp = {}
                for hp in range(8):
                    for qb in range(NQB):
                        u = hp * NQB + qb
                        nkt = qb + 1
                        nch = (nkt + 3) // 4
                        groups = []
                        for c in range(nch):
                            kts = list(range(4 * c, min(4 * c + 4, nkt)))
                            for hs in range(2):
                                bank = 2 * hs + (c % 2)
                                groups.append(dict(hs=hs, c=c, kts=kts, bank=bank,
                                                   qk=qk_gidx, exp=exp_gidx,
                                                   free_at=last_exp.get(bank)))
                                last_exp[bank] = exp_gidx + 1
                                qk_gidx += 1
                                exp_gidx += 1
                        sched.append(dict(u=u, hp=hp, qb=qb, nkt=nkt, nch=nch,
                                          groups=groups, exp_end=exp_gidx))
                return sched

            SCHED = p3_sched()
            WQCH = [(0, ch, half,
                     dW1T[ch * 128:(ch + 1) * 128, half * 2048:(half + 1) * 2048])
                    for ch in range(8) for half in range(2)] + \
                   [(1, pr, 0,
                     dW2T[pr * 256:(pr + 1) * 256, :].rearrange(
                        "(c p) e -> p c e", p=128)) for pr in range(16)]
            WQ_CPU = 1 if NUNITS >= 40 else 2
            WQ_UOF = lambda i: 1 + i // WQ_CPU
            WQ_U1 = WQ_UOF(31) + 2
            WQ_U2 = WQ_U1 + 2
            WQ_TCPU = 1 if NUNITS >= 110 else (2 if NUNITS >= 60 else 4)
            WQ_TOF = lambda i: WQ_U2 + 2 + i // WQ_TCPU
            WQPLAN = []
            _oc = 0
            for _i, (_m, _ci, _hf, _src) in enumerate(WQCH):
                _nd = 1 if _m == 0 else 2
                WQPLAN.append((_m, _ci, _hf, _src, _oc, _nd))
                _oc += _nd
            WQ_OCNT = _oc
            assert WQ_TOF(len(WQCH) - 1) < NUNITS, "wq ternarize schedule overflow"

            def ptsl(u, hs, lo, hi):
                base = (u % 2) * 2 * T + hs * T
                return pt[:, base + lo:base + hi]

            block2 = E2(nc.Block())

            @block2.tensor
            def _(e):
                e.wait_ge(S["pB1"], 3)
                e.wait_ge(S["p2ke"], NFC * NTQ)
                e.wait_ge(S["p2v"], NT)
                first_bank_wait = {0: ("p2ke", NFC * NTQ), 1: ("p2ke", NFC * NTQ),
                                   2: ("p2ve", NT), 3: ("p2ve", NT)}
                for un in SCHED:
                    u, hp, qb, nkt = un["u"], un["hp"], un["qb"], un["nkt"]
                    par = u % 2
                    for g in un["groups"]:
                        hs, c, kts, bank = g["hs"], g["c"], g["kts"], g["bank"]
                        if g["free_at"] is None:
                            sem, val = first_bank_wait.pop(bank, (None, 0))
                            if sem:
                                e.wait_ge(S[sem], val)
                        else:
                            e.wait_ge(S["aEXP"], g["free_at"])
                        hb = hs * 64
                        for j, kt in enumerate(kts):
                            ins = e.matmul(
                                out=ps[bank][:, j * 128:(j + 1) * 128],
                                lhsT=bufD[hb:hb + 64, hp * T + kt * 128:hp * T + (kt + 1) * 128],
                                rhs=bufC[hb:hb + 64, hp * T + qb * 128:hp * T + (qb + 1) * 128],
                                start=True, stop=True)
                            if j == len(kts) - 1:
                                ins.then_inc(S["aQK"], 1)
                    if u >= 1:
                        e.wait_ge(S["aOE"], u - 1)
                        e.wait_ge(S["aLR"], u - 1)
                    if u == WQ_U1:
                        e.wait_ge(S["wqA"], 1)
                        e.matmul(out=_ban[par][0:1, 388:389], lhsT=wqacc[:, 0:1],
                                 rhs=onesfcol[:], start=True, stop=True)
                        e.matmul(out=_ban[par][0:1, 389:390], lhsT=wqacc[:, 1:2],
                                 rhs=onesfcol[:], start=True, stop=True).then_inc(S["wqC"], 1)
                    if u == WQ_U2:
                        e.wait_ge(S["wqB"], 1)
                        e.matmul(out=_ban[par][:, 388:390], lhsT=onesfrow[:],
                                 rhs=mwsum[:], start=True, stop=True).then_inc(S["wqD"], 1)
                    if u == WQ_U1:
                        e.wait_ge(S["wqB"], 1)
                    if u == WQ_U2:
                        e.wait_ge(S["wqE"], 1)
                    if u == 0:
                        e.wait_ge(S["p2v"], NT)
                    lpair = lsl[par]
                    for hs in range(2):
                        for kt in range(nkt):
                            c = kt // 4
                            e.wait_ge(S["aEXP"], un["groups"][2 * c + hs]["exp"] + 1)
                            e.matmul(out=lpair[hs], lhsT=onescol[:],
                                     rhs=ptsl(u, hs, kt * 128, (kt + 1) * 128),
                                     start=(kt == 0), stop=(kt == nkt - 1),
                                     tile_position=(0, 32 * hs))
                    for hs in range(2):
                        for kt in range(nkt):
                            ins = e.matmul(
                                out=pv2[par][hs * 64:(hs + 1) * 64, :],
                                lhsT=bufE[:, kt * D + (2 * hp + hs) * 64:kt * D + (2 * hp + hs + 1) * 64],
                                rhs=ptsl(u, hs, kt * 128, (kt + 1) * 128),
                                start=(kt == 0), stop=(kt == nkt - 1),
                                tile_position=(0, hs * 64))
                            if kt == nkt - 1 and hs == 1:
                                ins.then_inc(S["aPV"], 1)
                    e.wait_ge(S["aLR"], u + 1)
                    e.matmul(out=lbp[par][0:64, :], lhsT=onesfrow[0:1, 0:64],
                             rhs=linv[0:1, 0:128], start=True, stop=True)
                    e.matmul(out=lbp[par][64:128, :], lhsT=onesfrow[0:1, 0:64],
                             rhs=linv[0:1, 128:256], start=True, stop=True,
                             tile_position=(0, 64)).then_inc(S["aLB"], 1)
                e.wait_ge(S["dWo"], 16)
                e.wait_ge(S["aOE"], NUNITS)
                for un2, (oc, tq) in enumerate([(a, b) for a in range(NFC) for b in range(NTQ)]):
                    bank = un2 % 2
                    if un2 >= 2:
                        e.wait_ge(S["p4qe"], un2 - 1)
                    for fc in range(NFC):
                        ins = e.matmul(out=ps[bank][:],
                                       lhsT=wstat[:, fc * D + oc * 128:fc * D + (oc + 1) * 128],
                                       rhs=bufA[:, fc * T + tq * 512:fc * T + (tq + 1) * 512],
                                       start=(fc == 0), stop=(fc == NFC - 1))
                        if fc == NFC - 1:
                            ins.then_inc(S["p4q"], 1)

            @block2.vector
            def _(e):
                e.wait_ge(S["pB1"], 3)
                e.wait_ge(S["p2ke"], NFC * NTQ)
                e.wait_ge(S["p2v"], NT)
                for un in SCHED:
                    u, hp, qb, nkt = un["u"], un["hp"], un["qb"], un["nkt"]
                    par = u % 2
                    dkt = qb
                    c = dkt // 4
                    j = dkt - 4 * c
                    for hs in range(2):
                        g = un["groups"][2 * c + hs]
                        e.wait_ge(S["aQK"], g["qk"] + 1)
                        dia = ps[g["bank"]][:, j * 128:(j + 1) * 128]
                        e.tensor_tensor(out=dia, in0=dia, in1=masksub[:],
                                        op=ALU.add)
                    e.drain()
                    e.engine_nop().then_inc(S["aRM"], 2)
                    for i in range(len(WQCH)):
                        if WQ_UOF(i) != u:
                            continue
                        m = WQCH[i][0]
                        e.wait_ge(S["wqT1" if i % 2 == 0 else "wqT2"], 16 * (i // 2 + 1))
                        sl2 = wqst[:, (i % 2) * 2048:(i % 2 + 1) * 2048]
                        if i == 0 or (m == 1 and WQCH[i - 1][0] == 0):
                            e.reduce_sum(out=wqacc[:, m:m + 1], in_=sl2, axis=AX.X,
                                         apply_absolute_value=True)
                            e.drain()
                        else:
                            e.reduce_sum(out=wqtmp[:], in_=sl2, axis=AX.X,
                                         apply_absolute_value=True)
                            e.drain()
                            e.tensor_tensor(out=wqacc[:, m:m + 1], in0=wqacc[:, m:m + 1],
                                            in1=wqtmp[:], op=ALU.add)
                            e.drain()
                        e.engine_nop().then_inc(S["wqV"], 1)
                        if i == len(WQCH) - 1:
                            e.engine_nop().then_inc(S["wqA"], 1)
                    if u == WQ_U1:
                        e.wait_ge(S["wqC"], 1)
                        e.tensor_copy(out=mwsum[:], in_=_ban[par][0:1, 388:390])
                        e.drain()
                        e.engine_nop().then_inc(S["wqB"], 1)
                    if u == WQ_U2:
                        e.wait_ge(S["wqD"], 1)
                        e.tensor_copy(out=sumrep[:], in_=_ban[par][:, 388:390])
                        e.drain()
                        e.engine_nop().then_inc(S["wqE"], 1)
                    for i3 in range(len(WQPLAN)):
                        if WQ_TOF(i3) != u:
                            continue
                        m3, _c3, _h3, _s3, oc3, nd3 = WQPLAN[i3]
                        slot3 = i3 % 2
                        if i3 == 0:
                            e.wait_ge(S["wqF"], 1)
                        e.wait_ge(S["wqT1" if slot3 == 0 else "wqT2"],
                                  16 * (16 + i3 // 2 + 1))
                        e.wait_ge(S["wqG1"], 16 * oc3)
                        sl3 = wqst[:, slot3 * 2048:(slot3 + 1) * 2048]
                        e.tensor_scalar(out=wqcnb[:], in0=sl3,
                                        scalar1=nthrrep[:, m3:m3 + 1],
                                        scalar2=None, op0=ALU.is_le)
                        e.drain()
                        e.tensor_scalar(out=wqout[:], in0=sl3,
                                        scalar1=thrrep[:, m3:m3 + 1],
                                        scalar2=None, op0=ALU.is_ge)
                        e.drain()
                        e.tensor_tensor(out=wqout[:], in0=wqout[:], in1=wqcnb[:],
                                        op=ALU.subtract)
                        e.drain()
                        e.engine_nop().then_inc(S["wqW"], 1)
                    e.wait_ge(S["aPV"], u + 1)
                    e.reciprocal(out=linv[0:1, 0:128], in_=lsl[par][0])
                    e.reciprocal(out=linv[0:1, 128:256], in_=lsl[par][1]
                                 ).then_inc(S["aLR"], 1)
                    e.wait_ge(S["aLB"], u + 1)
                    e.tensor_copy(out=lbs[:], in_=lbp[par])
                    e.drain()
                    if u == 0:
                        e.wait_ge(S["p2v"], NT)
                    e.tensor_tensor(out=bufA[:, hp * T + qb * 128:hp * T + (qb + 1) * 128],
                                    in0=pv2[par], in1=lbs[:],
                                    op=ALU.mult).then_inc(S["aOE"], 1)
                e.engine_nop().then_inc(S["pB2"], 1)

            @block2.scalar
            def _(e):
                e.wait_ge(S["pB1"], 3)
                e.wait_ge(S["p2ke"], NFC * NTQ)
                e.wait_ge(S["p2v"], NT)
                for un in SCHED:
                    u, hp, qb, nkt = un["u"], un["hp"], un["qb"], un["nkt"]
                    dkt = qb
                    for g in un["groups"]:
                        hs, c, kts, bank = g["hs"], g["c"], g["kts"], g["bank"]
                        e.wait_ge(S["aQK"], g["qk"] + 1)
                        if dkt in kts:
                            e.wait_ge(S["aRM"], 2 * u + 2)
                        if u >= 2:
                            e.wait_ge(S["aPV"], u - 1)
                        wid = len(kts) * 128
                        e.activation(out=ptsl(u, hs, 4 * c * 128, 4 * c * 128 + wid),
                                     in_=ps[bank][:, 0:wid], func=AF.Exp,
                                     bias=cneg16[:, 0:1],
                                     scale=1.0).then_inc(S["aEXP"], 1)
                for un2, (oc, tq) in enumerate([(a, b) for a in range(NFC) for b in range(NTQ)]):
                    bank = un2 % 2
                    e.wait_ge(S["p4q"], un2 + 1)
                    e.activation(out=bufC[:, oc * T + tq * 512:oc * T + (tq + 1) * 512],
                                 in_=ps[bank][:], func=AF.Identity, scale=1.0,
                                 bias=bopc[:, oc:oc + 1]).then_inc(S["p4qe"], 1)

            @block2.gpsimd
            def _(e):
                e.wait_ge(S["pB1"], 3)
                e.wait_ge(S["p2ke"], NFC * NTQ)
                e.wait_ge(S["p2v"], NT)
                e.dma_start(out=wstat[:].rearrange("p (c f) -> p c f", c=NFC),
                            in_=dWo.rearrange("(c p) f -> p c f", p=128)
                            ).then_inc(S["dWo"], 16)
                for i, (m, _c, _h, srcap) in enumerate(WQCH):
                    slot = i % 2
                    sem = S["wqT1" if slot == 0 else "wqT2"]
                    if i >= 2:
                        e.wait_ge(S["wqV"], i - 1)
                    tgt = wqst[:, slot * 2048:(slot + 1) * 2048]
                    if m == 1:
                        tgt = tgt.rearrange("p (c e) -> p c e", c=2)
                    e.dma_start(out=tgt, in_=srcap).then_inc(sem, 16)
                e.wait_ge(S["wqE"], 1)
                e.tensor_scalar(out=thrrep[:], in0=sumrep[:], scalar1=1.0 / W1N,
                                scalar2=None, op0=ALU.mult)
                e.drain()
                e.tensor_scalar(out=thrrep[:], in0=thrrep[:], scalar1=1e-5,
                                scalar2=None, op0=ALU.max)
                e.drain()
                e.tensor_scalar(out=mw127[:], in0=thrrep[:], scalar1=1.0 / 127.0,
                                scalar2=None, op0=ALU.mult)
                e.drain()
                e.tensor_scalar(out=thrrep[:], in0=thrrep[:], scalar1=0.5,
                                scalar2=None, op0=ALU.mult)
                e.drain()
                e.tensor_scalar(out=nthrrep[:], in0=thrrep[:], scalar1=-1.0,
                                scalar2=None, op0=ALU.mult)
                e.drain()
                e.engine_nop().then_inc(S["wqF"], 1)
                for i, (m, cidx, half, srcap, oc0, nd) in enumerate(WQPLAN):
                    slot = i % 2
                    sem = S["wqT1" if slot == 0 else "wqT2"]
                    tgt = wqst[:, slot * 2048:(slot + 1) * 2048]
                    if m == 1:
                        tgt = tgt.rearrange("p (c e) -> p c e", c=2)
                    if i >= 2:
                        e.wait_ge(S["wqW"], i - 1)
                    e.dma_start(out=tgt, in_=srcap).then_inc(sem, 16)
                    e.wait_ge(S["wqW"], i + 1)
                    if m == 0:
                        e.dma_start(
                            out=dC1[:, half * 4:(half + 1) * 4, cidx, :],
                            in_=wqout[:].rearrange("p (c t) -> p c t", c=4)
                            ).then_inc(S["wqG1"], 16)
                    else:
                        g2, j2 = (2 * cidx) // 4, (2 * cidx) % 4
                        obv = wqout[:].rearrange("p (c e) -> p c e", c=2)
                        e.dma_start(out=dC2[:, 0, g2, j2:j2 + 2, :],
                                    in_=obv[:, :, 0:512]).then_inc(S["wqG1"], 16)
                        e.dma_start(out=dC2[:, 1, g2, j2:j2 + 2, :],
                                    in_=obv[:, :, 512:1024]).then_inc(S["wqG1"], 16)
                e.wait_ge(S["wqG1"], 16 * WQ_OCNT)
                e.engine_nop().then_inc(S["pB2"], 1)

            @block2.sync
            def _(e):
                nd = 0
                if d_wq is not None:
                    e.wait_ge(S["wqF"], 1)
                    e.dma_start(out=d_wq[:, 0:2], in_=thrrep[:]).then_inc(S["dOUT"], 16)
                    e.dma_start(out=d_wq[:, 2:4], in_=mw127[:]).then_inc(S["dOUT"], 16)
                    nd += 2
                if d_oT is not None:
                    e.wait_ge(S["aOE"], NUNITS)
                    e.dma_start(out=d_oT, in_=bufA[:, 0:NFC * T]).then_inc(S["dOUT"], 16)
                    nd += 1
                if d_attnT is not None:
                    e.wait_ge(S["p4qe"], NFC * NTQ)
                    e.dma_start(out=d_attnT, in_=bufC[:]).then_inc(S["dOUT"], 16)
                    nd += 1
                dbg_cnt[0] += nd
                e.wait_ge(S["dOUT"], 16 * dbg_cnt[0])
                e.nop().then_inc(S["pB2"], 1)

        b12.close()

        if phases < 5:
            return nc, dbg_out
        b3 = ExitStack()
        with b3:
            E3 = b3.enter_context
            gbuf = E3(nc.sbuf_tensor("gbuf", [128, NHC * TF], F32))
            s1row = E3(nc.sbuf_tensor("s1row", [1, T], F32))
            k1st = E3(nc.sbuf_tensor("k1st", [128, 2 * NFC * (TF + 2)], BF16))
            c1b = E3(nc.sbuf_tensor("c1b", [128, 2 * NFC * 512], BF16))
            c2b = E3(nc.sbuf_tensor("c2b", [128, 2 * 4 * 512], BF16))
            hch = E3(nc.sbuf_tensor("hch", [128, 2 * (TF + 4)], F32))
            cva = E3(nc.sbuf_tensor("cva", [128, 2 * TF], F32))
            cvb = E3(nc.sbuf_tensor("cvb", [128, 2 * TF], F32))
            cvc = E3(nc.sbuf_tensor("cvc", [128, 2 * TF], F32))
            cvs = E3(nc.sbuf_tensor("cvs", [128, 2 * TF], F32))
            i16c = E3(nc.sbuf_tensor("i16c", [128, 2 * TF], I16))
            gsb = E3(nc.sbuf_tensor("gsb", [128, 2 * TF], F32))
            rr2b = E3(nc.sbuf_tensor("rr2b", [128, 2 * TF], F32))
            Mb = E3(nc.sbuf_tensor("Mb", [128, TF], F32))
            Mn = E3(nc.sbuf_tensor("Mn", [128, TF], F32))
            maCt = E3(nc.sbuf_tensor("maCt", [128, NQT], F32))
            maCrow = E3(nc.sbuf_tensor("maCrow", [1, TF], F32))
            s2rw = E3(nc.sbuf_tensor("s2rw", [1, TF], F32))
            s2erw = E3(nc.sbuf_tensor("s2erw", [1, TF], F32))
            s1all = E3(nc.sbuf_tensor("s1all", [128, NTF * 264], F32))
            s2bc = E3(nc.sbuf_tensor("s2bc", [128, TF], F32))
            s2ebc = E3(nc.sbuf_tensor("s2ebc", [128, TF], F32))
            t1b = E3(nc.sbuf_tensor("t1b", [128, 2 * TF], F32))
            t2b = E3(nc.sbuf_tensor("t2b", [128, 2 * TF], F32))
            xst2 = E3(nc.sbuf_tensor("xst2", [128, 8 * TF], F32))
            outst = E3(nc.sbuf_tensor("outst", [128, 2 * TF], F32))

            def t_range(tau):
                lo = max(tau * TF - 1, 0)
                hi = min(tau * TF + TF + 1, T)
                return lo, hi, lo - (tau * TF - 1)

            hbank = [psS, psT]
            tbank = [ps[0], ps[1]]

            block3 = E3(nc.Block())

            @block3.sync
            def _(e):
                e.wait_ge(S["pB2"], 3)
                e.wait_ge(S["p4qe"], NFC * NTQ)
                def k1_load(kt):
                    klo, khi, _ = t_range(kt)
                    kw = khi - klo
                    ksl = S["dk1a" if kt % 2 == 0 else "dk1b"]
                    if kt >= 2:
                        e.wait_ge(S["f1"], (kt - 1) * NHC)
                    e.dma_start(
                        out=k1st[:, (kt % 2) * NFC * (TF + 2):(kt % 2 + 1) * NFC * (TF + 2)].rearrange(
                            "p (c t) -> p c t", c=NFC)[:, :, 0:kw],
                        in_=dK1[:, :, klo:khi].rearrange("c p t -> p c t")).then_inc(ksl, 16)

                def c1_load(ct, hb):
                    gslot = hb % 2
                    csem = S["dc1a" if gslot == 0 else "dc1b"]
                    ghb = ct * 8 + hb
                    if ghb >= 2:
                        e.wait_ge(S["f1"], (ghb - 1) * 4)
                    e.dma_start(out=c1b[:, gslot * NFC * 512:(gslot + 1) * NFC * 512],
                                in_=dC1[:, hb].rearrange("p c t -> p (c t)")).then_inc(csem, 16)

                def c2_load(ct, half, gq):
                    gslot = gq % 2
                    csem = S["dc2a" if gslot == 0 else "dc2b"]
                    gidx = (ct * 2 + half) * 8 + gq
                    if gidx >= 2:
                        e.wait_ge(S["f2c"], gidx - 1)
                    e.dma_start(out=c2b[:, gslot * 4 * 512:(gslot + 1) * 4 * 512],
                                in_=dC2[:, half, gq].rearrange("p c t -> p (c t)")).then_inc(csem, 16)

                k1_load(0)
                for hb in range(8):
                    c1_load(0, hb)
                for tau in range(NTF):
                    if tau + 1 < NTF:
                        k1_load(tau + 1)
                    for g in range(8):
                        if tau + 1 < NTF:
                            c1_load(tau + 1, g)
                        c2_load(tau, 0, g)
                    for gq in range(8):
                        c2_load(tau, 1, gq)
                nd = 0
                if d_k2 is not None:
                    e.wait_ge(S["rnD"], NTF * NHC)
                    e.dma_start(out=d_k2, in_=bufA[:, 0:NHC * TF]).then_inc(S["dOUT"], 16)
                    nd += 1
                if d_g is not None:
                    e.wait_ge(S["sil"], NTF * NHC)
                    e.dma_start(out=d_g, in_=gbuf[:]).then_inc(S["dOUT"], 16)
                    nd += 1
                dbg_cnt[0] += nd
                e.wait_ge(S["dOUT"], 16 * dbg_cnt[0])
                e.wait_ge(S["dosa"], 16 * NTF * 4)
                e.wait_ge(S["dosb"], 16 * NTF * 4)

            @block3.tensor
            def _(e):
                e.wait_ge(S["pB2"], 3)
                e.wait_ge(S["p4qe"], NFC * NTQ)
                e.wait_ge(S["p4qe"], NFC * NTQ)
                e.wait_ge(S["p1d"], NT)
                for b4 in range(NT // 4):
                    if b4 >= 1:
                        e.wait_ge(S["s1q"], b4)
                    for j4 in range(4):
                        i = b4 * 4 + j4
                        ins = e.transpose(out=ps[0][0:1, j4 * 128:(j4 + 1) * 128],
                                          in_=macol[:, i:i + 1], identity=identf[:])
                    ins.then_inc(S["s1p"], 1)
                e.wait_ge(S["s1r"], 1)
                e.wait_ge(S["s1q"], NT // 4)
                for tau in range(NTF):
                    lo, hi, colo = t_range(tau)
                    mainw = min(512, hi - lo)
                    if tau >= 2:
                        e.wait_ge(S["s1e"], tau - 1)
                    e.matmul(out=ps[2 + tau % 2][:, 0:mainw], lhsT=onesfrow[:],
                             rhs=s1row[0:1, lo:lo + mainw], start=True,
                             stop=True).then_inc(S["s1c"], 1)

                def ffn2_chunk(ct, half, gq):
                    gidx = (ct * 2 + half) * 8 + gq
                    gslot = gq % 2
                    if gq == 0:
                        if half == 0:
                            e.wait_ge(S["q7"], ct + 1)
                        if ct * 2 + half >= 1:
                            e.wait_ge(S["f2e"], (ct * 2 + half) * 4)
                    e.wait_ge(S["dc2a" if gslot == 0 else "dc2b"],
                              16 * (gidx // 2 + 1))
                    for jj in range(4):
                        hc = gq * 4 + jj
                        e.wait_ge(S["rnD"], ct * NHC + hc + 1)
                        for eo in range(4):
                            ins = e.matmul(
                                out=ps[eo][:, 0:TF],
                                lhsT=c2b[:, gslot * 4 * 512 + jj * 512 + eo * 128:
                                         gslot * 4 * 512 + jj * 512 + (eo + 1) * 128],
                                rhs=bufA[:, (ct % 2) * NHC * TF + hc * TF:
                                         (ct % 2) * NHC * TF + (hc + 1) * TF],
                                start=(hc == 0), stop=(hc == NHC - 1))
                        if jj == 3:
                            ins.then_inc(S["f2c"], 1)

                for tau in range(NTF):
                    lo, hi, colo = t_range(tau)
                    w = hi - lo
                    mainw = min(512, w)
                    ksl = S["dk1a" if tau % 2 == 0 else "dk1b"]
                    e.wait_ge(ksl, 16 * (tau // 2 + 1))
                    for g in range(8):
                        for j4 in range(4):
                            hc = 4 * g + j4
                            ghc = tau * NHC + hc
                            hb, jj = hc // 4, hc % 4
                            gslot = hb % 2
                            e.wait_ge(S["dc1a" if gslot == 0 else "dc1b"],
                                      16 * ((tau * 8 + hb) // 2 + 1))
                            if ghc >= 2:
                                e.wait_ge(S["f1e"], ghc - 1)
                            bk = hbank[hc % 2]
                            for dc in range(NFC):
                                lt = c1b[:, gslot * NFC * 512 + dc * 512 + jj * 128:
                                         gslot * NFC * 512 + dc * 512 + (jj + 1) * 128]
                                rh = k1st[:, (tau % 2) * NFC * (TF + 2) + dc * (TF + 2):
                                          (tau % 2) * NFC * (TF + 2) + dc * (TF + 2) + w]
                                ins = e.matmul(out=bk[:, 0:mainw], lhsT=lt,
                                               rhs=rh[:, 0:mainw],
                                               start=(dc == 0), stop=(dc == NFC - 1))
                                if dc == NFC - 1:
                                    ins.then_inc(S["f1"], 1)
                        if tau >= 1:
                            ffn2_chunk(tau - 1, 0, g)
                    if tau >= 1:
                        for gq in range(8):
                            ffn2_chunk(tau - 1, 1, gq)
                    e.wait_ge(S["q1"], tau + 1)
                    if tau == 0:
                        e.wait_ge(S["s1e"], NTF)
                    else:
                        e.wait_ge(S["f2e"], tau * 8)
                    for k4 in range(NQT):
                        ins = e.transpose(out=ps[2][:, k4 * 128:(k4 + 1) * 128],
                                          in_=Mb[:, k4 * 128:(k4 + 1) * 128],
                                          identity=identf[:])
                    ins.then_inc(S["q2"], 1)
                    e.wait_ge(S["q3"], tau + 1)
                    for k4 in range(NQT):
                        ins = e.transpose(out=ps[3][0:1, k4 * 128:(k4 + 1) * 128],
                                          in_=maCt[:, k4:k4 + 1], identity=identf[:])
                    ins.then_inc(S["q4"], 1)
                    e.wait_ge(S["q5"], tau + 1)
                    e.matmul(out=ps[2][:, 0:TF], lhsT=onesfrow[:], rhs=s2rw[:],
                             start=True, stop=True)
                    e.matmul(out=ps[3][:, 0:TF], lhsT=onesfrow[:], rhs=s2erw[:],
                             start=True, stop=True).then_inc(S["q6"], 1)
                for g in range(8):
                    ffn2_chunk(NTF - 1, 0, g)
                for gq in range(8):
                    ffn2_chunk(NTF - 1, 1, gq)

            @block3.vector
            def _(e):
                e.wait_ge(S["pB2"], 3)
                e.wait_ge(S["p4qe"], NFC * NTQ)
                for b4 in range(NT // 4):
                    e.wait_ge(S["s1p"], b4 + 1)
                    e.tensor_copy(out=s1row[0:1, b4 * 512:(b4 + 1) * 512],
                                  in_=ps[0][0:1, 0:512])
                    e.drain()
                    e.engine_nop().then_inc(S["s1q"], 1)
                e.wait_ge(S["wqF"], 1)
                e.tensor_scalar(out=s1row[:], in0=s1row[:], scalar1=mw127[0:1, 0:1],
                                scalar2=None, op0=ALU.mult)
                e.drain()
                e.engine_nop().then_inc(S["s1r"], 1)
                for tau in range(NTF):
                    lo, hi, colo = t_range(tau)
                    mainw = min(512, hi - lo)
                    e.wait_ge(S["s1c"], tau + 1)
                    e.tensor_copy(out=s1all[:, tau * 264:tau * 264 + mainw],
                                  in_=ps[2 + tau % 2][:, 0:mainw])
                    e.drain()
                    e.engine_nop().then_inc(S["s1e"], 1)

                def epi(ct):
                    for half in range(2):
                        e.wait_ge(S["f2c"], 8 * (ct * 2 + half + 1))
                        for v in range(4):
                            ec = half * 4 + v
                            gx = ct * 8 + ec
                            if gx >= 2:
                                e.wait_ge(S["f2f"], gx - 1)
                            e.tensor_tensor(out=t1b[:, (ec % 2) * TF:(ec % 2 + 1) * TF],
                                            in0=ps[v][:, 0:TF], in1=s2ebc[:],
                                            op=ALU.mult).then_inc(S["f2e"], 1)

                for tau in range(NTF):
                    lo, hi, colo = t_range(tau)
                    w = hi - lo
                    mainw = min(512, w)
                    assert w - mainw == 0
                    for v in range(NHC + 2):
                        if v < NHC:
                            hc = v
                            ghc = tau * NHC + hc
                            slot = hc % 2
                            hsl = hch[:, slot * (TF + 4):(slot + 1) * (TF + 4)]
                            e.wait_ge(S["f1"], ghc + 1)
                            if ghc >= 2:
                                e.wait_ge(S["cv1"], ghc - 1)
                                e.wait_ge(S["cv3"], ghc - 1)
                            if tau == 0:
                                e.memset(hsl[:, 0:1], 0.0)
                                e.drain()
                            if tau == NTF - 1:
                                e.memset(hsl[:, TF + 1:TF + 2], 0.0)
                                e.drain()
                            e.tensor_tensor(out=hsl[:, colo:colo + mainw],
                                            in0=hbank[hc % 2][:, 0:mainw],
                                            in1=s1all[:, tau * 264:tau * 264 + mainw],
                                            op=ALU.mult).then_inc(S["f1e"], 1)
                        if 0 <= v - 1 < NHC:
                            h1 = v - 1
                            g1 = tau * NHC + h1
                            s1_ = h1 % 2
                            hs1 = hch[:, s1_ * (TF + 4):(s1_ + 1) * (TF + 4)]
                            e.wait_ge(S["cv2"], g1 + 1)
                            e.wait_ge(S["f1e"], g1 + 1)
                            if g1 >= 2:
                                e.wait_ge(S["sil"], g1 - 1)
                            e.scalar_tensor_tensor(out=cvc[:, s1_ * TF:(s1_ + 1) * TF],
                                                   in0=hs1[:, 2:TF + 2],
                                                   scalar=cw[:, h1 * 3 + 2:h1 * 3 + 3],
                                                   in1=cvb[:, s1_ * TF:(s1_ + 1) * TF],
                                                   op0=ALU.mult,
                                                   op1=ALU.add).then_inc(S["cv3"], 1)
                        if 0 <= v - 2 < NHC:
                            h2 = v - 2
                            g2 = tau * NHC + h2
                            e.wait_ge(S["sil"], g2 + 1)
                            e.wait_ge(S["qmn"], g2)
                            e.wait_ge(S["qab"], g2)
                            gsl = gbuf[:, h2 * TF:(h2 + 1) * TF]
                            if h2 == 0:
                                if tau >= 1:
                                    e.wait_ge(S["q2"], tau)
                                e.tensor_copy(out=Mb[:], in_=gsl).then_inc(S["qmn"], 1)
                                e.tensor_copy(out=Mn[:], in_=gsl).then_inc(S["qab"], 1)
                            else:
                                e.tensor_tensor(out=Mb[:], in0=Mb[:], in1=gsl,
                                                op=ALU.max).then_inc(S["qmn"], 1)
                                e.tensor_tensor(out=Mn[:], in0=Mn[:], in1=gsl,
                                                op=ALU.min).then_inc(S["qab"], 1)
                    e.wait_ge(S["qmn"], (tau + 1) * NHC)
                    e.wait_ge(S["qab"], (tau + 1) * NHC)
                    e.drain()
                    e.scalar_tensor_tensor(out=Mb[:], in0=Mn[:], scalar=-1.0,
                                           in1=Mb[:], op0=ALU.mult,
                                           op1=ALU.max).then_inc(S["q1"], 1)
                    if tau >= 1:
                        epi(tau - 1)
                    e.wait_ge(S["q2"], tau + 1)
                    if tau >= 1:
                        e.wait_ge(S["q4"], tau)
                    for k4 in range(NQT):
                        e.reduce_max(out=maCt[:, k4:k4 + 1],
                                     in_=ps[2][:, k4 * 128:(k4 + 1) * 128], axis=AX.X)
                    e.drain()
                    e.tensor_scalar(out=maCt[:], in0=maCt[:], scalar1=1e-5,
                                    scalar2=None, op0=ALU.max).then_inc(S["q3"], 1)
                    e.wait_ge(S["q4"], tau + 1)
                    e.tensor_copy(out=maCrow[0:1, 0:TF], in_=ps[3][0:1, 0:TF])
                    e.drain()
                    e.reciprocal(out=s2rw[:], in_=maCrow[:])
                    e.drain()
                    e.tensor_scalar(out=s2rw[:], in0=s2rw[:], scalar1=127.0,
                                    scalar2=None, op0=ALU.mult)
                    e.tensor_scalar(out=s2erw[:], in0=maCrow[:],
                                    scalar1=mw127[0:1, 1:2], scalar2=None,
                                    op0=ALU.mult).then_inc(S["q5"], 1)
                    e.wait_ge(S["q6"], tau + 1)
                    e.tensor_copy(out=s2bc[:], in_=ps[2][:, 0:TF])
                    e.tensor_copy(out=s2ebc[:], in_=ps[3][:, 0:TF])
                    e.drain()
                    e.engine_nop().then_inc(S["q7"], 1)
                    if tau >= 2:
                        e.wait_ge(S["f2c"], 16 * (tau - 1))
                    for v in range(NHC + 2):
                        if v < NHC:
                            hc = v
                            ghc = tau * NHC + hc
                            slot = hc % 2
                            if ghc >= 2:
                                e.wait_ge(S["rnC"], ghc - 1)
                            e.tensor_tensor(out=gsb[:, slot * TF:(slot + 1) * TF],
                                            in0=gbuf[:, hc * TF:(hc + 1) * TF],
                                            in1=s2bc[:], op=ALU.mult).then_inc(S["rnA"], 1)
                        if 0 <= v - 2 < NHC:
                            h2 = v - 2
                            g2 = tau * NHC + h2
                            s2_ = h2 % 2
                            rsl = rr2b[:, s2_ * TF:(s2_ + 1) * TF]
                            e.wait_ge(S["rnC"], g2 + 1)
                            e.tensor_scalar(out=rsl, in0=rsl, scalar1=-1535.5,
                                            scalar2=None, op0=ALU.is_ge)
                            e.drain()
                            if tau == 0 and h2 == 0:
                                e.wait_ge(S["p4q"], NFC * NTQ)
                            e.scalar_tensor_tensor(
                                out=bufA[:, (tau % 2) * NHC * TF + h2 * TF:
                                         (tau % 2) * NHC * TF + (h2 + 1) * TF],
                                in0=i16c[:, s2_ * TF:(s2_ + 1) * TF],
                                scalar=-1536.0,
                                in1=rsl, op0=ALU.add,
                                op1=ALU.add).then_inc(S["rnD"], 1)
                epi(NTF - 1)

            @block3.scalar
            def _(e):
                e.wait_ge(S["pB2"], 3)
                e.wait_ge(S["p4qe"], NFC * NTQ)
                for tau in range(NTF):
                    for v in range(NHC + 1):
                        if v < NHC:
                            hc = v
                            ghc = tau * NHC + hc
                            slot = hc % 2
                            hsl = hch[:, slot * (TF + 4):(slot + 1) * (TF + 4)]
                            e.wait_ge(S["f1e"], ghc + 1)
                            if ghc >= 2:
                                e.wait_ge(S["cv2"], ghc - 1)
                            e.activation(out=cvs[:, slot * TF:(slot + 1) * TF],
                                         in_=hsl[:, 1:TF + 1], func=AF.Identity,
                                         scale=cw[:, hc * 3 + 1:hc * 3 + 2],
                                         bias=0.0).then_inc(S["cvS"], 1)
                            e.activation(out=cva[:, slot * TF:(slot + 1) * TF],
                                         in_=hsl[:, 0:TF], func=AF.Identity,
                                         scale=cw[:, hc * 3:hc * 3 + 1],
                                         bias=0.0).then_inc(S["cv1"], 1)
                        if 0 <= v - 1 < NHC:
                            h1 = v - 1
                            g1 = tau * NHC + h1
                            s1_ = h1 % 2
                            e.wait_ge(S["cv3"], g1 + 1)
                            if tau >= 1:
                                e.wait_ge(S["rnA"], (tau - 1) * NHC + h1 + 1)
                            e.activation(out=gbuf[:, h1 * TF:(h1 + 1) * TF],
                                         in_=cvc[:, s1_ * TF:(s1_ + 1) * TF],
                                         func=AF.Silu,
                                         bias=cb[:, h1:h1 + 1]).then_inc(S["sil"], 1)
                    for hc in range(NHC):
                        ghc = tau * NHC + hc
                        slot = hc % 2
                        e.wait_ge(S["rnA"], ghc + 1)
                        if ghc >= 2:
                            e.wait_ge(S["rnD"], ghc - 1)
                        e.activation(out=i16c[:, slot * TF:(slot + 1) * TF],
                                     in_=gsb[:, slot * TF:(slot + 1) * TF],
                                     func=AF.Identity,
                                     scale=1.0, bias=c1535[:, 0:1]).then_inc(S["rnB"], 1)

            @block3.gpsimd
            def _(e):
                e.wait_ge(S["pB2"], 3)
                e.wait_ge(S["p4qe"], NFC * NTQ)

                def t2out(ct):
                    for ec in range(8):
                        gx = ct * 8 + ec
                        e.wait_ge(S["f2e"], gx + 1)
                        e.tensor_tensor(out=t2b[:, (ec % 2) * TF:(ec % 2 + 1) * TF],
                                        in0=t1b[:, (ec % 2) * TF:(ec % 2 + 1) * TF],
                                        in1=bufC[:, ec * T + ct * TF:ec * T + (ct + 1) * TF],
                                        op=ALU.add).then_inc(S["f2f"], 1)
                        e.wait_ge(S["dxta" if ec % 2 == 0 else "dxtb"],
                                  16 * (gx // 2 + 1))
                        if gx >= 2:
                            e.wait_ge(S["dosa" if ec % 2 == 0 else "dosb"],
                                      16 * ((gx - 2) // 2 + 1))
                        e.tensor_tensor(out=outst[:, (ec % 2) * TF:(ec % 2 + 1) * TF],
                                        in0=t2b[:, (ec % 2) * TF:(ec % 2 + 1) * TF],
                                        in1=xst2[:, ec * TF:(ec + 1) * TF],
                                        op=ALU.add).then_inc(S["f2g"], 1)
                        e.wait_ge(S["f2g"], gx + 1)
                        e.dma_start(out=dOUT[ec * 128:(ec + 1) * 128,
                                             ct * TF:(ct + 1) * TF],
                                    in_=outst[:, (ec % 2) * TF:(ec % 2 + 1) * TF]
                                    ).then_inc(S["dosa" if ec % 2 == 0 else "dosb"], 16)

                for tau in range(NTF):
                    for v in range(NHC):
                        if True:
                            hc = v
                            ghc = tau * NHC + hc
                            slot = hc % 2
                            e.wait_ge(S["cvS"], ghc + 1)
                            e.wait_ge(S["cv1"], ghc + 1)
                            if ghc >= 2:
                                e.wait_ge(S["cv3"], ghc - 1)
                            e.tensor_tensor(out=cvb[:, slot * TF:(slot + 1) * TF],
                                            in0=cvs[:, slot * TF:(slot + 1) * TF],
                                            in1=cva[:, slot * TF:(slot + 1) * TF],
                                            op=ALU.add).then_inc(S["cv2"], 1)
                    if tau >= 1:
                        t2out(tau - 1)
                    for ec in range(8):
                        xsem = S["dxta" if ec % 2 == 0 else "dxtb"]
                        if tau >= 1:
                            e.wait_ge(S["f2g"], 8 * (tau - 1) + ec + 1)
                        e.dma_start(out=xst2[:, ec * TF:(ec + 1) * TF],
                                    in_=dXT[ec * 128:(ec + 1) * 128,
                                            tau * TF:(tau + 1) * TF]).then_inc(xsem, 16)
                    for hc in range(NHC):
                        ghc = tau * NHC + hc
                        slot = hc % 2
                        e.wait_ge(S["rnB"], ghc + 1)
                        if ghc >= 2:
                            e.wait_ge(S["rnD"], ghc - 1)
                        e.tensor_tensor(out=rr2b[:, slot * TF:(slot + 1) * TF],
                                        in0=gsb[:, slot * TF:(slot + 1) * TF],
                                        in1=i16c[:, slot * TF:(slot + 1) * TF],
                                        op=ALU.subtract).then_inc(S["rnC"], 1)
                t2out(NTF - 1)

    return nc, dbg_out




def _prep_inputs(inputs, T):
    x = np.ascontiguousarray(inputs["L0_attn_x"])
    B = x.shape[0]
    shared = {
        "wqT": np.ascontiguousarray(inputs["L0_attn_Wq"].T),
        "wk": np.ascontiguousarray(inputs["L0_attn_Wk"]),
        "wv": np.ascontiguousarray(inputs["L0_attn_Wv"]),
        "wo": np.ascontiguousarray(inputs["L0_attn_Wo"]),
        "wc": np.ascontiguousarray(inputs["L0_attn_Wc"]),
        "bq": np.ascontiguousarray(inputs["L0_attn_bq"].reshape(NFC, 128).T),
        "bk": np.ascontiguousarray(inputs["L0_attn_bk"].reshape(NFC, 128).T),
        "bv": np.ascontiguousarray(inputs["L0_attn_bv"].reshape(1, D)),
        "bo": np.ascontiguousarray(inputs["L0_attn_bo"].reshape(NFC, 128).T),
        "w1T": np.ascontiguousarray(inputs["L0_ffn_w1"].T),
        "w2T": np.ascontiguousarray(inputs["L0_ffn_w2"].T),
        "convw": np.ascontiguousarray(
            inputs["L0_ffn_conv_w"].reshape(NHC, 128, 3).transpose(1, 0, 2).reshape(128, NHC * 3)),
        "convb": np.ascontiguousarray(inputs["L0_ffn_conv_b"].reshape(NHC, 128).T),
        "ident": np.eye(128, dtype=BF16NP),
        "masksub": np.tril(np.full((128, 128), -1e9, np.float32), k=-1),
    }
    maps = []
    for b in range(B):
        m = dict(shared)
        m["x_tok"] = np.ascontiguousarray(x[b, :T, :])
        m["xT"] = np.ascontiguousarray(x[b, :T, :].T)
        maps.append(m)
    return maps


def kernel(**inputs):
    T = 2048
    maps = _prep_inputs(inputs, T)
    nc, _ = build(T=T)
    res = run_bass_kernel_spmd(nc, maps, core_ids=list(range(len(maps))))
    out = np.stack([np.ascontiguousarray(r["outT"].T) for r in res.results])
    return out.astype(np.float32)
